# version 19
# baseline (speedup 1.0000x reference)
import contextlib
import numpy as np
import ml_dtypes
import concourse.bass as bass
import concourse.mybir as mybir
from concourse.bass_utils import run_bass_kernel_spmd

F32 = mybir.dt.float32
BF16 = mybir.dt.bfloat16
AF = mybir.ActivationFunctionType
ALU = mybir.AluOpType
AX = mybir.AxisListType

P = 128
D = 1024
DC = 8
TCORE = 2048
NT = 16
H = 16
HD = 64
NWIN = 16
GT = 64
NE = 8
FH = 4096
NFS = 32
NJ = 4
FS_JF = NFS // NJ
NTH = 4
EPS = 1e-5
W1SLOTS = 4
W2SLOTS = 10


class Ctr:
    __slots__ = ("sem", "v", "step", "key")
    _next = 0

    def __init__(self, sem, step=1):
        self.sem = sem
        self.v = 0
        self.step = step
        self.key = Ctr._next
        Ctr._next += 1


def build_nc(taps=()):
    nc = bass.Bass("TRN2", target_bir_lowering=False)
    es = contextlib.ExitStack()

    d_x = nc.dram_tensor("x", [TCORE, D], F32, kind="ExternalInput").ap()
    d_xg = nc.dram_tensor("xg", [GT, D], F32, kind="ExternalInput").ap()
    d_wqt = nc.dram_tensor("wqt", [P, DC, D], F32, kind="ExternalInput").ap()
    d_wkt = nc.dram_tensor("wkt", [P, DC, D], F32, kind="ExternalInput").ap()
    d_wvt = nc.dram_tensor("wvt", [P, DC, D], F32, kind="ExternalInput").ap()
    d_wot = nc.dram_tensor("wot", [P, DC, D], F32, kind="ExternalInput").ap()
    d_gw = nc.dram_tensor("gw", [P, DC, NE], F32, kind="ExternalInput").ap()
    d_gb = nc.dram_tensor("gb", [1, NE], F32, kind="ExternalInput").ap()
    d_w1r = nc.dram_tensor("w1r", [NE, NFS, P, DC, 2, P], F32, kind="ExternalInput").ap()
    d_w2 = nc.dram_tensor("w2", [NE, FH, D], F32, kind="ExternalInput").ap()
    d_b1r = nc.dram_tensor("b1r", [NFS, NE, 2, P], F32, kind="ExternalInput").ap()
    d_b2f = nc.dram_tensor("b2f", [1, NE * D], F32, kind="ExternalInput").ap()
    d_id = nc.dram_tensor("ident", [P, P], F32, kind="ExternalInput").ap()
    d_ones = nc.dram_tensor("onesr", [1, P], F32, kind="ExternalInput").ap()
    d_eps = nc.dram_tensor("epsc", [P, 2], F32, kind="ExternalInput").ap()
    d_out = nc.dram_tensor("out", [TCORE, D], F32, kind="ExternalOutput").ap()

    x_sb = nc.alloc_sbuf_tensor("x_sb", [P, NT, D], F32)
    s_id = nc.alloc_sbuf_tensor("s_id", [P, P], BF16)
    s_ones = nc.alloc_sbuf_tensor("s_ones", [1, P], BF16)
    s_eps = nc.alloc_sbuf_tensor("s_eps", [P, 2], F32)
    s_gwf = nc.alloc_sbuf_tensor("s_gwf", [P, DC, NE], F32)
    s_gh = nc.alloc_sbuf_tensor("s_gh", [P, DC, NE], BF16)
    s_gl = nc.alloc_sbuf_tensor("s_gl", [P, DC, NE], BF16)
    s_gbf = nc.alloc_sbuf_tensor("s_gbf", [1, NE], F32)
    s_gbh = nc.alloc_sbuf_tensor("s_gbh", [1, NE], BF16)
    s_gbl = nc.alloc_sbuf_tensor("s_gbl", [1, NE], BF16)
    b1c = nc.alloc_sbuf_tensor("b1c", [P, NE, 2 * NFS], F32)
    gateW = nc.alloc_sbuf_tensor("gateW", [P, NT, NE], F32)
    g_s1 = nc.alloc_sbuf_tensor("g_s1", [P, NT], F32)
    g_s2 = nc.alloc_sbuf_tensor("g_s2", [P, NT], F32)
    g_mean = nc.alloc_sbuf_tensor("g_mean", [P, NT], F32)
    g_var = nc.alloc_sbuf_tensor("g_var", [P, NT], F32)
    g_rstd = nc.alloc_sbuf_tensor("g_rstd", [P, NT], F32)
    g_nmr = nc.alloc_sbuf_tensor("g_nmr", [P, NT], F32)
    lnsq = nc.alloc_sbuf_tensor("lnsq", [P, 2 * D], F32)
    xn_stage = nc.alloc_sbuf_tensor("xn_stage", [P, 4, D], BF16)
    gt_t = nc.alloc_sbuf_tensor("gt_t", [P, 4, NE], F32)
    gc_t = nc.alloc_sbuf_tensor("gc_t", [P, 8], F32)
    es_b1 = contextlib.ExitStack()
    b1tm = es_b1.enter_context(nc.sbuf_tensor("b1tm", [NFS, NE * 2, P], BF16))

    pA = nc.alloc_psum_tensor("pA", [P, 512], F32)
    pB = nc.alloc_psum_tensor("pB", [P, 512], F32)
    pC = nc.alloc_psum_tensor("pC", [P, 512], F32)
    pD = nc.alloc_psum_tensor("pD", [P, 512], F32)
    pE = nc.alloc_psum_tensor("pE", [P, 512], F32)
    pF = nc.alloc_psum_tensor("pF", [P, 512], F32)
    pT = nc.alloc_psum_tensor("pT", [P, D], BF16)
    pU = nc.alloc_psum_tensor("pU", [P, D], BF16)

    sem_pe = es.enter_context(nc.semaphore("sem_pe"))
    sem_act = es.enter_context(nc.semaphore("sem_act"))
    sem_dve = es.enter_context(nc.semaphore("sem_dve"))
    sem_setup = es.enter_context(nc.semaphore("sem_setup"))
    sem_sy = es.enter_context(nc.semaphore("sem_sy"))
    c_pe = Ctr(sem_pe)
    c_act = Ctr(sem_act)
    c_dve = Ctr(sem_dve)
    c_setup = Ctr(sem_setup, 16)
    c_sy = Ctr(sem_sy, 16)
    c_wsl = [Ctr(es.enter_context(nc.semaphore(f"sem_wsl{i}")), 16) for i in range(2)]
    c_w1 = [Ctr(es.enter_context(nc.semaphore(f"sem_w1_{i}")), 16) for i in range(W1SLOTS)]
    c_w2 = [Ctr(es.enter_context(nc.semaphore(f"sem_w2_{i}")), 16) for i in range(W2SLOTS)]
    c_x = [Ctr(es.enter_context(nc.semaphore(f"sem_x{i}")), 16) for i in range(4)]

    streams = {k: [] for k in ("tensor", "scalar", "vector", "gpsimd", "sync")}

    def op(eng, fn, waits=(), inc=None):
        streams[eng].append((fn, tuple((c, v) for (c, v) in waits), inc))
        if inc is not None:
            inc.v += inc.step
        return inc.v if inc is not None else None

    xr = d_x.rearrange("(t p) d -> p t d", p=P)
    for ch in range(4):
        op("sync" if ch % 2 == 0 else "scalar", lambda e, ch=ch: e.dma_start(
            out=x_sb[:, ch * 4:(ch + 1) * 4, :], in_=xr[:, ch * 4:(ch + 1) * 4, :]),
           inc=c_x[ch])
    op("sync", lambda e: e.dma_start(out=s_eps[:], in_=d_eps), inc=c_sy)
    op("sync", lambda e: e.dma_start(out=s_gwf[:], in_=d_gw), inc=c_sy)
    op("sync", lambda e: e.dma_start(out=s_gbf[:], in_=d_gb), inc=c_sy)

    op("gpsimd", lambda e: e.dma_start(out=s_id[:], in_=d_id), inc=c_setup)
    op("gpsimd", lambda e: e.dma_start(out=s_ones[:], in_=d_ones), inc=c_setup)
    op("gpsimd", lambda e: e.dma_start(out=s_gh[:], in_=d_gw), inc=c_setup)
    op("gpsimd", lambda e: e.dma_start(out=s_gbh[:], in_=d_gb), inc=c_setup)
    op("gpsimd", lambda e: e.dma_start(out=b1tm[:], in_=d_b1r.rearrange(
        "c e h f -> c (e h) f")), inc=c_setup)
    v_gpsetup = c_setup.v
    v_id = v_gpsetup

    v_gate_consts_holder = [0]

    b1_ev = []
    for grp in range(2):
        pdst = pT if grp == 0 else pU
        for k in range(8):
            idx = grp * 8 + k
            e_, h_ = idx // 2, idx % 2

            def f(eng, pdst=pdst, k=k, idx=idx):
                return eng.transpose(pdst[:, k * NFS:(k + 1) * NFS],
                                     b1tm[:, idx, :], s_id[0:NFS, 0:NFS])
            w = [(c_setup, v_gpsetup)] if (grp == 0 and k == 0) else []
            op("tensor", f, waits=w, inc=c_pe)
        v_tr = c_pe.v

        def fev(eng, pdst=pdst, grp=grp):
            return eng.copy(
                b1c[:].rearrange("p e (h c) -> p (e h) c", h=2)[:, grp * 8:(grp + 1) * 8, :],
                pdst[:, 0:8 * NFS].rearrange("p (k c) -> p k c", c=NFS))
        op("scalar", fev, waits=[(c_pe, v_tr)], inc=c_act)
    v_b1_done = c_act.v
    v_b1_tr_pe = c_pe.v
    es_b1.close()

    v_sync_all = [0]

    def ln_stats(src_tiles, n_tiles, parts=P, chunked=False):
        v_r = {}
        v_n = {}
        step = 4 if chunked else n_tiles
        for c0 in range(0, n_tiles, step):
            for tt in range(c0, min(c0 + step, n_tiles)):
                src = src_tiles(tt)
                op("scalar", lambda e, src=src, tt=tt: e.activation(
                    lnsq[0:parts, 0:D], src, AF.Square,
                    accum_out=g_s2[0:parts, tt:tt + 1]),
                   waits=[(c_x[tt // 4], 16)], inc=c_act)
                op("vector", lambda e, src=src, tt=tt: e.reduce_sum(
                    g_s1[0:parts, tt:tt + 1], src, axis=AX.X),
                   waits=[(c_x[tt // 4], 16)], inc=c_dve)
            v_stats = (c_act.v, c_dve.v)
            c1 = min(c0 + step, n_tiles)
            op("vector", lambda e, c0=c0, c1=c1: e.tensor_scalar_mul(
                g_mean[0:parts, c0:c1], g_s1[0:parts, c0:c1], 1.0 / D),
               waits=[(c_act, v_stats[0])], inc=c_dve)
            op("vector", lambda e, c0=c0, c1=c1: e.tensor_mul(
                g_var[0:parts, c0:c1], g_mean[0:parts, c0:c1],
                g_mean[0:parts, c0:c1]), inc=c_dve)
            op("vector", lambda e, c0=c0, c1=c1: e.scalar_tensor_tensor(
                g_var[0:parts, c0:c1], g_s2[0:parts, c0:c1], 1.0 / D,
                g_var[0:parts, c0:c1], ALU.mult, ALU.subtract), inc=c_dve)
            v_var = c_dve.v
            op("scalar", lambda e, c0=c0, c1=c1: e.activation(
                g_rstd[0:parts, c0:c1], g_var[0:parts, c0:c1], AF.Ln,
                bias=s_eps[0:parts, 0:1]), waits=[(c_dve, v_var)], inc=c_act)
            op("scalar", lambda e, c0=c0, c1=c1: e.activation(
                g_rstd[0:parts, c0:c1], g_rstd[0:parts, c0:c1], AF.Exp,
                scale=-0.5), inc=c_act)
            v_rstd = c_act.v
            op("vector", lambda e, c0=c0, c1=c1: e.scalar_tensor_tensor(
                g_nmr[0:parts, c0:c1], g_mean[0:parts, c0:c1], -1.0,
                g_rstd[0:parts, c0:c1], ALU.mult, ALU.mult),
               waits=[(c_act, v_rstd)], inc=c_dve)
            for tt in range(c0, c1):
                v_r[tt] = v_rstd
                v_n[tt] = c_dve.v
        return v_r, v_n

    tr_parity = [0]
    stage_last_tr = {}

    def norm_transpose(tt, dst_view, v_extra_waits=(), src=None, parts=P,
                       n_dc=DC, stage_slot=None, rstd_col=None, nmr_col=None,
                       out_stage=None):
        par = tr_parity[0] % 2 if stage_slot is None else stage_slot
        tr_parity[0] += 1
        stg = xn_stage[0:parts, par, :] if out_stage is None else out_stage
        src = x_sb[:, tt, :] if src is None else src
        rc = g_rstd[0:parts, tt:tt + 1] if rstd_col is None else rstd_col
        nc_ = g_nmr[0:parts, tt:tt + 1] if nmr_col is None else nmr_col
        sg = stage_last_tr.get(par, 0)
        pg = stage_last_tr.get(("ev", par), 0)
        op("vector", lambda e, src=src, stg=stg, rc=rc, nc_=nc_: e.tensor_scalar(
            stg, src, rc, nc_, ALU.mult, ALU.add),
           waits=list(v_extra_waits) + [(c_pe, sg)], inc=c_dve)
        v_norm = c_dve.v
        pdst = pT if par == 0 else pU
        for dc in range(n_dc):
            op("tensor", lambda e, dc=dc, stg=stg, pdst=pdst, parts=parts: e.transpose(
                pdst[:, dc * parts:(dc + 1) * parts],
                stg.rearrange("p (c f) -> p c f", f=P)[:, dc, :],
                s_id[0:parts, 0:parts]),
               waits=[(c_dve, v_norm), (c_setup, v_id), (c_act, pg)], inc=c_pe)
        v_tr = c_pe.v
        stage_last_tr[par] = v_tr
        op("scalar", lambda e, pdst=pdst, dst_view=dst_view, parts=parts, n_dc=n_dc:
           e.copy(dst_view,
                         pdst[:, 0:n_dc * parts].rearrange("p (c f) -> p c f", c=n_dc)),
           waits=[(c_pe, v_tr)], inc=c_act)
        stage_last_tr[("ev", par)] = c_act.v
        return c_act.v

    es_attn = contextlib.ExitStack()
    xn1h = es_attn.enter_context(nc.sbuf_tensor("xn1h", [P, DC, 1024], BF16))
    qf = es_attn.enter_context(nc.sbuf_tensor("qf", [P, DC, 1024], BF16))
    kf = es_attn.enter_context(nc.sbuf_tensor("kf", [P, DC, 1024], BF16))
    vt = es_attn.enter_context(nc.sbuf_tensor("vt", [P, DC, 1024], BF16))
    aof = es_attn.enter_context(nc.sbuf_tensor("aof", [P, DC, 1024], BF16))
    wsl = es_attn.enter_context(nc.sbuf_tensor("wsl", [P, 2, DC, 1024], BF16))
    xg_sb = es_attn.enter_context(nc.sbuf_tensor("xg_sb", [GT, D], F32))
    xgn_fm = es_attn.enter_context(nc.sbuf_tensor("xgn_fm", [P, DC, GT], BF16))
    kgf = es_attn.enter_context(nc.sbuf_tensor("kgf", [P, DC, GT], BF16))
    vgt = es_attn.enter_context(nc.sbuf_tensor("vgt", [GT, D], BF16))
    gx = es_attn.enter_context(nc.sbuf_tensor("gx", [GT, 8], F32))
    P_sb = es_attn.enter_context(nc.sbuf_tensor("P_sb", [P, 4, 192], BF16))
    sPT = es_attn.enter_context(nc.sbuf_tensor("sPT", [P, 4, 256], BF16))
    rr = es_attn.enter_context(nc.sbuf_tensor("rr", [P, 4, 4], F32))

    op("sync", lambda e: e.dma_start(out=xg_sb[:], in_=d_xg), inc=c_sy)
    v_xg = c_sy.v
    v_sync_all[0] = c_sy.v

    op("vector", lambda e: e.tensor_tensor(s_gl[:], s_gwf[:], s_gh[:], ALU.subtract),
       waits=[(c_sy, v_sync_all[0]), (c_setup, v_gpsetup)], inc=c_dve)
    op("vector", lambda e: e.tensor_tensor(s_gbl[:], s_gbf[:], s_gbh[:], ALU.subtract),
       inc=c_dve)
    v_gate_consts_holder[0] = c_dve.v

    v_ln1_rstd, v_ln1_nmr = ln_stats(lambda tt: x_sb[:, tt, :], NT, chunked=True)

    op("scalar", lambda e: e.activation(
        lnsq[0:GT, 0:D], xg_sb[:], AF.Square, accum_out=gx[:, 1:2]),
       waits=[(c_sy, v_sync_all[0])], inc=c_act)
    op("vector", lambda e: e.reduce_sum(gx[:, 0:1], xg_sb[:], axis=AX.X),
       waits=[(c_sy, v_sync_all[0])], inc=c_dve)
    v_gxs = (c_act.v, c_dve.v)
    op("vector", lambda e: e.tensor_scalar_mul(gx[:, 2:3], gx[:, 0:1], 1.0 / D),
       waits=[(c_act, v_gxs[0])], inc=c_dve)
    op("vector", lambda e: e.tensor_mul(gx[:, 3:4], gx[:, 2:3], gx[:, 2:3]), inc=c_dve)
    op("vector", lambda e: e.scalar_tensor_tensor(
        gx[:, 3:4], gx[:, 1:2], 1.0 / D, gx[:, 3:4], ALU.mult, ALU.subtract), inc=c_dve)
    v_gv = c_dve.v
    op("scalar", lambda e: e.activation(gx[:, 4:5], gx[:, 3:4], AF.Ln,
                                        bias=s_eps[0:GT, 0:1]),
       waits=[(c_dve, v_gv)], inc=c_act)
    op("scalar", lambda e: e.activation(gx[:, 4:5], gx[:, 4:5], AF.Exp, scale=-0.5),
       inc=c_act)
    v_grstd = c_act.v
    op("vector", lambda e: e.scalar_tensor_tensor(
        gx[:, 5:6], gx[:, 2:3], -1.0, gx[:, 4:5], ALU.mult, ALU.mult),
       waits=[(c_act, v_grstd)], inc=c_dve)
    v_gnmr = c_dve.v
    v_xgn = norm_transpose(
        0, xgn_fm[:], v_extra_waits=[(c_act, v_grstd), (c_dve, v_gnmr)],
        src=xg_sb[:], parts=GT, rstd_col=gx[0:GT, 4:5], nmr_col=gx[0:GT, 5:6])

    proj_ps = [pE, pF]
    S_banks = [pA, pB, pE]
    PV_banks = [pC, pD]
    v_evac_prev = {}
    v_aof_done = [0]
    v_attn_tail_pe = [0]

    def wproj_load(slot, dram, vfree_pe):
        op("gpsimd", lambda e: e.dma_start(out=wsl[:, slot], in_=dram),
           waits=([(c_pe, vfree_pe)] if vfree_pe else []), inc=c_wsl[slot])
        return (c_wsl[slot], c_wsl[slot].v)

    def projection(dst, wslot, vw, half, scale, v_xn, kind, extra=()):
        par = [0]
        for ic in range(DC):
            for tc in range(2):
                ps = proj_ps[par[0] % 2]
                waits = [vw, (c_act, v_xn)] + list(extra)
                pv = v_evac_prev.get(("proj", par[0] % 2))
                if pv:
                    waits.append((c_act, pv))
                for dc in range(DC):
                    def f(e, ps=ps, dc=dc, ic=ic, tc=tc, wslot=wslot, half=half):
                        if kind == "fm":
                            return e.matmul(
                                ps[:], wsl[:, wslot, dc, ic * P:(ic + 1) * P],
                                xn1h[:, dc, tc * 512:(tc + 1) * 512],
                                start=(dc == 0), stop=(dc == DC - 1))
                        else:
                            return e.matmul(
                                ps[:], xn1h[:, dc, ic * P:(ic + 1) * P],
                                wsl[:, wslot, dc, tc * 512:(tc + 1) * 512],
                                start=(dc == 0), stop=(dc == DC - 1))
                    op("tensor", f, waits=waits if dc == 0 else [], inc=(c_pe if dc == DC - 1 else None))
                v_mm = c_pe.v

                def fe(e, ps=ps, ic=ic, tc=tc, scale=scale):
                    return e.activation(dst[:, ic, tc * 512:(tc + 1) * 512], ps[:],
                                        AF.Copy, scale=scale)
                op("scalar", fe, waits=[(c_pe, v_mm)], inc=c_act)
                v_evac_prev[("proj", par[0] % 2)] = c_act.v
                par[0] += 1
        return c_act.v, c_pe.v

    for half in range(2):
        hbase = half * 8
        v_xn1 = 0
        for t8 in range(8):
            tt = hbase + t8
            waits = [(c_act, v_ln1_rstd[tt]), (c_dve, v_ln1_nmr[tt])]
            if half == 1:
                waits.append((c_pe, v_attn_tail_pe[0]))
            v_xn1 = norm_transpose(tt, xn1h[:, :, t8 * P:(t8 + 1) * P],
                                   v_extra_waits=waits)
        vfree0 = v_attn_tail_pe[0] if half == 1 else v_b1_tr_pe
        vwq = wproj_load(0, d_wqt, vfree0)
        extra = []
        vv = v_evac_prev.get("oproj1")
        if vv:
            extra.append((c_dve, vv))
        v_q_act, v_q_pe = projection(qf, 0, vwq, half, 0.125, v_xn1, "fm",
                                     extra=extra)
        vwk = wproj_load(1, d_wkt, vfree0)
        v_k_act, v_k_pe = projection(kf, 1, vwk, half, 1.0, v_xn1, "fm")
        if half == 0:
            for ic in range(DC):
                ps = proj_ps[ic % 2]
                waits = [vwk, (c_act, v_xgn)]
                pv = v_evac_prev.get(("proj", ic % 2))
                if pv:
                    waits.append((c_act, pv))
                for dc in range(DC):
                    def f(e, ps=ps, dc=dc, ic=ic):
                        return e.matmul(ps[:, 0:GT],
                                        wsl[:, 1, dc, ic * P:(ic + 1) * P],
                                        xgn_fm[:, dc, :],
                                        start=(dc == 0), stop=(dc == DC - 1))
                    op("tensor", f, waits=waits if dc == 0 else [],
                       inc=(c_pe if dc == DC - 1 else None))
                v_mm = c_pe.v
                op("scalar", lambda e, ps=ps, ic=ic: e.copy(
                    kgf[:, ic, :], ps[:, 0:GT]), waits=[(c_pe, v_mm)], inc=c_act)
                v_evac_prev[("proj", ic % 2)] = c_act.v
            v_k_pe = c_pe.v
        vwv = wproj_load(0, d_wvt, v_q_pe)
        v_v_act, v_v_pe = projection(vt, 0, vwv, half, 1.0, v_xn1, "tm")
        if half == 0:
            for tc in range(2):
                ps = proj_ps[tc % 2]
                waits = [vwv, (c_act, v_xgn)]
                pv = v_evac_prev.get(("proj", tc % 2))
                if pv:
                    waits.append((c_act, pv))
                for dc in range(DC):
                    def f(e, ps=ps, dc=dc, tc=tc):
                        return e.matmul(ps[0:GT, :], xgn_fm[:, dc, :],
                                        wsl[:, 0, dc, tc * 512:(tc + 1) * 512],
                                        start=(dc == 0), stop=(dc == DC - 1))
                    op("tensor", f, waits=waits if dc == 0 else [],
                       inc=(c_pe if dc == DC - 1 else None))
                v_mm = c_pe.v
                op("scalar", lambda e, ps=ps, tc=tc: e.copy(
                    vgt[:, tc * 512:(tc + 1) * 512], ps[0:GT, :]),
                   waits=[(c_pe, v_mm)], inc=c_act)
                v_evac_prev[("proj", tc % 2)] = c_act.v
        vwo = wproj_load(1, d_wot, v_k_pe)

        LAG = 2
        NPAIR = 8 * H
        v_seed_act = c_act.v
        vS = {}
        vEXP = {}
        vPN = {}
        vPT = {}
        vPTEV = {}
        vPV = {}
        vAO = {}

        def pemit_S(pi):
            w8, head = pi // H, pi % H
            sb = S_banks[pi % 3]
            pb = (head % 2) * HD
            ic = head // 2
            waits = [(c_act, v_q_act), (c_act, v_k_act),
                     (c_act, vEXP.get(pi - 3, v_seed_act))]

            def fs1(e, sb=sb, pb=pb, ic=ic, w8=w8):
                return e.matmul(sb[:, 0:GT], qf[pb:pb + HD, ic, w8 * P:(w8 + 1) * P],
                                kgf[pb:pb + HD, ic, :], start=True, stop=True)
            op("tensor", fs1, waits=waits, inc=None)

            def fs2(e, sb=sb, pb=pb, ic=ic, w8=w8):
                return e.matmul(sb[:, GT:192], qf[pb:pb + HD, ic, w8 * P:(w8 + 1) * P],
                                kf[pb:pb + HD, ic, w8 * P:(w8 + 1) * P],
                                start=True, stop=True)
            op("tensor", fs2, inc=c_pe)
            vS[pi] = c_pe.v

        def pemit_front(pi):
            sb = S_banks[pi % 3]
            op("scalar", lambda e, sb=sb, pi=pi: e.activation(
                P_sb[:, pi % 4, :], sb[:, 0:192], AF.Exp,
                bias=s_eps[:, 1:2], accum_out=rr[:, pi % 4, 1:2]),
               waits=[(c_pe, vS[pi]), (c_pe, vPT.get(pi - 4, 0))], inc=c_act)
            vEXP[pi] = c_act.v
            op("vector", lambda e, pi=pi: e.reciprocal(
                rr[:, pi % 4, 2:3], rr[:, pi % 4, 1:2]),
               waits=[(c_act, vEXP[pi])], inc=c_dve)
            op("vector", lambda e, pi=pi: e.tensor_scalar_mul(
                P_sb[:, pi % 4, :], P_sb[:, pi % 4, :], rr[:, pi % 4, 2:3]),
               inc=c_dve)
            vPN[pi] = c_dve.v

        def pemit_tr(pi):
            ptb = pT if pi % 2 == 0 else pU

            def ft1(e, ptb=ptb, pi=pi):
                return e.transpose(ptb[0:GT, 0:P], P_sb[:, pi % 4, 0:GT], s_id[:])
            op("tensor", ft1,
               waits=[(c_dve, vPN[pi]), (c_dve, vPTEV.get(pi - 2, 0))], inc=None)

            def ft2(e, ptb=ptb, pi=pi):
                return e.transpose(ptb[:, P:256], P_sb[:, pi % 4, GT:192], s_id[:])
            op("tensor", ft2, inc=c_pe)
            vPT[pi] = c_pe.v
            op("vector", lambda e, ptb=ptb, pi=pi: e.tensor_copy(
                sPT[:, pi % 4, :], ptb[:, 0:256]),
               waits=[(c_pe, vPT[pi]), (c_pe, vPV.get(pi - 4, 0))], inc=c_dve)
            vPTEV[pi] = c_dve.v

        def pemit_pv(pi):
            w8, head = pi // H, pi % H
            pvb = PV_banks[pi % 2]
            pb = (head % 2) * HD
            ic = head // 2

            def fpv1(e, pvb=pvb, head=head, pi=pi):
                return e.matmul(pvb[0:HD, 0:P], vgt[0:GT, head * HD:(head + 1) * HD],
                                sPT[0:GT, pi % 4, 0:P], start=True, stop=False)
            op("tensor", fpv1,
               waits=[(c_dve, vPTEV[pi]), (c_act, v_v_act),
                      (c_act, vAO.get(pi - 2, 0))], inc=None)

            def fpv2(e, pvb=pvb, head=head, w8=w8, pi=pi):
                return e.matmul(pvb[0:HD, 0:P], vt[:, w8, head * HD:(head + 1) * HD],
                                sPT[:, pi % 4, P:256], start=False, stop=True)
            op("tensor", fpv2, inc=c_pe)
            vPV[pi] = c_pe.v
            op("scalar", lambda e, pvb=pvb, pb=pb, ic=ic, w8=w8: e.copy(
                aof[pb:pb + HD, ic, w8 * P:(w8 + 1) * P], pvb[0:HD, 0:P]),
               waits=[(c_pe, vPV[pi])], inc=c_act)
            vAO[pi] = c_act.v

        def pemit_oproj(t8):
            v_ao_w = vAO[t8 * H + H - 1]
            for dh in range(2):
                gidx = t8 * 2 + dh
                ps = pF
                waits = [vwo, (c_act, v_ao_w)]
                pv = v_evac_prev.get("oproj1")
                if pv:
                    waits.append((c_dve, pv))
                pv2 = v_evac_prev.get(("proj", 1))
                if pv2:
                    waits.append((c_act, pv2))
                for ic in range(DC):
                    def f(e, ps=ps, ic=ic, t8=t8, dh=dh):
                        return e.matmul(ps[:], aof[:, ic, t8 * P:(t8 + 1) * P],
                                        wsl[:, 1, ic, dh * 512:(dh + 1) * 512],
                                        start=(ic == 0), stop=(ic == DC - 1))
                    op("tensor", f, waits=waits if ic == 0 else [],
                       inc=(c_pe if ic == DC - 1 else None))
                v_mm = c_pe.v

                def fr(e, ps=ps, t8=t8, dh=dh, hbase=hbase):
                    xs = x_sb[:, hbase + t8, dh * 512:(dh + 1) * 512]
                    return e.tensor_tensor(xs, ps[:], xs, ALU.add)
                op("vector", fr, waits=[(c_pe, v_mm)], inc=c_dve)
                v_evac_prev["oproj1"] = c_dve.v
                if dh == 1:
                    tt = hbase + t8
                    v_add = c_dve.v
                    op("scalar", lambda e, tt=tt: e.activation(
                        lnsq[:, 0:D], x_sb[:, tt, :], AF.Square,
                        accum_out=g_s2[:, tt:tt + 1]),
                       waits=[(c_dve, v_add)], inc=c_act)
                    op("vector", lambda e, tt=tt: e.reduce_sum(
                        g_s1[:, tt:tt + 1], x_sb[:, tt, :], axis=AX.X), inc=c_dve)

        for j in range(NPAIR + 24):
            t8o = (j - 22) // 16
            if j >= 22 and (j - 22) % 16 == 0 and 0 <= t8o < 8:
                pemit_oproj(t8o)
            if 0 <= j - 6 < NPAIR:
                pemit_pv(j - 6)
            if 0 <= j - 5 < NPAIR:
                pemit_tr(j - 5)
            if 0 <= j - 3 < NPAIR:
                pemit_front(j - 3)
            if j < NPAIR:
                pemit_S(j)
        v_aof_done[0] = c_act.v

        v_attn_tail_pe[0] = c_pe.v
    v_attn_done_dve = c_dve.v
    v_attn_done_pe = c_pe.v
    v_attn_done_act = c_act.v

    es_attn.close()
    xn2f = nc.alloc_sbuf_tensor("xn2f", [P, DC, TCORE], BF16)
    act_sb = nc.alloc_sbuf_tensor("act_sb", [P, FS_JF, TCORE], BF16)
    lo_fm = act_sb
    w1rg = nc.alloc_sbuf_tensor("w1rg", [P, W1SLOTS, DC, 2, P], BF16)
    w2rg = nc.alloc_sbuf_tensor("w2rg", [P, W2SLOTS, D], BF16)
    a1st = nc.alloc_sbuf_tensor("a1st", [P, 2, 512], BF16)
    s_b2 = nc.alloc_sbuf_tensor("s_b2", [1, NE * D], BF16)

    v_st = (v_attn_done_act, v_attn_done_dve)
    op("vector", lambda e: e.tensor_scalar_mul(g_mean[:], g_s1[:], 1.0 / D),
       waits=[(c_act, v_st[0])], inc=c_dve)
    op("vector", lambda e: e.tensor_mul(g_var[:], g_mean[:], g_mean[:]), inc=c_dve)
    op("vector", lambda e: e.scalar_tensor_tensor(
        g_var[:], g_s2[:], 1.0 / D, g_var[:], ALU.mult, ALU.subtract), inc=c_dve)
    v_var = c_dve.v
    op("scalar", lambda e: e.activation(g_rstd[:], g_var[:], AF.Ln,
                                        bias=s_eps[:, 0:1]),
       waits=[(c_dve, v_var)], inc=c_act)
    op("scalar", lambda e: e.activation(g_rstd[:], g_rstd[:], AF.Exp, scale=-0.5),
       inc=c_act)
    v_rstd2 = c_act.v
    op("vector", lambda e: e.scalar_tensor_tensor(
        g_nmr[:], g_mean[:], -1.0, g_rstd[:], ALU.mult, ALU.mult),
       waits=[(c_act, v_rstd2)], inc=c_dve)
    v_nmr2 = c_dve.v

    v_gate_done_dve = 0
    v_gate_done_pe = 0
    gate_ps = [pA, pB]
    vU = {}
    vHI = {}
    vLO = {}
    vTRH = {}
    vTRL = {}
    vHIEV = {}
    vLOEV = {}
    vGATE = {}

    def ln2_stageA(tt):
        q = tt % 2
        op("vector", lambda e, tt=tt, q=q: e.tensor_scalar(
            lnsq[:, q * D:(q + 1) * D], x_sb[:, tt, :], g_rstd[:, tt:tt + 1],
            g_nmr[:, tt:tt + 1], ALU.mult, ALU.add),
           waits=[(c_act, v_rstd2), (c_dve, v_nmr2),
                  (c_act, vHI.get(tt - 2, 0))], inc=c_dve)
        vU[tt] = c_dve.v
        op("scalar", lambda e, q=q: e.copy(xn_stage[:, q, :], lnsq[:, q * D:(q + 1) * D]),
           waits=[(c_dve, vU[tt]), (c_pe, vTRH.get(tt - 2, 0))], inc=c_act)
        vHI[tt] = c_act.v
        op("vector", lambda e, q=q: e.tensor_tensor(
            xn_stage[:, 2 + q, :], lnsq[:, q * D:(q + 1) * D], xn_stage[:, q, :],
            ALU.subtract),
           waits=[(c_act, vHI[tt]), (c_pe, vTRL.get(tt - 2, 0))], inc=c_dve)
        vLO[tt] = c_dve.v

    def ln2_stageB(tt):
        q = tt % 2
        for dc in range(DC):
            op("tensor", lambda e, dc=dc, q=q: e.transpose(
                pT[:, dc * P:(dc + 1) * P], xn_stage[:, q, dc * P:(dc + 1) * P],
                s_id[:]),
               waits=[(c_act, vHI[tt]), (c_act, vHIEV.get(tt - 1, 0))]
               if dc == 0 else [], inc=(c_pe if dc == DC - 1 else None))
        vTRH[tt] = c_pe.v
        for dc in range(DC):
            op("tensor", lambda e, dc=dc, q=q: e.transpose(
                pU[:, dc * P:(dc + 1) * P], xn_stage[:, 2 + q, dc * P:(dc + 1) * P],
                s_id[:]),
               waits=[(c_dve, vLO[tt]), (c_act, vLOEV.get(tt - 1, 0))]
               if dc == 0 else [], inc=(c_pe if dc == DC - 1 else None))
        vTRL[tt] = c_pe.v
        op("scalar", lambda e, tt=tt: e.copy(
            xn2f[:, :, tt * P:(tt + 1) * P],
            pT[:].rearrange("p (c f) -> p c f", c=DC)),
           waits=[(c_pe, vTRH[tt])], inc=c_act)
        vHIEV[tt] = c_act.v
        op("scalar", lambda e, tt=tt: e.copy(
            lo_fm[:, :, tt * P:(tt + 1) * P],
            pU[:].rearrange("p (c f) -> p c f", c=DC)),
           waits=[(c_pe, vTRL[tt])], inc=c_act)
        vLOEV[tt] = c_act.v

        gp = gate_ps[tt % 2]
        wg = vGATE.get(tt - 2)
        first = True
        for (gsrc, gmat) in ((xn2f, s_gh), (xn2f, s_gl), (lo_fm, s_gh)):
            for dc in range(DC):
                def f(e, gsrc=gsrc, gmat=gmat, dc=dc, tt=tt, gp=gp, first=first):
                    return e.matmul(gp[:, 0:NE], gsrc[:, dc, tt * P:(tt + 1) * P],
                                    gmat[:, dc, :], start=first, stop=False)
                w = []
                if first:
                    w = [(c_act, vHIEV[tt]), (c_act, vLOEV[tt]),
                         (c_dve, v_gate_consts_holder[0]), (c_setup, v_gpsetup)]
                    if wg:
                        w.append((c_dve, wg))
                op("tensor", f, waits=w, inc=None)
                first = False
        for (brow, last) in ((s_gbh, False), (s_gbl, True)):
            def f(e, brow=brow, gp=gp, tt=tt, last=last):
                return e.matmul(gp[:, 0:NE], s_ones[:], brow[:],
                                start=False, stop=last)
            op("tensor", f, inc=(c_pe if last else None))
        v_gmm = c_pe.v
        op("vector", lambda e, gp=gp: e.reduce_max(gc_t[:, 0:1], gp[:, 0:NE],
                                                   axis=AX.X),
           waits=[(c_pe, v_gmm)], inc=c_dve)
        op("vector", lambda e, gp=gp: e.tensor_scalar(
            gt_t[:, 0, :], gp[:, 0:NE], gc_t[:, 0:1], None, ALU.is_equal), inc=c_dve)
        op("vector", lambda e, gp=gp: e.scalar_tensor_tensor(
            gt_t[:, 1, :], gt_t[:, 0, :], -1e30, gp[:, 0:NE], ALU.mult, ALU.add),
           inc=c_dve)
        op("vector", lambda e: e.reduce_max(gc_t[:, 1:2], gt_t[:, 1, :], axis=AX.X),
           inc=c_dve)
        op("vector", lambda e: e.tensor_scalar(
            gt_t[:, 2, :], gt_t[:, 1, :], gc_t[:, 1:2], None, ALU.is_equal), inc=c_dve)
        op("vector", lambda e: e.tensor_tensor(
            gc_t[:, 2:3], gc_t[:, 1:2], gc_t[:, 0:1], ALU.subtract), inc=c_dve)
        v_d = c_dve.v
        op("scalar", lambda e: e.activation(gc_t[:, 3:4], gc_t[:, 2:3], AF.Sigmoid),
           waits=[(c_dve, v_d)], inc=c_act)
        v_sg = c_act.v
        op("vector", lambda e: e.tensor_scalar(
            gc_t[:, 5:6], gc_t[:, 3:4], -1.0, 1.0, ALU.mult, ALU.add),
           waits=[(c_act, v_sg)], inc=c_dve)
        op("vector", lambda e: e.tensor_scalar(
            gt_t[:, 3, :], gt_t[:, 0, :], gc_t[:, 5:6], None, ALU.mult), inc=c_dve)
        op("vector", lambda e, tt=tt: e.scalar_tensor_tensor(
            gateW[:, tt, :], gt_t[:, 2, :], gc_t[:, 3:4], gt_t[:, 3, :],
            ALU.mult, ALU.add), inc=c_dve)
        vGATE[tt] = c_dve.v

    for j in range(NT + 2):
        if j >= 2:
            ln2_stageB(j - 2)
        if j < NT:
            ln2_stageA(j)
    v_gate_done_dve = c_dve.v
    v_gate_done_pe = c_pe.v
    v_ln2_done_act = c_act.v

    op("gpsimd", lambda e: e.dma_start(out=s_b2[:], in_=d_b2f),
       waits=[(c_pe, v_attn_done_pe)], inc=c_setup)
    v_b2ld = c_setup.v

    w1_fill_v = {}
    w2_fill_v = {}
    w1_free_v = {}
    w2_free_v = {}
    mm1_ep = {}
    mm2_ev = {}
    xh_b = [pA, pC]
    gh_b = [pB, pD]
    m2_b = [pE, pF]
    g1 = [0]
    store_after = {}

    def w1_fill(ei, fs):
        slot = (ei * NFS + fs) % W1SLOTS
        waits = []
        fv = w1_free_v.get(slot)
        if fv:
            waits.append((c_pe, fv))
        else:
            waits.append((c_pe, v_attn_done_pe))
        op("gpsimd", lambda e, ei=ei, fs=fs, slot=slot: e.dma_start(
            out=w1rg[:, slot], in_=d_w1r[ei, fs]), waits=waits, inc=c_w1[slot])
        w1_fill_v[(ei, fs)] = (c_w1[slot], c_w1[slot].v)

    def w2_fill(ei, fs):
        gfs = ei * NFS + fs
        slot = gfs % W2SLOTS
        waits = []
        fv = w2_free_v.get(slot)
        if fv:
            waits.append((c_pe, fv))
        else:
            waits.append((c_pe, v_attn_done_pe))
        op("gpsimd", lambda e, ei=ei, fs=fs, slot=slot: e.dma_start(
            out=w2rg[:, slot], in_=d_w2[ei, fs * P:(fs + 1) * P, :]),
           waits=waits, inc=c_w2[slot])
        w2_fill_v[gfs] = (c_w2[slot], c_w2[slot].v)

    fill_q1 = [(ei, fs) for ei in range(NE) for fs in range(NFS)]
    fill_q2 = list(fill_q1)
    fq1 = [0]
    fq2 = [0]

    def pump_fills(n1, n2):
        for _ in range(n1):
            if fq1[0] < len(fill_q1):
                w1_fill(*fill_q1[fq1[0]])
                fq1[0] += 1
        for _ in range(n2):
            if fq2[0] < len(fill_q2):
                w2_fill(*fill_q2[fq2[0]])
                fq2[0] += 1

    pump_fills(W1SLOTS, W2SLOTS)

    for ei in range(NE):
        for jfi in range(NJ):
            for fsl in range(FS_JF):
                fs = jfi * FS_JF + fsl
                slot = (ei * NFS + fs) % W1SLOTS
                for th in range(NTH):
                    par = g1[0] % 2
                    xh, gh = xh_b[par], gh_b[par]
                    waits = [w1_fill_v[(ei, fs)],
                             (c_act, v_ln2_done_act),
                             (c_dve, v_gate_done_dve)]
                    ep = mm1_ep.get(par)
                    if ep:
                        waits.append((c_act, ep[0]))
                        waits.append((c_dve, ep[1]))
                    mv = mm2_ev.get("last_jf")
                    if mv and fsl == 0 and th == 0:
                        waits.append((c_dve, mv))
                    for hx in range(2):
                        ps = xh if hx == 0 else gh
                        for dc in range(DC):
                            def f(e, ps=ps, slot=slot, dc=dc, hx=hx, th=th):
                                return e.matmul(
                                    ps[:], w1rg[:, slot, dc, hx, :],
                                    xn2f[:, dc, th * 512:(th + 1) * 512],
                                    start=(dc == 0), stop=(dc == DC - 1))
                            op("tensor", f,
                               waits=waits if (hx == 0 and dc == 0) else [],
                               inc=(c_pe if (hx == 1 and dc == DC - 1) else None))
                    v_mm = c_pe.v
                    if th == NTH - 1:
                        w1_free_v[slot] = v_mm
                        pump_fills(1, 0)
                    op("scalar", lambda e, gh=gh, par=par, ei=ei, fs=fs: e.activation(
                        a1st[:, par, :], gh[:], AF.Silu,
                        bias=b1c[:, ei, NFS + fs:NFS + fs + 1]),
                       waits=[(c_pe, v_mm), (c_act, v_b1_done)], inc=c_act)
                    v_a1 = c_act.v

                    def fa2(e, xh=xh, par=par, ei=ei, fs=fs, fsl=fsl, th=th):
                        return e.scalar_tensor_tensor(
                            act_sb[:, fsl, th * 512:(th + 1) * 512], xh[:],
                            b1c[:, ei, fs:fs + 1], a1st[:, par, :],
                            ALU.add, ALU.mult)
                    op("vector", fa2, waits=[(c_act, v_a1), (c_pe, v_mm)],
                       inc=c_dve)
                    mm1_ep[par] = (v_a1, c_dve.v)
                    g1[0] += 1
            v_act_ready = c_dve.v
            for tt in range(NT):
                for dh in range(2):
                    gq = (tt * 2 + dh) % 2
                    ps = m2_b[gq]
                    waits = [(c_dve, v_act_ready),
                             (c_dve, v_gate_done_dve),
                             (c_setup, v_b2ld)]
                    for _fsl in range(FS_JF):
                        waits.append(w2_fill_v[ei * NFS + jfi * FS_JF + _fsl])
                    mv = mm2_ev.get(gq)
                    if mv:
                        waits.append((c_dve, mv))
                    first = True
                    if jfi == 0:
                        def fb(e, ps=ps, ei=ei, dh=dh):
                            return e.matmul(
                                ps[:], s_ones[:],
                                s_b2[:, ei * D + dh * 512: ei * D + (dh + 1) * 512],
                                start=True, stop=False)
                        op("tensor", fb, waits=waits, inc=None)
                        first = False
                        waits = []
                    for fsl in range(FS_JF):
                        fs = jfi * FS_JF + fsl
                        slot = (ei * NFS + fs) % W2SLOTS

                        def f(e, ps=ps, fsl=fsl, slot=slot, tt=tt, dh=dh,
                              first=first, fin=(fsl == FS_JF - 1)):
                            return e.matmul(
                                ps[:], act_sb[:, fsl, tt * P:(tt + 1) * P],
                                w2rg[:, slot, dh * 512:(dh + 1) * 512],
                                start=first, stop=fin)
                        op("tensor", f, waits=waits if first else [],
                           inc=(c_pe if fsl == FS_JF - 1 else None))
                        first = False
                        waits = []
                    v_mm = c_pe.v

                    def fc(e, ps=ps, tt=tt, dh=dh, ei=ei):
                        xs = x_sb[:, tt, dh * 512:(dh + 1) * 512]
                        return e.scalar_tensor_tensor(
                            xs, ps[:], gateW[:, tt, ei:ei + 1], xs,
                            ALU.mult, ALU.add)
                    op("vector", fc, waits=[(c_pe, v_mm)], inc=c_dve)
                    mm2_ev[gq] = c_dve.v
                    if ei == NE - 1 and jfi == NJ - 1:
                        store_after[(tt, dh)] = c_dve.v
            v_jf_pe = c_pe.v
            for fsl in range(FS_JF):
                w2_free_v[(ei * NFS + jfi * FS_JF + fsl) % W2SLOTS] = v_jf_pe
            pump_fills(0, FS_JF)
            mm2_ev["last_jf"] = c_dve.v

    out_r = d_out.rearrange("(t p) d -> p t d", p=P)
    for tt in range(NT):
        op("sync", lambda e, tt=tt: e.dma_start(
            out=out_r[:, tt, :], in_=x_sb[:, tt, :]),
           waits=[(c_dve, store_after[(tt, 1)])], inc=c_sy)
    v_all_stored = c_sy.v

    tapmap = {
        "xn2f": (xn2f, BF16, [P, DC, TCORE]),
        "act_sb": (act_sb, BF16, [P, FS_JF, TCORE]),
        "gateW": (gateW, F32, [P, NT, NE]),
        "g_rstd": (g_rstd, F32, [P, NT]),
        "x_sb": (x_sb, F32, [P, NT, D]),
    }
    for name in taps:
        t, dt_, shp = tapmap[name]
        dtap = nc.dram_tensor("tap_" + name, shp, F32, kind="ExternalOutput").ap()
        if dt_ == F32:
            op("sync", lambda e, t=t, dtap=dtap: e.dma_start(out=dtap, in_=t[:]),
               waits=[(c_dve, c_dve.v), (c_act, c_act.v), (c_pe, c_pe.v)], inc=c_sy)
        else:
            op("gpsimd", lambda e, t=t, dtap=dtap: e.dma_start(out=dtap, in_=t[:]),
               waits=[(c_dve, c_dve.v), (c_act, c_act.v), (c_pe, c_pe.v)], inc=c_setup)
    v_tap_wq = c_setup.v

    op("sync", lambda e: None, waits=[(c_sy, c_sy.v), (c_setup, v_tap_wq)])

    with nc.Block() as block:
        own_ctr = {"scalar": c_act, "vector": c_dve}

        def runner(name):
            def body(eng):
                marks = {}
                own = own_ctr.get(name)
                n_own = 0
                for fn, waits, inc in streams[name]:
                    if own is not None and n_own > 0:
                        eng.wait_ge(own.sem, n_own)
                    for (ctr, v) in waits:
                        if v > marks.get(ctr.key, 0):
                            eng.wait_ge(ctr.sem, v)
                            marks[ctr.key] = v
                    ins = fn(eng)
                    if inc is not None:
                        assert ins is not None
                        ins.then_inc(inc.sem, inc.step)
                        if inc is own:
                            n_own += 1
            return body
        block.tensor(runner("tensor"))
        block.scalar(runner("scalar"))
        block.vector(runner("vector"))
        block.gpsimd(runner("gpsimd"))
        block.sync(runner("sync"))
    es.close()
    return nc


def prep_inputs(L0_attn_x, L0_attn_Wq, L0_attn_Wk, L0_attn_Wv, L0_attn_Wo,
                L0_ffn_gate_w, L0_ffn_gate_b, L0_ffn_w1, L0_ffn_b1,
                L0_ffn_w2, L0_ffn_b2):
    f32 = np.float32
    x = np.asarray(L0_attn_x, f32)
    x2 = np.ascontiguousarray(x.reshape(2 * 8192, D))

    def wt(w):
        return np.ascontiguousarray(
            np.asarray(w, f32).T.reshape(DC, P, D).transpose(1, 0, 2))
    wqt, wkt, wvt, wot = wt(L0_attn_Wq), wt(L0_attn_Wk), wt(L0_attn_Wv), wt(L0_attn_Wo)
    gw = np.ascontiguousarray(
        np.asarray(L0_ffn_gate_w, f32).reshape(DC, P, NE).transpose(1, 0, 2))
    gb = np.asarray(L0_ffn_gate_b, f32).reshape(1, NE)
    w1r = np.ascontiguousarray(
        np.asarray(L0_ffn_w1, f32).reshape(NE, DC, P, 2, NFS, P)
        .transpose(0, 4, 2, 1, 3, 5))
    w2 = np.ascontiguousarray(np.asarray(L0_ffn_w2, f32))
    b1r = np.ascontiguousarray(
        np.asarray(L0_ffn_b1, f32).reshape(NE, 2, NFS, P).transpose(2, 0, 1, 3))
    b2f = np.ascontiguousarray(np.asarray(L0_ffn_b2, f32).reshape(1, NE * D))
    ident = np.eye(P, dtype=f32)
    onesr = np.ones((1, P), f32)
    epsc = np.tile(np.array([[EPS, -12.0]], f32), (P, 1))
    in_maps = []
    for c in range(8):
        seq = c // 4
        in_maps.append(dict(
            x=np.ascontiguousarray(x2[c * TCORE:(c + 1) * TCORE]),
            xg=np.ascontiguousarray(x[seq, 0:GT, :]),
            wqt=wqt, wkt=wkt, wvt=wvt, wot=wot, gw=gw, gb=gb,
            w1r=w1r, w2=w2, b1r=b1r, b2f=b2f,
            ident=ident, onesr=onesr, epsc=epsc))
    return in_maps


def kernel(**inputs):
    in_maps = prep_inputs(**inputs)
    nc = build_nc()
    res = run_bass_kernel_spmd(nc, in_maps, core_ids=list(range(8)))
    out = np.concatenate([r["out"] for r in res.results], axis=0)
    return out.reshape(2, 8192, D)


# version 20
# speedup vs baseline: 1.0287x; 1.0287x over previous
import contextlib
import numpy as np
import ml_dtypes
import concourse.bass as bass
import concourse.mybir as mybir
from concourse.bass_utils import run_bass_kernel_spmd

F32 = mybir.dt.float32
BF16 = mybir.dt.bfloat16
AF = mybir.ActivationFunctionType
ALU = mybir.AluOpType
AX = mybir.AxisListType

P = 128
D = 1024
DC = 8
TCORE = 2048
NT = 16
H = 16
HD = 64
NWIN = 16
GT = 64
NE = 8
FH = 4096
NFS = 32
NJ = 4
FS_JF = NFS // NJ
NTH = 4
EPS = 1e-5
W1SLOTS = 4
W2SLOTS = 10


class Ctr:
    __slots__ = ("sem", "v", "step", "key")
    _next = 0

    def __init__(self, sem, step=1):
        self.sem = sem
        self.v = 0
        self.step = step
        self.key = Ctr._next
        Ctr._next += 1


def build_nc(taps=()):
    nc = bass.Bass("TRN2", target_bir_lowering=False)
    es = contextlib.ExitStack()

    d_x = nc.dram_tensor("x", [P, NT, D], F32, kind="ExternalInput").ap()
    d_xg = nc.dram_tensor("xg", [GT, D], F32, kind="ExternalInput").ap()
    d_wqt = nc.dram_tensor("wqt", [P, DC, D], F32, kind="ExternalInput").ap()
    d_wkt = nc.dram_tensor("wkt", [P, DC, D], F32, kind="ExternalInput").ap()
    d_wvt = nc.dram_tensor("wvt", [P, DC, D], F32, kind="ExternalInput").ap()
    d_wot = nc.dram_tensor("wot", [P, DC, D], F32, kind="ExternalInput").ap()
    d_gw = nc.dram_tensor("gw", [P, DC, NE], F32, kind="ExternalInput").ap()
    d_gb = nc.dram_tensor("gb", [1, NE], F32, kind="ExternalInput").ap()
    d_w1r = nc.dram_tensor("w1r", [NE, NFS, P, DC, 2, P], F32, kind="ExternalInput").ap()
    d_w2 = nc.dram_tensor("w2", [NE, FH, D], F32, kind="ExternalInput").ap()
    d_b1r = nc.dram_tensor("b1r", [NFS, NE, 2, P], F32, kind="ExternalInput").ap()
    d_b2f = nc.dram_tensor("b2f", [1, NE * D], F32, kind="ExternalInput").ap()
    d_id = nc.dram_tensor("ident", [P, P], F32, kind="ExternalInput").ap()
    d_ones = nc.dram_tensor("onesr", [1, P], F32, kind="ExternalInput").ap()
    d_eps = nc.dram_tensor("epsc", [P, 2], F32, kind="ExternalInput").ap()
    d_out = nc.dram_tensor("out", [P, NT, D], F32, kind="ExternalOutput").ap()

    x_sb = nc.alloc_sbuf_tensor("x_sb", [P, NT, D], F32)
    s_id = nc.alloc_sbuf_tensor("s_id", [P, P], BF16)
    s_ones = nc.alloc_sbuf_tensor("s_ones", [1, P], BF16)
    s_eps = nc.alloc_sbuf_tensor("s_eps", [P, 2], F32)
    s_gwf = nc.alloc_sbuf_tensor("s_gwf", [P, DC, NE], F32)
    s_gh = nc.alloc_sbuf_tensor("s_gh", [P, DC, NE], BF16)
    s_gl = nc.alloc_sbuf_tensor("s_gl", [P, DC, NE], BF16)
    s_gbf = nc.alloc_sbuf_tensor("s_gbf", [1, NE], F32)
    s_gbh = nc.alloc_sbuf_tensor("s_gbh", [1, NE], BF16)
    s_gbl = nc.alloc_sbuf_tensor("s_gbl", [1, NE], BF16)
    b1c = nc.alloc_sbuf_tensor("b1c", [P, NE, 2 * NFS], F32)
    gateW = nc.alloc_sbuf_tensor("gateW", [P, NT, NE], F32)
    g_s1 = nc.alloc_sbuf_tensor("g_s1", [P, NT], F32)
    g_s2 = nc.alloc_sbuf_tensor("g_s2", [P, NT], F32)
    g_mean = nc.alloc_sbuf_tensor("g_mean", [P, NT], F32)
    g_var = nc.alloc_sbuf_tensor("g_var", [P, NT], F32)
    g_rstd = nc.alloc_sbuf_tensor("g_rstd", [P, NT], F32)
    g_nmr = nc.alloc_sbuf_tensor("g_nmr", [P, NT], F32)
    lnsq = nc.alloc_sbuf_tensor("lnsq", [P, 2 * D], F32)
    xn_stage = nc.alloc_sbuf_tensor("xn_stage", [P, 4, D], BF16)
    gt_t = nc.alloc_sbuf_tensor("gt_t", [P, 4, NE], F32)
    gc_t = nc.alloc_sbuf_tensor("gc_t", [P, 8], F32)
    es_b1 = contextlib.ExitStack()
    b1tm = es_b1.enter_context(nc.sbuf_tensor("b1tm", [NFS, NE * 2, P], BF16))

    pA = nc.alloc_psum_tensor("pA", [P, 512], F32)
    pB = nc.alloc_psum_tensor("pB", [P, 512], F32)
    pC = nc.alloc_psum_tensor("pC", [P, 512], F32)
    pD = nc.alloc_psum_tensor("pD", [P, 512], F32)
    pE = nc.alloc_psum_tensor("pE", [P, 512], F32)
    pF = nc.alloc_psum_tensor("pF", [P, 512], F32)
    pT = nc.alloc_psum_tensor("pT", [P, D], BF16)
    pU = nc.alloc_psum_tensor("pU", [P, D], BF16)

    sem_pe = es.enter_context(nc.semaphore("sem_pe"))
    sem_act = es.enter_context(nc.semaphore("sem_act"))
    sem_dve = es.enter_context(nc.semaphore("sem_dve"))
    sem_setup = es.enter_context(nc.semaphore("sem_setup"))
    sem_sy = es.enter_context(nc.semaphore("sem_sy"))
    c_pe = Ctr(sem_pe)
    c_act = Ctr(sem_act)
    c_dve = Ctr(sem_dve)
    c_setup = Ctr(sem_setup, 16)
    c_sy = Ctr(sem_sy, 16)
    c_wsl = [Ctr(es.enter_context(nc.semaphore(f"sem_wsl{i}")), 16) for i in range(2)]
    c_w1 = [Ctr(es.enter_context(nc.semaphore(f"sem_w1_{i}")), 16) for i in range(W1SLOTS)]
    c_w2 = [Ctr(es.enter_context(nc.semaphore(f"sem_w2_{i}")), 16) for i in range(W2SLOTS)]
    c_x = [Ctr(es.enter_context(nc.semaphore(f"sem_x{i}")), 16) for i in range(4)]

    streams = {k: [] for k in ("tensor", "scalar", "vector", "gpsimd", "sync")}

    def op(eng, fn, waits=(), inc=None):
        streams[eng].append((fn, tuple((c, v) for (c, v) in waits), inc))
        if inc is not None:
            inc.v += inc.step
        return inc.v if inc is not None else None

    v_sync_all = [0]
    xg_sb_holder = [None]

    op("sync", lambda e: e.dma_start(out=s_eps[:], in_=d_eps), inc=c_sy)
    op("sync", lambda e: e.dma_start(out=s_gwf[:], in_=d_gw), inc=c_sy)
    op("sync", lambda e: e.dma_start(out=s_gbf[:], in_=d_gb), inc=c_sy)
    op("sync", lambda e: e.dma_start(out=xg_sb_holder[0][:], in_=d_xg), inc=c_sy)
    v_sync_all[0] = c_sy.v
    xr = d_x
    for ch in range(4):
        op("sync" if ch % 2 == 0 else "scalar", lambda e, ch=ch: e.dma_start(
            out=x_sb[:, ch * 4:(ch + 1) * 4, :], in_=xr[:, ch * 4:(ch + 1) * 4, :]),
           inc=c_x[ch])

    op("gpsimd", lambda e: e.dma_start(out=s_id[:], in_=d_id), inc=c_setup)
    op("gpsimd", lambda e: e.dma_start(out=s_ones[:], in_=d_ones), inc=c_setup)
    op("gpsimd", lambda e: e.dma_start(out=s_gh[:], in_=d_gw), inc=c_setup)
    op("gpsimd", lambda e: e.dma_start(out=s_gbh[:], in_=d_gb), inc=c_setup)
    op("gpsimd", lambda e: e.dma_start(out=b1tm[:], in_=d_b1r.rearrange(
        "c e h f -> c (e h) f")), inc=c_setup)
    v_gpsetup = c_setup.v
    v_id = v_gpsetup

    v_gate_consts_holder = [0]

    b1_ev = []
    for grp in range(2):
        pdst = pT if grp == 0 else pU
        for k in range(8):
            idx = grp * 8 + k
            e_, h_ = idx // 2, idx % 2

            def f(eng, pdst=pdst, k=k, idx=idx):
                return eng.transpose(pdst[:, k * NFS:(k + 1) * NFS],
                                     b1tm[:, idx, :], s_id[0:NFS, 0:NFS])
            w = [(c_setup, v_gpsetup)] if (grp == 0 and k == 0) else []
            op("tensor", f, waits=w, inc=c_pe)
        v_tr = c_pe.v

        def fev(eng, pdst=pdst, grp=grp):
            return eng.copy(
                b1c[:].rearrange("p e (h c) -> p (e h) c", h=2)[:, grp * 8:(grp + 1) * 8, :],
                pdst[:, 0:8 * NFS].rearrange("p (k c) -> p k c", c=NFS))
        op("scalar", fev, waits=[(c_pe, v_tr)], inc=c_act)
    v_b1_done = c_act.v
    v_b1_tr_pe = c_pe.v
    es_b1.close()

    def ln_stats(src_tiles, n_tiles, parts=P, chunked=False):
        v_r = {}
        v_n = {}
        step = 4 if chunked else n_tiles
        for c0 in range(0, n_tiles, step):
            for tt in range(c0, min(c0 + step, n_tiles)):
                src = src_tiles(tt)
                op("scalar", lambda e, src=src, tt=tt: e.activation(
                    lnsq[0:parts, 0:D], src, AF.Square,
                    accum_out=g_s2[0:parts, tt:tt + 1]),
                   waits=[(c_x[tt // 4], 16)], inc=c_act)
                op("vector", lambda e, src=src, tt=tt: e.reduce_sum(
                    g_s1[0:parts, tt:tt + 1], src, axis=AX.X),
                   waits=[(c_x[tt // 4], 16)], inc=c_dve)
            v_stats = (c_act.v, c_dve.v)
            c1 = min(c0 + step, n_tiles)
            op("vector", lambda e, c0=c0, c1=c1: e.tensor_scalar_mul(
                g_mean[0:parts, c0:c1], g_s1[0:parts, c0:c1], 1.0 / D),
               waits=[(c_act, v_stats[0])], inc=c_dve)
            op("vector", lambda e, c0=c0, c1=c1: e.tensor_mul(
                g_var[0:parts, c0:c1], g_mean[0:parts, c0:c1],
                g_mean[0:parts, c0:c1]), inc=c_dve)
            op("vector", lambda e, c0=c0, c1=c1: e.scalar_tensor_tensor(
                g_var[0:parts, c0:c1], g_s2[0:parts, c0:c1], 1.0 / D,
                g_var[0:parts, c0:c1], ALU.mult, ALU.subtract), inc=c_dve)
            v_var = c_dve.v
            op("scalar", lambda e, c0=c0, c1=c1: e.activation(
                g_rstd[0:parts, c0:c1], g_var[0:parts, c0:c1], AF.Ln,
                bias=s_eps[0:parts, 0:1]), waits=[(c_dve, v_var), (c_sy, 64)],
               inc=c_act)
            op("scalar", lambda e, c0=c0, c1=c1: e.activation(
                g_rstd[0:parts, c0:c1], g_rstd[0:parts, c0:c1], AF.Exp,
                scale=-0.5), inc=c_act)
            v_rstd = c_act.v
            op("vector", lambda e, c0=c0, c1=c1: e.scalar_tensor_tensor(
                g_nmr[0:parts, c0:c1], g_mean[0:parts, c0:c1], -1.0,
                g_rstd[0:parts, c0:c1], ALU.mult, ALU.mult),
               waits=[(c_act, v_rstd)], inc=c_dve)
            for tt in range(c0, c1):
                v_r[tt] = v_rstd
                v_n[tt] = c_dve.v
        return v_r, v_n

    tr_parity = [0]
    stage_last_tr = {}

    def norm_transpose(tt, dst_view, v_extra_waits=(), src=None, parts=P,
                       n_dc=DC, stage_slot=None, rstd_col=None, nmr_col=None,
                       out_stage=None):
        par = tr_parity[0] % 2 if stage_slot is None else stage_slot
        tr_parity[0] += 1
        stg = xn_stage[0:parts, par, :] if out_stage is None else out_stage
        src = x_sb[:, tt, :] if src is None else src
        rc = g_rstd[0:parts, tt:tt + 1] if rstd_col is None else rstd_col
        nc_ = g_nmr[0:parts, tt:tt + 1] if nmr_col is None else nmr_col
        sg = stage_last_tr.get(par, 0)
        pg = stage_last_tr.get(("ev", par), 0)
        op("vector", lambda e, src=src, stg=stg, rc=rc, nc_=nc_: e.tensor_scalar(
            stg, src, rc, nc_, ALU.mult, ALU.add),
           waits=list(v_extra_waits) + [(c_pe, sg)], inc=c_dve)
        v_norm = c_dve.v
        pdst = pT if par == 0 else pU
        for dc in range(n_dc):
            op("tensor", lambda e, dc=dc, stg=stg, pdst=pdst, parts=parts: e.transpose(
                pdst[:, dc * parts:(dc + 1) * parts],
                stg.rearrange("p (c f) -> p c f", f=P)[:, dc, :],
                s_id[0:parts, 0:parts]),
               waits=[(c_dve, v_norm), (c_setup, v_id), (c_act, pg)], inc=c_pe)
        v_tr = c_pe.v
        stage_last_tr[par] = v_tr
        op("scalar", lambda e, pdst=pdst, dst_view=dst_view, parts=parts, n_dc=n_dc:
           e.copy(dst_view,
                         pdst[:, 0:n_dc * parts].rearrange("p (c f) -> p c f", c=n_dc)),
           waits=[(c_pe, v_tr)], inc=c_act)
        stage_last_tr[("ev", par)] = c_act.v
        return c_act.v

    es_attn = contextlib.ExitStack()
    xn1h = es_attn.enter_context(nc.sbuf_tensor("xn1h", [P, DC, 1024], BF16))
    qf = es_attn.enter_context(nc.sbuf_tensor("qf", [P, DC, 1024], BF16))
    kf = es_attn.enter_context(nc.sbuf_tensor("kf", [P, DC, 1024], BF16))
    vt = es_attn.enter_context(nc.sbuf_tensor("vt", [P, DC, 1024], BF16))
    aof = es_attn.enter_context(nc.sbuf_tensor("aof", [P, DC, 1024], BF16))
    wsl = es_attn.enter_context(nc.sbuf_tensor("wsl", [P, 2, DC, 1024], BF16))
    xg_sb = es_attn.enter_context(nc.sbuf_tensor("xg_sb", [GT, D], F32))
    xg_sb_holder[0] = xg_sb
    xgn_fm = es_attn.enter_context(nc.sbuf_tensor("xgn_fm", [P, DC, GT], BF16))
    kgf = es_attn.enter_context(nc.sbuf_tensor("kgf", [P, DC, GT], BF16))
    vgt = es_attn.enter_context(nc.sbuf_tensor("vgt", [GT, D], BF16))
    gx = es_attn.enter_context(nc.sbuf_tensor("gx", [GT, 8], F32))
    P_sb = es_attn.enter_context(nc.sbuf_tensor("P_sb", [P, 4, 192], BF16))
    sPT = es_attn.enter_context(nc.sbuf_tensor("sPT", [P, 4, 256], BF16))
    rr = es_attn.enter_context(nc.sbuf_tensor("rr", [P, 4, 4], F32))



    v_ln1_rstd, v_ln1_nmr = ln_stats(lambda tt: x_sb[:, tt, :], NT, chunked=True)

    op("scalar", lambda e: e.activation(
        lnsq[0:GT, 0:D], xg_sb[:], AF.Square, accum_out=gx[:, 1:2]),
       waits=[(c_sy, v_sync_all[0])], inc=c_act)
    op("vector", lambda e: e.reduce_sum(gx[:, 0:1], xg_sb[:], axis=AX.X),
       waits=[(c_sy, v_sync_all[0])], inc=c_dve)
    v_gxs = (c_act.v, c_dve.v)
    op("vector", lambda e: e.tensor_scalar_mul(gx[:, 2:3], gx[:, 0:1], 1.0 / D),
       waits=[(c_act, v_gxs[0])], inc=c_dve)
    op("vector", lambda e: e.tensor_mul(gx[:, 3:4], gx[:, 2:3], gx[:, 2:3]), inc=c_dve)
    op("vector", lambda e: e.scalar_tensor_tensor(
        gx[:, 3:4], gx[:, 1:2], 1.0 / D, gx[:, 3:4], ALU.mult, ALU.subtract), inc=c_dve)
    v_gv = c_dve.v
    op("scalar", lambda e: e.activation(gx[:, 4:5], gx[:, 3:4], AF.Ln,
                                        bias=s_eps[0:GT, 0:1]),
       waits=[(c_dve, v_gv), (c_sy, 64)], inc=c_act)
    op("scalar", lambda e: e.activation(gx[:, 4:5], gx[:, 4:5], AF.Exp, scale=-0.5),
       inc=c_act)
    v_grstd = c_act.v
    op("vector", lambda e: e.scalar_tensor_tensor(
        gx[:, 5:6], gx[:, 2:3], -1.0, gx[:, 4:5], ALU.mult, ALU.mult),
       waits=[(c_act, v_grstd)], inc=c_dve)
    v_gnmr = c_dve.v
    v_xgn = norm_transpose(
        0, xgn_fm[:], v_extra_waits=[(c_act, v_grstd), (c_dve, v_gnmr)],
        src=xg_sb[:], parts=GT, rstd_col=gx[0:GT, 4:5], nmr_col=gx[0:GT, 5:6])

    proj_ps = [pE, pF]
    S_banks = [pA, pB, pE]
    PV_banks = [pC, pD]
    v_evac_prev = {}
    v_aof_done = [0]
    v_attn_tail_pe = [0]

    def wproj_load(slot, dram, vfree_pe):
        op("gpsimd", lambda e: e.dma_start(out=wsl[:, slot], in_=dram),
           waits=([(c_pe, vfree_pe)] if vfree_pe else []), inc=c_wsl[slot])
        return (c_wsl[slot], c_wsl[slot].v)

    def projection(dst, wslot, vw, half, scale, v_xn, kind, extra=()):
        par = [0]
        for ic in range(DC):
            for tc in range(2):
                ps = proj_ps[par[0] % 2]
                waits = [vw, (c_act, v_xn)] + list(extra)
                pv = v_evac_prev.get(("proj", par[0] % 2))
                if pv:
                    waits.append((c_act, pv))
                for dc in range(DC):
                    def f(e, ps=ps, dc=dc, ic=ic, tc=tc, wslot=wslot, half=half):
                        if kind == "fm":
                            return e.matmul(
                                ps[:], wsl[:, wslot, dc, ic * P:(ic + 1) * P],
                                xn1h[:, dc, tc * 512:(tc + 1) * 512],
                                start=(dc == 0), stop=(dc == DC - 1))
                        else:
                            return e.matmul(
                                ps[:], xn1h[:, dc, ic * P:(ic + 1) * P],
                                wsl[:, wslot, dc, tc * 512:(tc + 1) * 512],
                                start=(dc == 0), stop=(dc == DC - 1))
                    op("tensor", f, waits=waits if dc == 0 else [], inc=(c_pe if dc == DC - 1 else None))
                v_mm = c_pe.v

                def fe(e, ps=ps, ic=ic, tc=tc, scale=scale):
                    return e.activation(dst[:, ic, tc * 512:(tc + 1) * 512], ps[:],
                                        AF.Copy, scale=scale)
                op("scalar", fe, waits=[(c_pe, v_mm)], inc=c_act)
                v_evac_prev[("proj", par[0] % 2)] = c_act.v
                par[0] += 1
        return c_act.v, c_pe.v

    for half in range(2):
        hbase = half * 8
        v_xn1 = 0
        for t8 in range(8):
            tt = hbase + t8
            waits = [(c_act, v_ln1_rstd[tt]), (c_dve, v_ln1_nmr[tt])]
            if half == 1:
                waits.append((c_pe, v_attn_tail_pe[0]))
            v_xn1 = norm_transpose(tt, xn1h[:, :, t8 * P:(t8 + 1) * P],
                                   v_extra_waits=waits)
        vfree0 = v_attn_tail_pe[0] if half == 1 else v_b1_tr_pe
        vwq = wproj_load(0, d_wqt, vfree0)
        extra = []
        vv = v_evac_prev.get("oproj1")
        if vv:
            extra.append((c_dve, vv))
        v_q_act, v_q_pe = projection(qf, 0, vwq, half, 0.125, v_xn1, "fm",
                                     extra=extra)
        vwk = wproj_load(1, d_wkt, vfree0)
        v_k_act, v_k_pe = projection(kf, 1, vwk, half, 1.0, v_xn1, "fm")
        if half == 0:
            for ic in range(DC):
                ps = proj_ps[ic % 2]
                waits = [vwk, (c_act, v_xgn)]
                pv = v_evac_prev.get(("proj", ic % 2))
                if pv:
                    waits.append((c_act, pv))
                for dc in range(DC):
                    def f(e, ps=ps, dc=dc, ic=ic):
                        return e.matmul(ps[:, 0:GT],
                                        wsl[:, 1, dc, ic * P:(ic + 1) * P],
                                        xgn_fm[:, dc, :],
                                        start=(dc == 0), stop=(dc == DC - 1))
                    op("tensor", f, waits=waits if dc == 0 else [],
                       inc=(c_pe if dc == DC - 1 else None))
                v_mm = c_pe.v
                op("scalar", lambda e, ps=ps, ic=ic: e.copy(
                    kgf[:, ic, :], ps[:, 0:GT]), waits=[(c_pe, v_mm)], inc=c_act)
                v_evac_prev[("proj", ic % 2)] = c_act.v
            v_k_pe = c_pe.v
        vwv = wproj_load(0, d_wvt, v_q_pe)
        v_v_act, v_v_pe = projection(vt, 0, vwv, half, 1.0, v_xn1, "tm")
        if half == 0:
            for tc in range(2):
                ps = proj_ps[tc % 2]
                waits = [vwv, (c_act, v_xgn)]
                pv = v_evac_prev.get(("proj", tc % 2))
                if pv:
                    waits.append((c_act, pv))
                for dc in range(DC):
                    def f(e, ps=ps, dc=dc, tc=tc):
                        return e.matmul(ps[0:GT, :], xgn_fm[:, dc, :],
                                        wsl[:, 0, dc, tc * 512:(tc + 1) * 512],
                                        start=(dc == 0), stop=(dc == DC - 1))
                    op("tensor", f, waits=waits if dc == 0 else [],
                       inc=(c_pe if dc == DC - 1 else None))
                v_mm = c_pe.v
                op("scalar", lambda e, ps=ps, tc=tc: e.copy(
                    vgt[:, tc * 512:(tc + 1) * 512], ps[0:GT, :]),
                   waits=[(c_pe, v_mm)], inc=c_act)
                v_evac_prev[("proj", tc % 2)] = c_act.v
        vwo = wproj_load(1, d_wot, v_k_pe)

        LAG = 2
        NPAIR = 8 * H
        v_seed_act = c_act.v
        vS = {}
        vEXP = {}
        vPN = {}
        vPT = {}
        vPTEV = {}
        vPV = {}
        vAO = {}

        def pemit_S(pi):
            w8, head = pi // H, pi % H
            sb = S_banks[pi % 3]
            pb = (head % 2) * HD
            ic = head // 2
            waits = [(c_act, v_q_act), (c_act, v_k_act),
                     (c_act, vEXP.get(pi - 3, v_seed_act))]

            def fs1(e, sb=sb, pb=pb, ic=ic, w8=w8):
                return e.matmul(sb[:, 0:GT], qf[pb:pb + HD, ic, w8 * P:(w8 + 1) * P],
                                kgf[pb:pb + HD, ic, :], start=True, stop=True)
            op("tensor", fs1, waits=waits, inc=None)

            def fs2(e, sb=sb, pb=pb, ic=ic, w8=w8):
                return e.matmul(sb[:, GT:192], qf[pb:pb + HD, ic, w8 * P:(w8 + 1) * P],
                                kf[pb:pb + HD, ic, w8 * P:(w8 + 1) * P],
                                start=True, stop=True)
            op("tensor", fs2, inc=c_pe)
            vS[pi] = c_pe.v

        def pemit_front(pi):
            sb = S_banks[pi % 3]
            op("scalar", lambda e, sb=sb, pi=pi: e.activation(
                P_sb[:, pi % 4, :], sb[:, 0:192], AF.Exp,
                bias=s_eps[:, 1:2], accum_out=rr[:, pi % 4, 1:2]),
               waits=[(c_pe, vS[pi]), (c_pe, vPT.get(pi - 4, 0)), (c_sy, 64)],
               inc=c_act)
            vEXP[pi] = c_act.v
            op("vector", lambda e, pi=pi: e.reciprocal(
                rr[:, pi % 4, 2:3], rr[:, pi % 4, 1:2]),
               waits=[(c_act, vEXP[pi])], inc=c_dve)
            op("vector", lambda e, pi=pi: e.tensor_scalar_mul(
                P_sb[:, pi % 4, :], P_sb[:, pi % 4, :], rr[:, pi % 4, 2:3]),
               inc=c_dve)
            vPN[pi] = c_dve.v

        def pemit_tr(pi):
            ptb = pT if pi % 2 == 0 else pU

            def ft1(e, ptb=ptb, pi=pi):
                return e.transpose(ptb[0:GT, 0:P], P_sb[:, pi % 4, 0:GT], s_id[:])
            op("tensor", ft1,
               waits=[(c_dve, vPN[pi]), (c_dve, vPTEV.get(pi - 2, 0))], inc=None)

            def ft2(e, ptb=ptb, pi=pi):
                return e.transpose(ptb[:, P:256], P_sb[:, pi % 4, GT:192], s_id[:])
            op("tensor", ft2, inc=c_pe)
            vPT[pi] = c_pe.v
            op("vector", lambda e, ptb=ptb, pi=pi: e.tensor_copy(
                sPT[:, pi % 4, :], ptb[:, 0:256]),
               waits=[(c_pe, vPT[pi]), (c_pe, vPV.get(pi - 4, 0))], inc=c_dve)
            vPTEV[pi] = c_dve.v

        def pemit_pv(pi):
            w8, head = pi // H, pi % H
            pvb = PV_banks[pi % 2]
            pb = (head % 2) * HD
            ic = head // 2

            def fpv1(e, pvb=pvb, head=head, pi=pi):
                return e.matmul(pvb[0:HD, 0:P], vgt[0:GT, head * HD:(head + 1) * HD],
                                sPT[0:GT, pi % 4, 0:P], start=True, stop=False)
            op("tensor", fpv1,
               waits=[(c_dve, vPTEV[pi]), (c_act, v_v_act),
                      (c_act, vAO.get(pi - 2, 0))], inc=None)

            def fpv2(e, pvb=pvb, head=head, w8=w8, pi=pi):
                return e.matmul(pvb[0:HD, 0:P], vt[:, w8, head * HD:(head + 1) * HD],
                                sPT[:, pi % 4, P:256], start=False, stop=True)
            op("tensor", fpv2, inc=c_pe)
            vPV[pi] = c_pe.v
            op("scalar", lambda e, pvb=pvb, pb=pb, ic=ic, w8=w8: e.copy(
                aof[pb:pb + HD, ic, w8 * P:(w8 + 1) * P], pvb[0:HD, 0:P]),
               waits=[(c_pe, vPV[pi])], inc=c_act)
            vAO[pi] = c_act.v

        def pemit_oproj(t8):
            v_ao_w = vAO[t8 * H + H - 1]
            for dh in range(2):
                gidx = t8 * 2 + dh
                ps = pF
                waits = [vwo, (c_act, v_ao_w)]
                pv = v_evac_prev.get("oproj1")
                if pv:
                    waits.append((c_dve, pv))
                pv2 = v_evac_prev.get(("proj", 1))
                if pv2:
                    waits.append((c_act, pv2))
                for ic in range(DC):
                    def f(e, ps=ps, ic=ic, t8=t8, dh=dh):
                        return e.matmul(ps[:], aof[:, ic, t8 * P:(t8 + 1) * P],
                                        wsl[:, 1, ic, dh * 512:(dh + 1) * 512],
                                        start=(ic == 0), stop=(ic == DC - 1))
                    op("tensor", f, waits=waits if ic == 0 else [],
                       inc=(c_pe if ic == DC - 1 else None))
                v_mm = c_pe.v

                def fr(e, ps=ps, t8=t8, dh=dh, hbase=hbase):
                    xs = x_sb[:, hbase + t8, dh * 512:(dh + 1) * 512]
                    return e.tensor_tensor(xs, ps[:], xs, ALU.add)
                op("vector", fr, waits=[(c_pe, v_mm)], inc=c_dve)
                v_evac_prev["oproj1"] = c_dve.v
                if dh == 1:
                    tt = hbase + t8
                    v_add = c_dve.v
                    op("scalar", lambda e, tt=tt: e.activation(
                        lnsq[:, 0:D], x_sb[:, tt, :], AF.Square,
                        accum_out=g_s2[:, tt:tt + 1]),
                       waits=[(c_dve, v_add)], inc=c_act)
                    op("vector", lambda e, tt=tt: e.reduce_sum(
                        g_s1[:, tt:tt + 1], x_sb[:, tt, :], axis=AX.X), inc=c_dve)

        for j in range(NPAIR + 24):
            t8o = (j - 22) // 16
            if j >= 22 and (j - 22) % 16 == 0 and 0 <= t8o < 8:
                pemit_oproj(t8o)
            if 0 <= j - 6 < NPAIR:
                pemit_pv(j - 6)
            if 0 <= j - 5 < NPAIR:
                pemit_tr(j - 5)
            if 0 <= j - 3 < NPAIR:
                pemit_front(j - 3)
            if j < NPAIR:
                pemit_S(j)
        v_aof_done[0] = c_act.v

        v_attn_tail_pe[0] = c_pe.v
    v_attn_done_dve = c_dve.v
    v_attn_done_pe = c_pe.v
    v_attn_done_act = c_act.v

    es_attn.close()
    xn2f = nc.alloc_sbuf_tensor("xn2f", [P, DC, TCORE], BF16)
    act_sb = nc.alloc_sbuf_tensor("act_sb", [P, FS_JF, TCORE], BF16)
    lo_fm = act_sb
    w1rg = nc.alloc_sbuf_tensor("w1rg", [P, W1SLOTS, DC, 2, P], BF16)
    w2rg = nc.alloc_sbuf_tensor("w2rg", [P, W2SLOTS, D], BF16)
    a1st = nc.alloc_sbuf_tensor("a1st", [P, 2, 512], BF16)
    s_b2 = nc.alloc_sbuf_tensor("s_b2", [1, NE * D], BF16)

    op("vector", lambda e: e.tensor_tensor(s_gl[:], s_gwf[:], s_gh[:], ALU.subtract),
       waits=[(c_sy, v_sync_all[0]), (c_setup, v_gpsetup)], inc=c_dve)
    op("vector", lambda e: e.tensor_tensor(s_gbl[:], s_gbf[:], s_gbh[:], ALU.subtract),
       inc=c_dve)
    v_gate_consts_holder[0] = c_dve.v

    v_st = (v_attn_done_act, v_attn_done_dve)
    op("vector", lambda e: e.tensor_scalar_mul(g_mean[:], g_s1[:], 1.0 / D),
       waits=[(c_act, v_st[0])], inc=c_dve)
    op("vector", lambda e: e.tensor_mul(g_var[:], g_mean[:], g_mean[:]), inc=c_dve)
    op("vector", lambda e: e.scalar_tensor_tensor(
        g_var[:], g_s2[:], 1.0 / D, g_var[:], ALU.mult, ALU.subtract), inc=c_dve)
    v_var = c_dve.v
    op("scalar", lambda e: e.activation(g_rstd[:], g_var[:], AF.Ln,
                                        bias=s_eps[:, 0:1]),
       waits=[(c_dve, v_var), (c_sy, 64)], inc=c_act)
    op("scalar", lambda e: e.activation(g_rstd[:], g_rstd[:], AF.Exp, scale=-0.5),
       inc=c_act)
    v_rstd2 = c_act.v
    op("vector", lambda e: e.scalar_tensor_tensor(
        g_nmr[:], g_mean[:], -1.0, g_rstd[:], ALU.mult, ALU.mult),
       waits=[(c_act, v_rstd2)], inc=c_dve)
    v_nmr2 = c_dve.v

    v_gate_done_dve = 0
    v_gate_done_pe = 0
    gate_ps = [pA, pB]
    vU = {}
    vHI = {}
    vLO = {}
    vTRH = {}
    vTRL = {}
    vHIEV = {}
    vLOEV = {}
    vGATE = {}

    def ln2_stageA(tt):
        q = tt % 2
        op("vector", lambda e, tt=tt, q=q: e.tensor_scalar(
            lnsq[:, q * D:(q + 1) * D], x_sb[:, tt, :], g_rstd[:, tt:tt + 1],
            g_nmr[:, tt:tt + 1], ALU.mult, ALU.add),
           waits=[(c_act, v_rstd2), (c_dve, v_nmr2),
                  (c_act, vHI.get(tt - 2, 0))], inc=c_dve)
        vU[tt] = c_dve.v
        op("scalar", lambda e, q=q: e.copy(xn_stage[:, q, :], lnsq[:, q * D:(q + 1) * D]),
           waits=[(c_dve, vU[tt]), (c_pe, vTRH.get(tt - 2, 0))], inc=c_act)
        vHI[tt] = c_act.v
        op("vector", lambda e, q=q: e.tensor_tensor(
            xn_stage[:, 2 + q, :], lnsq[:, q * D:(q + 1) * D], xn_stage[:, q, :],
            ALU.subtract),
           waits=[(c_act, vHI[tt]), (c_pe, vTRL.get(tt - 2, 0))], inc=c_dve)
        vLO[tt] = c_dve.v

    def ln2_stageB(tt):
        q = tt % 2
        for dc in range(DC):
            op("tensor", lambda e, dc=dc, q=q: e.transpose(
                pT[:, dc * P:(dc + 1) * P], xn_stage[:, q, dc * P:(dc + 1) * P],
                s_id[:]),
               waits=[(c_act, vHI[tt]), (c_act, vHIEV.get(tt - 1, 0))]
               if dc == 0 else [], inc=(c_pe if dc == DC - 1 else None))
        vTRH[tt] = c_pe.v
        for dc in range(DC):
            op("tensor", lambda e, dc=dc, q=q: e.transpose(
                pU[:, dc * P:(dc + 1) * P], xn_stage[:, 2 + q, dc * P:(dc + 1) * P],
                s_id[:]),
               waits=[(c_dve, vLO[tt]), (c_act, vLOEV.get(tt - 1, 0))]
               if dc == 0 else [], inc=(c_pe if dc == DC - 1 else None))
        vTRL[tt] = c_pe.v
        op("scalar", lambda e, tt=tt: e.copy(
            xn2f[:, :, tt * P:(tt + 1) * P],
            pT[:].rearrange("p (c f) -> p c f", c=DC)),
           waits=[(c_pe, vTRH[tt])], inc=c_act)
        vHIEV[tt] = c_act.v
        op("scalar", lambda e, tt=tt: e.copy(
            lo_fm[:, :, tt * P:(tt + 1) * P],
            pU[:].rearrange("p (c f) -> p c f", c=DC)),
           waits=[(c_pe, vTRL[tt])], inc=c_act)
        vLOEV[tt] = c_act.v

        gp = gate_ps[tt % 2]
        wg = vGATE.get(tt - 2)
        first = True
        for (gsrc, gmat) in ((xn2f, s_gh), (xn2f, s_gl), (lo_fm, s_gh)):
            for dc in range(DC):
                def f(e, gsrc=gsrc, gmat=gmat, dc=dc, tt=tt, gp=gp, first=first):
                    return e.matmul(gp[:, 0:NE], gsrc[:, dc, tt * P:(tt + 1) * P],
                                    gmat[:, dc, :], start=first, stop=False)
                w = []
                if first:
                    w = [(c_act, vHIEV[tt]), (c_act, vLOEV[tt]),
                         (c_dve, v_gate_consts_holder[0]), (c_setup, v_gpsetup)]
                    if wg:
                        w.append((c_dve, wg))
                op("tensor", f, waits=w, inc=None)
                first = False
        for (brow, last) in ((s_gbh, False), (s_gbl, True)):
            def f(e, brow=brow, gp=gp, tt=tt, last=last):
                return e.matmul(gp[:, 0:NE], s_ones[:], brow[:],
                                start=False, stop=last)
            op("tensor", f, inc=(c_pe if last else None))
        v_gmm = c_pe.v
        op("vector", lambda e, gp=gp: e.reduce_max(gc_t[:, 0:1], gp[:, 0:NE],
                                                   axis=AX.X),
           waits=[(c_pe, v_gmm)], inc=c_dve)
        op("vector", lambda e, gp=gp: e.tensor_scalar(
            gt_t[:, 0, :], gp[:, 0:NE], gc_t[:, 0:1], None, ALU.is_equal), inc=c_dve)
        op("vector", lambda e, gp=gp: e.scalar_tensor_tensor(
            gt_t[:, 1, :], gt_t[:, 0, :], -1e30, gp[:, 0:NE], ALU.mult, ALU.add),
           inc=c_dve)
        op("vector", lambda e: e.reduce_max(gc_t[:, 1:2], gt_t[:, 1, :], axis=AX.X),
           inc=c_dve)
        op("vector", lambda e: e.tensor_scalar(
            gt_t[:, 2, :], gt_t[:, 1, :], gc_t[:, 1:2], None, ALU.is_equal), inc=c_dve)
        op("vector", lambda e: e.tensor_tensor(
            gc_t[:, 2:3], gc_t[:, 1:2], gc_t[:, 0:1], ALU.subtract), inc=c_dve)
        v_d = c_dve.v
        op("scalar", lambda e: e.activation(gc_t[:, 3:4], gc_t[:, 2:3], AF.Sigmoid),
           waits=[(c_dve, v_d)], inc=c_act)
        v_sg = c_act.v
        op("vector", lambda e: e.tensor_scalar(
            gc_t[:, 5:6], gc_t[:, 3:4], -1.0, 1.0, ALU.mult, ALU.add),
           waits=[(c_act, v_sg)], inc=c_dve)
        op("vector", lambda e: e.tensor_scalar(
            gt_t[:, 3, :], gt_t[:, 0, :], gc_t[:, 5:6], None, ALU.mult), inc=c_dve)
        op("vector", lambda e, tt=tt: e.scalar_tensor_tensor(
            gateW[:, tt, :], gt_t[:, 2, :], gc_t[:, 3:4], gt_t[:, 3, :],
            ALU.mult, ALU.add), inc=c_dve)
        vGATE[tt] = c_dve.v

    for j in range(NT + 2):
        if j >= 2:
            ln2_stageB(j - 2)
        if j < NT:
            ln2_stageA(j)
    v_gate_done_dve = c_dve.v
    v_gate_done_pe = c_pe.v
    v_ln2_done_act = c_act.v

    op("gpsimd", lambda e: e.dma_start(out=s_b2[:], in_=d_b2f),
       waits=[(c_pe, v_attn_done_pe)], inc=c_setup)
    v_b2ld = c_setup.v

    w1_fill_v = {}
    w2_fill_v = {}
    w1_free_v = {}
    w2_free_v = {}
    mm1_ep = {}
    mm2_ev = {}
    xh_b = [pA, pC]
    gh_b = [pB, pD]
    m2_b = [pE, pF]
    g1 = [0]
    store_after = {}

    def w1_fill(ei, fs):
        slot = (ei * NFS + fs) % W1SLOTS
        waits = []
        fv = w1_free_v.get(slot)
        if fv:
            waits.append((c_pe, fv))
        else:
            waits.append((c_pe, v_attn_done_pe))
        op("gpsimd", lambda e, ei=ei, fs=fs, slot=slot: e.dma_start(
            out=w1rg[:, slot], in_=d_w1r[ei, fs]), waits=waits, inc=c_w1[slot])
        w1_fill_v[(ei, fs)] = (c_w1[slot], c_w1[slot].v)

    def w2_fill(ei, fs):
        gfs = ei * NFS + fs
        slot = gfs % W2SLOTS
        waits = []
        fv = w2_free_v.get(slot)
        if fv:
            waits.append((c_pe, fv))
        else:
            waits.append((c_pe, v_attn_done_pe))
        op("gpsimd", lambda e, ei=ei, fs=fs, slot=slot: e.dma_start(
            out=w2rg[:, slot], in_=d_w2[ei, fs * P:(fs + 1) * P, :]),
           waits=waits, inc=c_w2[slot])
        w2_fill_v[gfs] = (c_w2[slot], c_w2[slot].v)

    fill_q1 = [(ei, fs) for ei in range(NE) for fs in range(NFS)]
    fill_q2 = list(fill_q1)
    fq1 = [0]
    fq2 = [0]

    def pump_fills(n1, n2):
        for _ in range(n1):
            if fq1[0] < len(fill_q1):
                w1_fill(*fill_q1[fq1[0]])
                fq1[0] += 1
        for _ in range(n2):
            if fq2[0] < len(fill_q2):
                w2_fill(*fill_q2[fq2[0]])
                fq2[0] += 1

    pump_fills(W1SLOTS, W2SLOTS)

    for ei in range(NE):
        for jfi in range(NJ):
            for fsl in range(FS_JF):
                fs = jfi * FS_JF + fsl
                slot = (ei * NFS + fs) % W1SLOTS
                for th in range(NTH):
                    par = g1[0] % 2
                    xh, gh = xh_b[par], gh_b[par]
                    waits = [w1_fill_v[(ei, fs)],
                             (c_act, v_ln2_done_act),
                             (c_dve, v_gate_done_dve)]
                    ep = mm1_ep.get(par)
                    if ep:
                        waits.append((c_act, ep[0]))
                        waits.append((c_dve, ep[1]))
                    mv = mm2_ev.get("last_jf")
                    if mv and fsl == 0 and th == 0:
                        waits.append((c_dve, mv))
                    for hx in range(2):
                        ps = xh if hx == 0 else gh
                        for dc in range(DC):
                            def f(e, ps=ps, slot=slot, dc=dc, hx=hx, th=th):
                                return e.matmul(
                                    ps[:], w1rg[:, slot, dc, hx, :],
                                    xn2f[:, dc, th * 512:(th + 1) * 512],
                                    start=(dc == 0), stop=(dc == DC - 1))
                            op("tensor", f,
                               waits=waits if (hx == 0 and dc == 0) else [],
                               inc=(c_pe if (hx == 1 and dc == DC - 1) else None))
                    v_mm = c_pe.v
                    if th == NTH - 1:
                        w1_free_v[slot] = v_mm
                        pump_fills(1, 0)
                    op("scalar", lambda e, gh=gh, par=par, ei=ei, fs=fs: e.activation(
                        a1st[:, par, :], gh[:], AF.Silu,
                        bias=b1c[:, ei, NFS + fs:NFS + fs + 1]),
                       waits=[(c_pe, v_mm), (c_act, v_b1_done)], inc=c_act)
                    v_a1 = c_act.v

                    def fa2(e, xh=xh, par=par, ei=ei, fs=fs, fsl=fsl, th=th):
                        return e.scalar_tensor_tensor(
                            act_sb[:, fsl, th * 512:(th + 1) * 512], xh[:],
                            b1c[:, ei, fs:fs + 1], a1st[:, par, :],
                            ALU.add, ALU.mult)
                    op("vector", fa2, waits=[(c_act, v_a1), (c_pe, v_mm)],
                       inc=c_dve)
                    mm1_ep[par] = (v_a1, c_dve.v)
                    g1[0] += 1
            v_act_ready = c_dve.v
            for tt in range(NT):
                for dh in range(2):
                    gq = (tt * 2 + dh) % 2
                    ps = m2_b[gq]
                    waits = [(c_dve, v_act_ready),
                             (c_dve, v_gate_done_dve),
                             (c_setup, v_b2ld)]
                    for _fsl in range(FS_JF):
                        waits.append(w2_fill_v[ei * NFS + jfi * FS_JF + _fsl])
                    mv = mm2_ev.get(gq)
                    if mv:
                        waits.append((c_dve, mv))
                    first = True
                    if jfi == 0:
                        def fb(e, ps=ps, ei=ei, dh=dh):
                            return e.matmul(
                                ps[:], s_ones[:],
                                s_b2[:, ei * D + dh * 512: ei * D + (dh + 1) * 512],
                                start=True, stop=False)
                        op("tensor", fb, waits=waits, inc=None)
                        first = False
                        waits = []
                    for fsl in range(FS_JF):
                        fs = jfi * FS_JF + fsl
                        slot = (ei * NFS + fs) % W2SLOTS

                        def f(e, ps=ps, fsl=fsl, slot=slot, tt=tt, dh=dh,
                              first=first, fin=(fsl == FS_JF - 1)):
                            return e.matmul(
                                ps[:], act_sb[:, fsl, tt * P:(tt + 1) * P],
                                w2rg[:, slot, dh * 512:(dh + 1) * 512],
                                start=first, stop=fin)
                        op("tensor", f, waits=waits if first else [],
                           inc=(c_pe if fsl == FS_JF - 1 else None))
                        first = False
                        waits = []
                    v_mm = c_pe.v

                    def fc(e, ps=ps, tt=tt, dh=dh, ei=ei):
                        xs = x_sb[:, tt, dh * 512:(dh + 1) * 512]
                        return e.scalar_tensor_tensor(
                            xs, ps[:], gateW[:, tt, ei:ei + 1], xs,
                            ALU.mult, ALU.add)
                    op("vector", fc, waits=[(c_pe, v_mm)], inc=c_dve)
                    mm2_ev[gq] = c_dve.v
                    if ei == NE - 1 and jfi == NJ - 1:
                        store_after[(tt, dh)] = c_dve.v
            v_jf_pe = c_pe.v
            for fsl in range(FS_JF):
                w2_free_v[(ei * NFS + jfi * FS_JF + fsl) % W2SLOTS] = v_jf_pe
            pump_fills(0, FS_JF)
            mm2_ev["last_jf"] = c_dve.v

    out_r = d_out
    for tt in range(NT):
        op("sync", lambda e, tt=tt: e.dma_start(
            out=out_r[:, tt, :], in_=x_sb[:, tt, :]),
           waits=[(c_dve, store_after[(tt, 1)])], inc=c_sy)
    v_all_stored = c_sy.v

    tapmap = {
        "xn2f": (xn2f, BF16, [P, DC, TCORE]),
        "act_sb": (act_sb, BF16, [P, FS_JF, TCORE]),
        "gateW": (gateW, F32, [P, NT, NE]),
        "g_rstd": (g_rstd, F32, [P, NT]),
        "x_sb": (x_sb, F32, [P, NT, D]),
    }
    for name in taps:
        t, dt_, shp = tapmap[name]
        dtap = nc.dram_tensor("tap_" + name, shp, F32, kind="ExternalOutput").ap()
        if dt_ == F32:
            op("sync", lambda e, t=t, dtap=dtap: e.dma_start(out=dtap, in_=t[:]),
               waits=[(c_dve, c_dve.v), (c_act, c_act.v), (c_pe, c_pe.v)], inc=c_sy)
        else:
            op("gpsimd", lambda e, t=t, dtap=dtap: e.dma_start(out=dtap, in_=t[:]),
               waits=[(c_dve, c_dve.v), (c_act, c_act.v), (c_pe, c_pe.v)], inc=c_setup)
    v_tap_wq = c_setup.v

    op("sync", lambda e: None, waits=[(c_sy, c_sy.v), (c_setup, v_tap_wq)])

    with nc.Block() as block:
        own_ctr = {"scalar": c_act, "vector": c_dve}

        def runner(name):
            def body(eng):
                marks = {}
                own = own_ctr.get(name)
                n_own = 0
                for fn, waits, inc in streams[name]:
                    if own is not None and n_own > 0:
                        eng.wait_ge(own.sem, n_own)
                    for (ctr, v) in waits:
                        if v > marks.get(ctr.key, 0):
                            eng.wait_ge(ctr.sem, v)
                            marks[ctr.key] = v
                    ins = fn(eng)
                    if inc is not None:
                        assert ins is not None
                        ins.then_inc(inc.sem, inc.step)
                        if inc is own:
                            n_own += 1
            return body
        block.tensor(runner("tensor"))
        block.scalar(runner("scalar"))
        block.vector(runner("vector"))
        block.gpsimd(runner("gpsimd"))
        block.sync(runner("sync"))
    es.close()
    return nc


def prep_inputs(L0_attn_x, L0_attn_Wq, L0_attn_Wk, L0_attn_Wv, L0_attn_Wo,
                L0_ffn_gate_w, L0_ffn_gate_b, L0_ffn_w1, L0_ffn_b1,
                L0_ffn_w2, L0_ffn_b2):
    f32 = np.float32
    x = np.asarray(L0_attn_x, f32)
    x2 = np.ascontiguousarray(x.reshape(2 * 8192, D))

    def wt(w):
        return np.ascontiguousarray(
            np.asarray(w, f32).T.reshape(DC, P, D).transpose(1, 0, 2))
    wqt, wkt, wvt, wot = wt(L0_attn_Wq), wt(L0_attn_Wk), wt(L0_attn_Wv), wt(L0_attn_Wo)
    gw = np.ascontiguousarray(
        np.asarray(L0_ffn_gate_w, f32).reshape(DC, P, NE).transpose(1, 0, 2))
    gb = np.asarray(L0_ffn_gate_b, f32).reshape(1, NE)
    w1r = np.ascontiguousarray(
        np.asarray(L0_ffn_w1, f32).reshape(NE, DC, P, 2, NFS, P)
        .transpose(0, 4, 2, 1, 3, 5))
    w2 = np.ascontiguousarray(np.asarray(L0_ffn_w2, f32))
    b1r = np.ascontiguousarray(
        np.asarray(L0_ffn_b1, f32).reshape(NE, 2, NFS, P).transpose(2, 0, 1, 3))
    b2f = np.ascontiguousarray(np.asarray(L0_ffn_b2, f32).reshape(1, NE * D))
    ident = np.eye(P, dtype=f32)
    onesr = np.ones((1, P), f32)
    epsc = np.tile(np.array([[EPS, -12.0]], f32), (P, 1))
    in_maps = []
    for c in range(8):
        seq = c // 4
        in_maps.append(dict(
            x=np.ascontiguousarray(
                x2[c * TCORE:(c + 1) * TCORE].reshape(NT, P, D).transpose(1, 0, 2)),
            xg=np.ascontiguousarray(x[seq, 0:GT, :]),
            wqt=wqt, wkt=wkt, wvt=wvt, wot=wot, gw=gw, gb=gb,
            w1r=w1r, w2=w2, b1r=b1r, b2f=b2f,
            ident=ident, onesr=onesr, epsc=epsc))
    return in_maps


def kernel(**inputs):
    in_maps = prep_inputs(**inputs)
    nc = build_nc()
    res = run_bass_kernel_spmd(nc, in_maps, core_ids=list(range(8)))
    out = np.concatenate(
        [r["out"].reshape(P, NT, D).transpose(1, 0, 2).reshape(TCORE, D)
         for r in res.results], axis=0)
    return out.reshape(2, 8192, D)


# version 21
# speedup vs baseline: 1.0318x; 1.0030x over previous
import contextlib
import numpy as np
import ml_dtypes
import concourse.bass as bass
import concourse.mybir as mybir
from concourse.bass_utils import run_bass_kernel_spmd

F32 = mybir.dt.float32
BF16 = mybir.dt.bfloat16
AF = mybir.ActivationFunctionType
ALU = mybir.AluOpType
AX = mybir.AxisListType

P = 128
D = 1024
DC = 8
TCORE = 2048
NT = 16
H = 16
HD = 64
NWIN = 16
GT = 64
NE = 8
FH = 4096
NFS = 32
NJ = 4
FS_JF = NFS // NJ
NTH = 4
EPS = 1e-5
W1SLOTS = 4
W2SLOTS = 10


class Ctr:
    __slots__ = ("sem", "v", "step", "key")
    _next = 0

    def __init__(self, sem, step=1):
        self.sem = sem
        self.v = 0
        self.step = step
        self.key = Ctr._next
        Ctr._next += 1


def build_nc(taps=()):
    nc = bass.Bass("TRN2", target_bir_lowering=False)
    es = contextlib.ExitStack()

    d_x = nc.dram_tensor("x", [P, NT, D], F32, kind="ExternalInput").ap()
    d_xg = nc.dram_tensor("xg", [GT, D], F32, kind="ExternalInput").ap()
    d_wqt = nc.dram_tensor("wqt", [P, DC, D], F32, kind="ExternalInput").ap()
    d_wkt = nc.dram_tensor("wkt", [P, DC, D], F32, kind="ExternalInput").ap()
    d_wvt = nc.dram_tensor("wvt", [P, DC, D], F32, kind="ExternalInput").ap()
    d_wot = nc.dram_tensor("wot", [P, DC, D], F32, kind="ExternalInput").ap()
    d_gw = nc.dram_tensor("gw", [P, DC, NE], F32, kind="ExternalInput").ap()
    d_gb = nc.dram_tensor("gb", [1, NE], F32, kind="ExternalInput").ap()
    d_w1r = nc.dram_tensor("w1r", [NE, NFS, P, DC, 2, P], F32, kind="ExternalInput").ap()
    d_w2 = nc.dram_tensor("w2", [NE, FH, D], F32, kind="ExternalInput").ap()
    d_b1r = nc.dram_tensor("b1r", [NFS, NE, 2, P], F32, kind="ExternalInput").ap()
    d_b2f = nc.dram_tensor("b2f", [1, NE * D], F32, kind="ExternalInput").ap()
    d_id = nc.dram_tensor("ident", [P, P], F32, kind="ExternalInput").ap()
    d_ones = nc.dram_tensor("onesr", [1, P], F32, kind="ExternalInput").ap()
    d_eps = nc.dram_tensor("epsc", [P, 2], F32, kind="ExternalInput").ap()
    d_out = nc.dram_tensor("out", [P, NT, D], F32, kind="ExternalOutput").ap()

    x_sb = nc.alloc_sbuf_tensor("x_sb", [P, NT, D], F32)
    s_id = nc.alloc_sbuf_tensor("s_id", [P, P], BF16)
    s_ones = nc.alloc_sbuf_tensor("s_ones", [1, P], BF16)
    s_eps = nc.alloc_sbuf_tensor("s_eps", [P, 2], F32)
    s_gwf = nc.alloc_sbuf_tensor("s_gwf", [P, DC, NE], F32)
    s_gh = nc.alloc_sbuf_tensor("s_gh", [P, DC, NE], BF16)
    s_gl = nc.alloc_sbuf_tensor("s_gl", [P, DC, NE], BF16)
    s_gbf = nc.alloc_sbuf_tensor("s_gbf", [1, NE], F32)
    s_gbh = nc.alloc_sbuf_tensor("s_gbh", [1, NE], BF16)
    s_gbl = nc.alloc_sbuf_tensor("s_gbl", [1, NE], BF16)
    b1c = nc.alloc_sbuf_tensor("b1c", [P, NE, 2 * NFS], F32)
    gateW = nc.alloc_sbuf_tensor("gateW", [P, NT, NE], F32)
    g_s1 = nc.alloc_sbuf_tensor("g_s1", [P, NT], F32)
    g_s2 = nc.alloc_sbuf_tensor("g_s2", [P, NT], F32)
    g_mean = nc.alloc_sbuf_tensor("g_mean", [P, NT], F32)
    g_var = nc.alloc_sbuf_tensor("g_var", [P, NT], F32)
    g_rstd = nc.alloc_sbuf_tensor("g_rstd", [P, NT], F32)
    g_nmr = nc.alloc_sbuf_tensor("g_nmr", [P, NT], F32)
    lnsq = nc.alloc_sbuf_tensor("lnsq", [P, 2 * D], F32)
    xn_stage = nc.alloc_sbuf_tensor("xn_stage", [P, 4, D], BF16)
    gt_t = nc.alloc_sbuf_tensor("gt_t", [P, 4, NE], F32)
    gc_t = nc.alloc_sbuf_tensor("gc_t", [P, 8], F32)
    es_b1 = contextlib.ExitStack()
    b1tm = es_b1.enter_context(nc.sbuf_tensor("b1tm", [NFS, NE * 2, P], BF16))

    pA = nc.alloc_psum_tensor("pA", [P, 512], F32)
    pB = nc.alloc_psum_tensor("pB", [P, 512], F32)
    pC = nc.alloc_psum_tensor("pC", [P, 512], F32)
    pD = nc.alloc_psum_tensor("pD", [P, 512], F32)
    pE = nc.alloc_psum_tensor("pE", [P, 512], F32)
    pF = nc.alloc_psum_tensor("pF", [P, 512], F32)
    pT = nc.alloc_psum_tensor("pT", [P, D], BF16)
    pU = nc.alloc_psum_tensor("pU", [P, D], BF16)

    sem_pe = es.enter_context(nc.semaphore("sem_pe"))
    sem_act = es.enter_context(nc.semaphore("sem_act"))
    sem_dve = es.enter_context(nc.semaphore("sem_dve"))
    sem_setup = es.enter_context(nc.semaphore("sem_setup"))
    sem_sy = es.enter_context(nc.semaphore("sem_sy"))
    c_pe = Ctr(sem_pe)
    c_act = Ctr(sem_act)
    c_dve = Ctr(sem_dve)
    c_setup = Ctr(sem_setup, 16)
    c_sy = Ctr(sem_sy, 16)
    c_wsl = [Ctr(es.enter_context(nc.semaphore(f"sem_wsl{i}")), 16) for i in range(2)]
    c_w1 = [Ctr(es.enter_context(nc.semaphore(f"sem_w1_{i}")), 16) for i in range(W1SLOTS)]
    c_w2 = [Ctr(es.enter_context(nc.semaphore(f"sem_w2_{i}")), 16) for i in range(W2SLOTS)]
    c_x = [Ctr(es.enter_context(nc.semaphore(f"sem_x{i}")), 16) for i in range(4)]

    streams = {k: [] for k in ("tensor", "scalar", "vector", "gpsimd", "sync")}

    def op(eng, fn, waits=(), inc=None):
        streams[eng].append((fn, tuple((c, v) for (c, v) in waits), inc))
        if inc is not None:
            inc.v += inc.step
        return inc.v if inc is not None else None

    v_sync_all = [0]
    xg_sb_holder = [None]

    op("sync", lambda e: e.dma_start(out=s_eps[:], in_=d_eps), inc=c_sy)
    op("sync", lambda e: e.dma_start(out=s_gwf[:], in_=d_gw), inc=c_sy)
    op("sync", lambda e: e.dma_start(out=s_gbf[:], in_=d_gb), inc=c_sy)
    op("sync", lambda e: e.dma_start(out=xg_sb_holder[0][:], in_=d_xg), inc=c_sy)
    v_sync_all[0] = c_sy.v
    xr = d_x
    for ch in range(4):
        op("sync" if ch % 2 == 0 else "scalar", lambda e, ch=ch: e.dma_start(
            out=x_sb[:, ch * 4:(ch + 1) * 4, :], in_=xr[:, ch * 4:(ch + 1) * 4, :]),
           inc=c_x[ch])

    op("gpsimd", lambda e: e.dma_start(out=s_id[:], in_=d_id), inc=c_setup)
    op("gpsimd", lambda e: e.dma_start(out=s_ones[:], in_=d_ones), inc=c_setup)
    op("gpsimd", lambda e: e.dma_start(out=s_gh[:], in_=d_gw), inc=c_setup)
    op("gpsimd", lambda e: e.dma_start(out=s_gbh[:], in_=d_gb), inc=c_setup)
    op("gpsimd", lambda e: e.dma_start(out=b1tm[:], in_=d_b1r.rearrange(
        "c e h f -> c (e h) f")), inc=c_setup)
    v_gpsetup = c_setup.v
    v_id = v_gpsetup

    v_gate_consts_holder = [0]

    b1_ev = []
    for grp in range(2):
        pdst = pT if grp == 0 else pU
        for k in range(8):
            idx = grp * 8 + k
            e_, h_ = idx // 2, idx % 2

            def f(eng, pdst=pdst, k=k, idx=idx):
                return eng.transpose(pdst[:, k * NFS:(k + 1) * NFS],
                                     b1tm[:, idx, :], s_id[0:NFS, 0:NFS])
            w = [(c_setup, v_gpsetup)] if (grp == 0 and k == 0) else []
            op("tensor", f, waits=w, inc=c_pe)
        v_tr = c_pe.v

        def fev(eng, pdst=pdst, grp=grp):
            return eng.copy(
                b1c[:].rearrange("p e (h c) -> p (e h) c", h=2)[:, grp * 8:(grp + 1) * 8, :],
                pdst[:, 0:8 * NFS].rearrange("p (k c) -> p k c", c=NFS))
        op("scalar", fev, waits=[(c_pe, v_tr)], inc=c_act)
    v_b1_done = c_act.v
    v_b1_tr_pe = c_pe.v
    es_b1.close()

    def ln_stats(src_tiles, n_tiles, parts=P, chunked=False):
        v_r = {}
        v_n = {}
        step = 4 if chunked else n_tiles
        for c0 in range(0, n_tiles, step):
            for tt in range(c0, min(c0 + step, n_tiles)):
                src = src_tiles(tt)
                op("scalar", lambda e, src=src, tt=tt: e.activation(
                    lnsq[0:parts, 0:D], src, AF.Square,
                    accum_out=g_s2[0:parts, tt:tt + 1]),
                   waits=[(c_x[tt // 4], 16)], inc=c_act)
                op("vector", lambda e, src=src, tt=tt: e.reduce_sum(
                    g_s1[0:parts, tt:tt + 1], src, axis=AX.X),
                   waits=[(c_x[tt // 4], 16)], inc=c_dve)
            v_stats = (c_act.v, c_dve.v)
            c1 = min(c0 + step, n_tiles)
            op("vector", lambda e, c0=c0, c1=c1: e.tensor_scalar_mul(
                g_mean[0:parts, c0:c1], g_s1[0:parts, c0:c1], 1.0 / D),
               waits=[(c_act, v_stats[0])], inc=c_dve)
            op("vector", lambda e, c0=c0, c1=c1: e.tensor_mul(
                g_var[0:parts, c0:c1], g_mean[0:parts, c0:c1],
                g_mean[0:parts, c0:c1]), inc=c_dve)
            op("vector", lambda e, c0=c0, c1=c1: e.scalar_tensor_tensor(
                g_var[0:parts, c0:c1], g_s2[0:parts, c0:c1], 1.0 / D,
                g_var[0:parts, c0:c1], ALU.mult, ALU.subtract), inc=c_dve)
            v_var = c_dve.v
            op("scalar", lambda e, c0=c0, c1=c1: e.activation(
                g_rstd[0:parts, c0:c1], g_var[0:parts, c0:c1], AF.Ln,
                bias=s_eps[0:parts, 0:1]), waits=[(c_dve, v_var), (c_sy, 64)],
               inc=c_act)
            op("scalar", lambda e, c0=c0, c1=c1: e.activation(
                g_rstd[0:parts, c0:c1], g_rstd[0:parts, c0:c1], AF.Exp,
                scale=-0.5), inc=c_act)
            v_rstd = c_act.v
            op("vector", lambda e, c0=c0, c1=c1: e.scalar_tensor_tensor(
                g_nmr[0:parts, c0:c1], g_mean[0:parts, c0:c1], -1.0,
                g_rstd[0:parts, c0:c1], ALU.mult, ALU.mult),
               waits=[(c_act, v_rstd)], inc=c_dve)
            for tt in range(c0, c1):
                v_r[tt] = v_rstd
                v_n[tt] = c_dve.v
        return v_r, v_n

    tr_parity = [0]
    stage_last_tr = {}

    def norm_transpose(tt, dst_view, v_extra_waits=(), src=None, parts=P,
                       n_dc=DC, stage_slot=None, rstd_col=None, nmr_col=None,
                       out_stage=None):
        par = tr_parity[0] % 2 if stage_slot is None else stage_slot
        tr_parity[0] += 1
        stg = xn_stage[0:parts, par, :] if out_stage is None else out_stage
        src = x_sb[:, tt, :] if src is None else src
        rc = g_rstd[0:parts, tt:tt + 1] if rstd_col is None else rstd_col
        nc_ = g_nmr[0:parts, tt:tt + 1] if nmr_col is None else nmr_col
        sg = stage_last_tr.get(par, 0)
        pg = stage_last_tr.get(("ev", par), 0)
        op("vector", lambda e, src=src, stg=stg, rc=rc, nc_=nc_: e.tensor_scalar(
            stg, src, rc, nc_, ALU.mult, ALU.add),
           waits=list(v_extra_waits) + [(c_pe, sg)], inc=c_dve)
        v_norm = c_dve.v
        pdst = pT if par == 0 else pU
        for dc in range(n_dc):
            op("tensor", lambda e, dc=dc, stg=stg, pdst=pdst, parts=parts: e.transpose(
                pdst[:, dc * parts:(dc + 1) * parts],
                stg.rearrange("p (c f) -> p c f", f=P)[:, dc, :],
                s_id[0:parts, 0:parts]),
               waits=[(c_dve, v_norm), (c_setup, v_id), (c_act, pg)], inc=c_pe)
        v_tr = c_pe.v
        stage_last_tr[par] = v_tr
        op("scalar", lambda e, pdst=pdst, dst_view=dst_view, parts=parts, n_dc=n_dc:
           e.copy(dst_view,
                         pdst[:, 0:n_dc * parts].rearrange("p (c f) -> p c f", c=n_dc)),
           waits=[(c_pe, v_tr)], inc=c_act)
        stage_last_tr[("ev", par)] = c_act.v
        return c_act.v

    es_attn = contextlib.ExitStack()
    xn1h = es_attn.enter_context(nc.sbuf_tensor("xn1h", [P, DC, 1024], BF16))
    qf = es_attn.enter_context(nc.sbuf_tensor("qf", [P, DC, 1024], BF16))
    kf = es_attn.enter_context(nc.sbuf_tensor("kf", [P, DC, 1024], BF16))
    vt = es_attn.enter_context(nc.sbuf_tensor("vt", [P, DC, 1024], BF16))
    aof = es_attn.enter_context(nc.sbuf_tensor("aof", [P, DC, 1024], BF16))
    wsl = es_attn.enter_context(nc.sbuf_tensor("wsl", [P, 2, DC, 1024], BF16))
    xg_sb = es_attn.enter_context(nc.sbuf_tensor("xg_sb", [GT, D], F32))
    xg_sb_holder[0] = xg_sb
    xgn_fm = es_attn.enter_context(nc.sbuf_tensor("xgn_fm", [P, DC, GT], BF16))
    kgf = es_attn.enter_context(nc.sbuf_tensor("kgf", [P, DC, GT], BF16))
    vgt = es_attn.enter_context(nc.sbuf_tensor("vgt", [GT, D], BF16))
    gx = es_attn.enter_context(nc.sbuf_tensor("gx", [GT, 8], F32))
    P_sb = es_attn.enter_context(nc.sbuf_tensor("P_sb", [P, 4, 192], BF16))
    sPT = es_attn.enter_context(nc.sbuf_tensor("sPT", [P, 4, 256], BF16))
    rr = es_attn.enter_context(nc.sbuf_tensor("rr", [P, 4, 4], F32))



    v_ln1_rstd, v_ln1_nmr = ln_stats(lambda tt: x_sb[:, tt, :], NT, chunked=True)

    op("scalar", lambda e: e.activation(
        lnsq[0:GT, 0:D], xg_sb[:], AF.Square, accum_out=gx[:, 1:2]),
       waits=[(c_sy, v_sync_all[0])], inc=c_act)
    op("vector", lambda e: e.reduce_sum(gx[:, 0:1], xg_sb[:], axis=AX.X),
       waits=[(c_sy, v_sync_all[0])], inc=c_dve)
    v_gxs = (c_act.v, c_dve.v)
    op("vector", lambda e: e.tensor_scalar_mul(gx[:, 2:3], gx[:, 0:1], 1.0 / D),
       waits=[(c_act, v_gxs[0])], inc=c_dve)
    op("vector", lambda e: e.tensor_mul(gx[:, 3:4], gx[:, 2:3], gx[:, 2:3]), inc=c_dve)
    op("vector", lambda e: e.scalar_tensor_tensor(
        gx[:, 3:4], gx[:, 1:2], 1.0 / D, gx[:, 3:4], ALU.mult, ALU.subtract), inc=c_dve)
    v_gv = c_dve.v
    op("scalar", lambda e: e.activation(gx[:, 4:5], gx[:, 3:4], AF.Ln,
                                        bias=s_eps[0:GT, 0:1]),
       waits=[(c_dve, v_gv), (c_sy, 64)], inc=c_act)
    op("scalar", lambda e: e.activation(gx[:, 4:5], gx[:, 4:5], AF.Exp, scale=-0.5),
       inc=c_act)
    v_grstd = c_act.v
    op("vector", lambda e: e.scalar_tensor_tensor(
        gx[:, 5:6], gx[:, 2:3], -1.0, gx[:, 4:5], ALU.mult, ALU.mult),
       waits=[(c_act, v_grstd)], inc=c_dve)
    v_gnmr = c_dve.v
    v_xgn = norm_transpose(
        0, xgn_fm[:], v_extra_waits=[(c_act, v_grstd), (c_dve, v_gnmr)],
        src=xg_sb[:], parts=GT, rstd_col=gx[0:GT, 4:5], nmr_col=gx[0:GT, 5:6])

    proj_ps = [pE, pF]
    S_banks = [pA, pB, pE]
    PV_banks = [pC, pD]
    v_evac_prev = {}
    v_aof_done = [0]
    v_attn_tail_pe = [0]

    def wproj_load(slot, dram, vfree_pe):
        op("gpsimd", lambda e: e.dma_start(out=wsl[:, slot], in_=dram),
           waits=([(c_pe, vfree_pe)] if vfree_pe else []), inc=c_wsl[slot])
        return (c_wsl[slot], c_wsl[slot].v)

    def projection(dst, wslot, vw, half, scale, v_xn, kind, extra=()):
        par = [0]
        for ic in range(DC):
            for tc in range(2):
                ps = proj_ps[par[0] % 2]
                waits = [vw, (c_act, v_xn)] + list(extra)
                pv = v_evac_prev.get(("proj", par[0] % 2))
                if pv:
                    waits.append((c_act, pv))
                for dc in range(DC):
                    def f(e, ps=ps, dc=dc, ic=ic, tc=tc, wslot=wslot, half=half):
                        if kind == "fm":
                            return e.matmul(
                                ps[:], wsl[:, wslot, dc, ic * P:(ic + 1) * P],
                                xn1h[:, dc, tc * 512:(tc + 1) * 512],
                                start=(dc == 0), stop=(dc == DC - 1))
                        else:
                            return e.matmul(
                                ps[:], xn1h[:, dc, ic * P:(ic + 1) * P],
                                wsl[:, wslot, dc, tc * 512:(tc + 1) * 512],
                                start=(dc == 0), stop=(dc == DC - 1))
                    op("tensor", f, waits=waits if dc == 0 else [], inc=(c_pe if dc == DC - 1 else None))
                v_mm = c_pe.v

                def fe(e, ps=ps, ic=ic, tc=tc, scale=scale):
                    return e.activation(dst[:, ic, tc * 512:(tc + 1) * 512], ps[:],
                                        AF.Copy, scale=scale)
                op("scalar", fe, waits=[(c_pe, v_mm)], inc=c_act)
                v_evac_prev[("proj", par[0] % 2)] = c_act.v
                par[0] += 1
        return c_act.v, c_pe.v

    for half in range(2):
        hbase = half * 8
        v_xn1 = 0
        for t8 in range(8):
            tt = hbase + t8
            waits = [(c_act, v_ln1_rstd[tt]), (c_dve, v_ln1_nmr[tt])]
            if half == 1:
                waits.append((c_pe, v_attn_tail_pe[0]))
            v_xn1 = norm_transpose(tt, xn1h[:, :, t8 * P:(t8 + 1) * P],
                                   v_extra_waits=waits)
        vfree0 = v_attn_tail_pe[0] if half == 1 else v_b1_tr_pe
        vwq = wproj_load(0, d_wqt, vfree0)
        extra = []
        vv = v_evac_prev.get("oproj1")
        if vv:
            extra.append((c_dve, vv))
        v_q_act, v_q_pe = projection(qf, 0, vwq, half, 0.125, v_xn1, "fm",
                                     extra=extra)
        vwk = wproj_load(1, d_wkt, vfree0)
        v_k_act, v_k_pe = projection(kf, 1, vwk, half, 1.0, v_xn1, "fm")
        if half == 0:
            for ic in range(DC):
                ps = proj_ps[ic % 2]
                waits = [vwk, (c_act, v_xgn)]
                pv = v_evac_prev.get(("proj", ic % 2))
                if pv:
                    waits.append((c_act, pv))
                for dc in range(DC):
                    def f(e, ps=ps, dc=dc, ic=ic):
                        return e.matmul(ps[:, 0:GT],
                                        wsl[:, 1, dc, ic * P:(ic + 1) * P],
                                        xgn_fm[:, dc, :],
                                        start=(dc == 0), stop=(dc == DC - 1))
                    op("tensor", f, waits=waits if dc == 0 else [],
                       inc=(c_pe if dc == DC - 1 else None))
                v_mm = c_pe.v
                op("scalar", lambda e, ps=ps, ic=ic: e.copy(
                    kgf[:, ic, :], ps[:, 0:GT]), waits=[(c_pe, v_mm)], inc=c_act)
                v_evac_prev[("proj", ic % 2)] = c_act.v
            v_k_pe = c_pe.v
        vwv = wproj_load(0, d_wvt, v_q_pe)
        v_v_act, v_v_pe = projection(vt, 0, vwv, half, 1.0, v_xn1, "tm")
        if half == 0:
            for tc in range(2):
                ps = proj_ps[tc % 2]
                waits = [vwv, (c_act, v_xgn)]
                pv = v_evac_prev.get(("proj", tc % 2))
                if pv:
                    waits.append((c_act, pv))
                for dc in range(DC):
                    def f(e, ps=ps, dc=dc, tc=tc):
                        return e.matmul(ps[0:GT, :], xgn_fm[:, dc, :],
                                        wsl[:, 0, dc, tc * 512:(tc + 1) * 512],
                                        start=(dc == 0), stop=(dc == DC - 1))
                    op("tensor", f, waits=waits if dc == 0 else [],
                       inc=(c_pe if dc == DC - 1 else None))
                v_mm = c_pe.v
                op("scalar", lambda e, ps=ps, tc=tc: e.copy(
                    vgt[:, tc * 512:(tc + 1) * 512], ps[0:GT, :]),
                   waits=[(c_pe, v_mm)], inc=c_act)
                v_evac_prev[("proj", tc % 2)] = c_act.v
        vwo = wproj_load(1, d_wot, v_k_pe)

        LAG = 2
        NPAIR = 8 * H
        v_seed_act = c_act.v
        vS = {}
        vEXP = {}
        vPN = {}
        vPT = {}
        vPTEV = {}
        vPV = {}
        vAO = {}

        def pemit_S(pi):
            w8, head = pi // H, pi % H
            sb = S_banks[pi % 3]
            pb = (head % 2) * HD
            ic = head // 2
            waits = [(c_act, v_q_act), (c_act, v_k_act),
                     (c_act, vEXP.get(pi - 3, v_seed_act))]

            def fs1(e, sb=sb, pb=pb, ic=ic, w8=w8):
                return e.matmul(sb[:, 0:GT], qf[pb:pb + HD, ic, w8 * P:(w8 + 1) * P],
                                kgf[pb:pb + HD, ic, :], start=True, stop=True)
            op("tensor", fs1, waits=waits, inc=None)

            def fs2(e, sb=sb, pb=pb, ic=ic, w8=w8):
                return e.matmul(sb[:, GT:192], qf[pb:pb + HD, ic, w8 * P:(w8 + 1) * P],
                                kf[pb:pb + HD, ic, w8 * P:(w8 + 1) * P],
                                start=True, stop=True)
            op("tensor", fs2, inc=c_pe)
            vS[pi] = c_pe.v

        def pemit_front(pi):
            sb = S_banks[pi % 3]
            op("scalar", lambda e, sb=sb, pi=pi: e.activation(
                P_sb[:, pi % 4, :], sb[:, 0:192], AF.Exp,
                bias=s_eps[:, 1:2], accum_out=rr[:, pi % 4, 1:2]),
               waits=[(c_pe, vS[pi]), (c_pe, vPT.get(pi - 4, 0)), (c_sy, 64)],
               inc=c_act)
            vEXP[pi] = c_act.v
            op("vector", lambda e, pi=pi: e.reciprocal(
                rr[:, pi % 4, 2:3], rr[:, pi % 4, 1:2]),
               waits=[(c_act, vEXP[pi])], inc=c_dve)
            op("vector", lambda e, pi=pi: e.tensor_scalar_mul(
                P_sb[:, pi % 4, :], P_sb[:, pi % 4, :], rr[:, pi % 4, 2:3]),
               inc=c_dve)
            vPN[pi] = c_dve.v

        def pemit_tr(pi):
            ptb = pT if pi % 2 == 0 else pU

            def ft1(e, ptb=ptb, pi=pi):
                return e.transpose(ptb[0:GT, 0:P], P_sb[:, pi % 4, 0:GT], s_id[:])
            op("tensor", ft1,
               waits=[(c_dve, vPN[pi]), (c_dve, vPTEV.get(pi - 2, 0))], inc=None)

            def ft2(e, ptb=ptb, pi=pi):
                return e.transpose(ptb[:, P:256], P_sb[:, pi % 4, GT:192], s_id[:])
            op("tensor", ft2, inc=c_pe)
            vPT[pi] = c_pe.v
            op("vector", lambda e, ptb=ptb, pi=pi: e.tensor_copy(
                sPT[:, pi % 4, :], ptb[:, 0:256]),
               waits=[(c_pe, vPT[pi]), (c_pe, vPV.get(pi - 4, 0))], inc=c_dve)
            vPTEV[pi] = c_dve.v

        def pemit_pv(pi):
            w8, head = pi // H, pi % H
            pvb = PV_banks[pi % 2]
            pb = (head % 2) * HD
            ic = head // 2

            def fpv1(e, pvb=pvb, head=head, pi=pi):
                return e.matmul(pvb[0:HD, 0:P], vgt[0:GT, head * HD:(head + 1) * HD],
                                sPT[0:GT, pi % 4, 0:P], start=True, stop=False)
            op("tensor", fpv1,
               waits=[(c_dve, vPTEV[pi]), (c_act, v_v_act),
                      (c_act, vAO.get(pi - 2, 0))], inc=None)

            def fpv2(e, pvb=pvb, head=head, w8=w8, pi=pi):
                return e.matmul(pvb[0:HD, 0:P], vt[:, w8, head * HD:(head + 1) * HD],
                                sPT[:, pi % 4, P:256], start=False, stop=True)
            op("tensor", fpv2, inc=c_pe)
            vPV[pi] = c_pe.v
            op("scalar", lambda e, pvb=pvb, pb=pb, ic=ic, w8=w8: e.copy(
                aof[pb:pb + HD, ic, w8 * P:(w8 + 1) * P], pvb[0:HD, 0:P]),
               waits=[(c_pe, vPV[pi])], inc=c_act)
            vAO[pi] = c_act.v

        def pemit_oproj(t8):
            v_ao_w = vAO[t8 * H + H - 1]
            for dh in range(2):
                gidx = t8 * 2 + dh
                ps = pF
                waits = [vwo, (c_act, v_ao_w)]
                pv = v_evac_prev.get("oproj1")
                if pv:
                    waits.append((c_dve, pv))
                pv2 = v_evac_prev.get(("proj", 1))
                if pv2:
                    waits.append((c_act, pv2))
                for ic in range(DC):
                    def f(e, ps=ps, ic=ic, t8=t8, dh=dh):
                        return e.matmul(ps[:], aof[:, ic, t8 * P:(t8 + 1) * P],
                                        wsl[:, 1, ic, dh * 512:(dh + 1) * 512],
                                        start=(ic == 0), stop=(ic == DC - 1))
                    op("tensor", f, waits=waits if ic == 0 else [],
                       inc=(c_pe if ic == DC - 1 else None))
                v_mm = c_pe.v

                def fr(e, ps=ps, t8=t8, dh=dh, hbase=hbase):
                    xs = x_sb[:, hbase + t8, dh * 512:(dh + 1) * 512]
                    return e.tensor_tensor(xs, ps[:], xs, ALU.add)
                op("vector", fr, waits=[(c_pe, v_mm)], inc=c_dve)
                v_evac_prev["oproj1"] = c_dve.v
                if dh == 1:
                    tt = hbase + t8
                    v_add = c_dve.v
                    op("scalar", lambda e, tt=tt: e.activation(
                        lnsq[:, 0:D], x_sb[:, tt, :], AF.Square,
                        accum_out=g_s2[:, tt:tt + 1]),
                       waits=[(c_dve, v_add)], inc=c_act)
                    op("vector", lambda e, tt=tt: e.reduce_sum(
                        g_s1[:, tt:tt + 1], x_sb[:, tt, :], axis=AX.X), inc=c_dve)

        for j in range(NPAIR + 24):
            t8o = (j - 22) // 16
            if j >= 22 and (j - 22) % 16 == 0 and 0 <= t8o < 8:
                pemit_oproj(t8o)
            if 0 <= j - 6 < NPAIR:
                pemit_pv(j - 6)
            if 0 <= j - 5 < NPAIR:
                pemit_tr(j - 5)
            if 0 <= j - 3 < NPAIR:
                pemit_front(j - 3)
            if j < NPAIR:
                pemit_S(j)
        v_aof_done[0] = c_act.v

        v_attn_tail_pe[0] = c_pe.v
    v_attn_done_dve = c_dve.v
    v_attn_done_pe = c_pe.v
    v_attn_done_act = c_act.v

    es_attn.close()
    xn2f = nc.alloc_sbuf_tensor("xn2f", [P, DC, TCORE], BF16)
    act_sb = nc.alloc_sbuf_tensor("act_sb", [P, FS_JF, TCORE], BF16)
    lo_fm = act_sb
    w1rg = nc.alloc_sbuf_tensor("w1rg", [P, W1SLOTS, DC, 2, P], BF16)
    w2rg = nc.alloc_sbuf_tensor("w2rg", [P, W2SLOTS, D], BF16)
    a1st = nc.alloc_sbuf_tensor("a1st", [P, 2, 512], BF16)
    s_b2 = nc.alloc_sbuf_tensor("s_b2", [1, NE * D], BF16)

    op("vector", lambda e: e.tensor_tensor(s_gl[:], s_gwf[:], s_gh[:], ALU.subtract),
       waits=[(c_sy, v_sync_all[0]), (c_setup, v_gpsetup)], inc=c_dve)
    op("vector", lambda e: e.tensor_tensor(s_gbl[:], s_gbf[:], s_gbh[:], ALU.subtract),
       inc=c_dve)
    v_gate_consts_holder[0] = c_dve.v

    v_st = (v_attn_done_act, v_attn_done_dve)
    op("vector", lambda e: e.tensor_scalar_mul(g_mean[:], g_s1[:], 1.0 / D),
       waits=[(c_act, v_st[0])], inc=c_dve)
    op("vector", lambda e: e.tensor_mul(g_var[:], g_mean[:], g_mean[:]), inc=c_dve)
    op("vector", lambda e: e.scalar_tensor_tensor(
        g_var[:], g_s2[:], 1.0 / D, g_var[:], ALU.mult, ALU.subtract), inc=c_dve)
    v_var = c_dve.v
    op("scalar", lambda e: e.activation(g_rstd[:], g_var[:], AF.Ln,
                                        bias=s_eps[:, 0:1]),
       waits=[(c_dve, v_var), (c_sy, 64)], inc=c_act)
    op("scalar", lambda e: e.activation(g_rstd[:], g_rstd[:], AF.Exp, scale=-0.5),
       inc=c_act)
    v_rstd2 = c_act.v
    op("vector", lambda e: e.scalar_tensor_tensor(
        g_nmr[:], g_mean[:], -1.0, g_rstd[:], ALU.mult, ALU.mult),
       waits=[(c_act, v_rstd2)], inc=c_dve)
    v_nmr2 = c_dve.v

    v_gate_done_dve = 0
    v_gate_done_pe = 0
    gate_ps = [pA, pB]
    vU = {}
    vHI = {}
    vLO = {}
    vTRH = {}
    vTRL = {}
    vHIEV = {}
    vLOEV = {}
    vGATE = {}

    def ln2_stageA(tt):
        q = tt % 2
        op("vector", lambda e, tt=tt, q=q: e.tensor_scalar(
            lnsq[:, q * D:(q + 1) * D], x_sb[:, tt, :], g_rstd[:, tt:tt + 1],
            g_nmr[:, tt:tt + 1], ALU.mult, ALU.add),
           waits=[(c_act, v_rstd2), (c_dve, v_nmr2),
                  (c_act, vHI.get(tt - 2, 0))], inc=c_dve)
        vU[tt] = c_dve.v
        op("scalar", lambda e, q=q: e.copy(xn_stage[:, q, :], lnsq[:, q * D:(q + 1) * D]),
           waits=[(c_dve, vU[tt]), (c_pe, vTRH.get(tt - 2, 0))], inc=c_act)
        vHI[tt] = c_act.v
        op("vector", lambda e, q=q: e.tensor_tensor(
            xn_stage[:, 2 + q, :], lnsq[:, q * D:(q + 1) * D], xn_stage[:, q, :],
            ALU.subtract),
           waits=[(c_act, vHI[tt]), (c_pe, vTRL.get(tt - 2, 0))], inc=c_dve)
        vLO[tt] = c_dve.v

    def ln2_stageB(tt):
        q = tt % 2
        for dc in range(DC):
            op("tensor", lambda e, dc=dc, q=q: e.transpose(
                pT[:, dc * P:(dc + 1) * P], xn_stage[:, q, dc * P:(dc + 1) * P],
                s_id[:]),
               waits=[(c_act, vHI[tt]), (c_act, vHIEV.get(tt - 1, 0))]
               if dc == 0 else [], inc=(c_pe if dc == DC - 1 else None))
        vTRH[tt] = c_pe.v
        for dc in range(DC):
            op("tensor", lambda e, dc=dc, q=q: e.transpose(
                pU[:, dc * P:(dc + 1) * P], xn_stage[:, 2 + q, dc * P:(dc + 1) * P],
                s_id[:]),
               waits=[(c_dve, vLO[tt]), (c_act, vLOEV.get(tt - 1, 0))]
               if dc == 0 else [], inc=(c_pe if dc == DC - 1 else None))
        vTRL[tt] = c_pe.v
        op("scalar", lambda e, tt=tt: e.copy(
            xn2f[:, :, tt * P:(tt + 1) * P],
            pT[:].rearrange("p (c f) -> p c f", c=DC)),
           waits=[(c_pe, vTRH[tt])], inc=c_act)
        vHIEV[tt] = c_act.v
        op("scalar", lambda e, tt=tt: e.copy(
            lo_fm[:, :, tt * P:(tt + 1) * P],
            pU[:].rearrange("p (c f) -> p c f", c=DC)),
           waits=[(c_pe, vTRL[tt])], inc=c_act)
        vLOEV[tt] = c_act.v

    def ln2_stageB2(tt):
        gp = gate_ps[tt % 2]
        wg = vGATE.get(tt - 2)
        first = True
        for (gsrc, gmat) in ((xn2f, s_gh), (xn2f, s_gl), (lo_fm, s_gh)):
            for dc in range(DC):
                def f(e, gsrc=gsrc, gmat=gmat, dc=dc, tt=tt, gp=gp, first=first):
                    return e.matmul(gp[:, 0:NE], gsrc[:, dc, tt * P:(tt + 1) * P],
                                    gmat[:, dc, :], start=first, stop=False)
                w = []
                if first:
                    w = [(c_act, vHIEV[tt]), (c_act, vLOEV[tt]),
                         (c_dve, v_gate_consts_holder[0]), (c_setup, v_gpsetup)]
                    if wg:
                        w.append((c_dve, wg))
                op("tensor", f, waits=w, inc=None)
                first = False
        for (brow, last) in ((s_gbh, False), (s_gbl, True)):
            def f(e, brow=brow, gp=gp, tt=tt, last=last):
                return e.matmul(gp[:, 0:NE], s_ones[:], brow[:],
                                start=False, stop=last)
            op("tensor", f, inc=(c_pe if last else None))
        v_gmm = c_pe.v
        op("vector", lambda e, gp=gp: e.reduce_max(gc_t[:, 0:1], gp[:, 0:NE],
                                                   axis=AX.X),
           waits=[(c_pe, v_gmm)], inc=c_dve)
        op("vector", lambda e, gp=gp: e.tensor_scalar(
            gt_t[:, 0, :], gp[:, 0:NE], gc_t[:, 0:1], None, ALU.is_equal), inc=c_dve)
        op("vector", lambda e, gp=gp: e.scalar_tensor_tensor(
            gt_t[:, 1, :], gt_t[:, 0, :], -1e30, gp[:, 0:NE], ALU.mult, ALU.add),
           inc=c_dve)
        op("vector", lambda e: e.reduce_max(gc_t[:, 1:2], gt_t[:, 1, :], axis=AX.X),
           inc=c_dve)
        op("vector", lambda e: e.tensor_scalar(
            gt_t[:, 2, :], gt_t[:, 1, :], gc_t[:, 1:2], None, ALU.is_equal), inc=c_dve)
        op("vector", lambda e: e.tensor_tensor(
            gc_t[:, 2:3], gc_t[:, 1:2], gc_t[:, 0:1], ALU.subtract), inc=c_dve)
        v_d = c_dve.v
        op("scalar", lambda e: e.activation(gc_t[:, 3:4], gc_t[:, 2:3], AF.Sigmoid),
           waits=[(c_dve, v_d)], inc=c_act)
        v_sg = c_act.v
        op("vector", lambda e: e.tensor_scalar(
            gc_t[:, 5:6], gc_t[:, 3:4], -1.0, 1.0, ALU.mult, ALU.add),
           waits=[(c_act, v_sg)], inc=c_dve)
        op("vector", lambda e: e.tensor_scalar(
            gt_t[:, 3, :], gt_t[:, 0, :], gc_t[:, 5:6], None, ALU.mult), inc=c_dve)
        op("vector", lambda e, tt=tt: e.scalar_tensor_tensor(
            gateW[:, tt, :], gt_t[:, 2, :], gc_t[:, 3:4], gt_t[:, 3, :],
            ALU.mult, ALU.add), inc=c_dve)
        vGATE[tt] = c_dve.v

    for j in range(NT + 3):
        if 3 <= j < NT + 3:
            ln2_stageB2(j - 3)
        if 2 <= j < NT + 2:
            ln2_stageB(j - 2)
        if j < NT:
            ln2_stageA(j)
    v_gate_done_dve = c_dve.v
    v_gate_done_pe = c_pe.v
    v_ln2_done_act = c_act.v

    op("gpsimd", lambda e: e.dma_start(out=s_b2[:], in_=d_b2f),
       waits=[(c_pe, v_attn_done_pe)], inc=c_setup)
    v_b2ld = c_setup.v

    w1_fill_v = {}
    w2_fill_v = {}
    w1_free_v = {}
    w2_free_v = {}
    mm1_ep = {}
    mm2_ev = {}
    xh_b = [pA, pC]
    gh_b = [pB, pD]
    m2_b = [pE, pF]
    g1 = [0]
    store_after = {}

    def w1_fill(ei, fs):
        slot = (ei * NFS + fs) % W1SLOTS
        waits = []
        fv = w1_free_v.get(slot)
        if fv:
            waits.append((c_pe, fv))
        else:
            waits.append((c_pe, v_attn_done_pe))
        op("gpsimd", lambda e, ei=ei, fs=fs, slot=slot: e.dma_start(
            out=w1rg[:, slot], in_=d_w1r[ei, fs]), waits=waits, inc=c_w1[slot])
        w1_fill_v[(ei, fs)] = (c_w1[slot], c_w1[slot].v)

    def w2_fill(ei, fs):
        gfs = ei * NFS + fs
        slot = gfs % W2SLOTS
        waits = []
        fv = w2_free_v.get(slot)
        if fv:
            waits.append((c_pe, fv))
        else:
            waits.append((c_pe, v_attn_done_pe))
        op("gpsimd", lambda e, ei=ei, fs=fs, slot=slot: e.dma_start(
            out=w2rg[:, slot], in_=d_w2[ei, fs * P:(fs + 1) * P, :]),
           waits=waits, inc=c_w2[slot])
        w2_fill_v[gfs] = (c_w2[slot], c_w2[slot].v)

    fill_q1 = [(ei, fs) for ei in range(NE) for fs in range(NFS)]
    fill_q2 = list(fill_q1)
    fq1 = [0]
    fq2 = [0]

    def pump_fills(n1, n2):
        for _ in range(n1):
            if fq1[0] < len(fill_q1):
                w1_fill(*fill_q1[fq1[0]])
                fq1[0] += 1
        for _ in range(n2):
            if fq2[0] < len(fill_q2):
                w2_fill(*fill_q2[fq2[0]])
                fq2[0] += 1

    pump_fills(W1SLOTS, W2SLOTS)

    for ei in range(NE):
        for jfi in range(NJ):
            for fsl in range(FS_JF):
                fs = jfi * FS_JF + fsl
                slot = (ei * NFS + fs) % W1SLOTS
                for th in range(NTH):
                    par = g1[0] % 2
                    xh, gh = xh_b[par], gh_b[par]
                    waits = [w1_fill_v[(ei, fs)],
                             (c_act, v_ln2_done_act),
                             (c_dve, v_gate_done_dve)]
                    ep = mm1_ep.get(par)
                    if ep:
                        waits.append((c_act, ep[0]))
                        waits.append((c_dve, ep[1]))
                    mv = mm2_ev.get("last_jf")
                    if mv and fsl == 0 and th == 0:
                        waits.append((c_dve, mv))
                    for hx in range(2):
                        ps = xh if hx == 0 else gh
                        for dc in range(DC):
                            def f(e, ps=ps, slot=slot, dc=dc, hx=hx, th=th):
                                return e.matmul(
                                    ps[:], w1rg[:, slot, dc, hx, :],
                                    xn2f[:, dc, th * 512:(th + 1) * 512],
                                    start=(dc == 0), stop=(dc == DC - 1))
                            op("tensor", f,
                               waits=waits if (hx == 0 and dc == 0) else [],
                               inc=(c_pe if (hx == 1 and dc == DC - 1) else None))
                    v_mm = c_pe.v
                    if th == NTH - 1:
                        w1_free_v[slot] = v_mm
                        pump_fills(1, 0)
                    op("scalar", lambda e, gh=gh, par=par, ei=ei, fs=fs: e.activation(
                        a1st[:, par, :], gh[:], AF.Silu,
                        bias=b1c[:, ei, NFS + fs:NFS + fs + 1]),
                       waits=[(c_pe, v_mm), (c_act, v_b1_done)], inc=c_act)
                    v_a1 = c_act.v

                    def fa2(e, xh=xh, par=par, ei=ei, fs=fs, fsl=fsl, th=th):
                        return e.scalar_tensor_tensor(
                            act_sb[:, fsl, th * 512:(th + 1) * 512], xh[:],
                            b1c[:, ei, fs:fs + 1], a1st[:, par, :],
                            ALU.add, ALU.mult)
                    op("vector", fa2, waits=[(c_act, v_a1), (c_pe, v_mm)],
                       inc=c_dve)
                    mm1_ep[par] = (v_a1, c_dve.v)
                    g1[0] += 1
            v_act_ready = c_dve.v
            for tt in range(NT):
                for dh in range(2):
                    gq = (tt * 2 + dh) % 2
                    ps = m2_b[gq]
                    waits = [(c_dve, v_act_ready),
                             (c_dve, v_gate_done_dve),
                             (c_setup, v_b2ld)]
                    for _fsl in range(FS_JF):
                        waits.append(w2_fill_v[ei * NFS + jfi * FS_JF + _fsl])
                    mv = mm2_ev.get(gq)
                    if mv:
                        waits.append((c_dve, mv))
                    first = True
                    if jfi == 0:
                        def fb(e, ps=ps, ei=ei, dh=dh):
                            return e.matmul(
                                ps[:], s_ones[:],
                                s_b2[:, ei * D + dh * 512: ei * D + (dh + 1) * 512],
                                start=True, stop=False)
                        op("tensor", fb, waits=waits, inc=None)
                        first = False
                        waits = []
                    for fsl in range(FS_JF):
                        fs = jfi * FS_JF + fsl
                        slot = (ei * NFS + fs) % W2SLOTS

                        def f(e, ps=ps, fsl=fsl, slot=slot, tt=tt, dh=dh,
                              first=first, fin=(fsl == FS_JF - 1)):
                            return e.matmul(
                                ps[:], act_sb[:, fsl, tt * P:(tt + 1) * P],
                                w2rg[:, slot, dh * 512:(dh + 1) * 512],
                                start=first, stop=fin)
                        op("tensor", f, waits=waits if first else [],
                           inc=(c_pe if fsl == FS_JF - 1 else None))
                        first = False
                        waits = []
                    v_mm = c_pe.v

                    def fc(e, ps=ps, tt=tt, dh=dh, ei=ei):
                        xs = x_sb[:, tt, dh * 512:(dh + 1) * 512]
                        return e.scalar_tensor_tensor(
                            xs, ps[:], gateW[:, tt, ei:ei + 1], xs,
                            ALU.mult, ALU.add)
                    op("vector", fc, waits=[(c_pe, v_mm)], inc=c_dve)
                    mm2_ev[gq] = c_dve.v
                    if ei == NE - 1 and jfi == NJ - 1:
                        store_after[(tt, dh)] = c_dve.v
            v_jf_pe = c_pe.v
            for fsl in range(FS_JF):
                w2_free_v[(ei * NFS + jfi * FS_JF + fsl) % W2SLOTS] = v_jf_pe
            pump_fills(0, FS_JF)
            mm2_ev["last_jf"] = c_dve.v

    out_r = d_out
    for tt in range(NT):
        op("sync", lambda e, tt=tt: e.dma_start(
            out=out_r[:, tt, :], in_=x_sb[:, tt, :]),
           waits=[(c_dve, store_after[(tt, 1)])], inc=c_sy)
    v_all_stored = c_sy.v

    tapmap = {
        "xn2f": (xn2f, BF16, [P, DC, TCORE]),
        "act_sb": (act_sb, BF16, [P, FS_JF, TCORE]),
        "gateW": (gateW, F32, [P, NT, NE]),
        "g_rstd": (g_rstd, F32, [P, NT]),
        "x_sb": (x_sb, F32, [P, NT, D]),
    }
    for name in taps:
        t, dt_, shp = tapmap[name]
        dtap = nc.dram_tensor("tap_" + name, shp, F32, kind="ExternalOutput").ap()
        if dt_ == F32:
            op("sync", lambda e, t=t, dtap=dtap: e.dma_start(out=dtap, in_=t[:]),
               waits=[(c_dve, c_dve.v), (c_act, c_act.v), (c_pe, c_pe.v)], inc=c_sy)
        else:
            op("gpsimd", lambda e, t=t, dtap=dtap: e.dma_start(out=dtap, in_=t[:]),
               waits=[(c_dve, c_dve.v), (c_act, c_act.v), (c_pe, c_pe.v)], inc=c_setup)
    v_tap_wq = c_setup.v

    op("sync", lambda e: None, waits=[(c_sy, c_sy.v), (c_setup, v_tap_wq)])

    with nc.Block() as block:
        own_ctr = {"scalar": c_act, "vector": c_dve}

        def runner(name):
            def body(eng):
                marks = {}
                own = own_ctr.get(name)
                n_own = 0
                for fn, waits, inc in streams[name]:
                    if own is not None and n_own > 0:
                        eng.wait_ge(own.sem, n_own)
                    for (ctr, v) in waits:
                        if v > marks.get(ctr.key, 0):
                            eng.wait_ge(ctr.sem, v)
                            marks[ctr.key] = v
                    ins = fn(eng)
                    if inc is not None:
                        assert ins is not None
                        ins.then_inc(inc.sem, inc.step)
                        if inc is own:
                            n_own += 1
            return body
        block.tensor(runner("tensor"))
        block.scalar(runner("scalar"))
        block.vector(runner("vector"))
        block.gpsimd(runner("gpsimd"))
        block.sync(runner("sync"))
    es.close()
    return nc


def prep_inputs(L0_attn_x, L0_attn_Wq, L0_attn_Wk, L0_attn_Wv, L0_attn_Wo,
                L0_ffn_gate_w, L0_ffn_gate_b, L0_ffn_w1, L0_ffn_b1,
                L0_ffn_w2, L0_ffn_b2):
    f32 = np.float32
    x = np.asarray(L0_attn_x, f32)
    x2 = np.ascontiguousarray(x.reshape(2 * 8192, D))

    def wt(w):
        return np.ascontiguousarray(
            np.asarray(w, f32).T.reshape(DC, P, D).transpose(1, 0, 2))
    wqt, wkt, wvt, wot = wt(L0_attn_Wq), wt(L0_attn_Wk), wt(L0_attn_Wv), wt(L0_attn_Wo)
    gw = np.ascontiguousarray(
        np.asarray(L0_ffn_gate_w, f32).reshape(DC, P, NE).transpose(1, 0, 2))
    gb = np.asarray(L0_ffn_gate_b, f32).reshape(1, NE)
    w1r = np.ascontiguousarray(
        np.asarray(L0_ffn_w1, f32).reshape(NE, DC, P, 2, NFS, P)
        .transpose(0, 4, 2, 1, 3, 5))
    w2 = np.ascontiguousarray(np.asarray(L0_ffn_w2, f32))
    b1r = np.ascontiguousarray(
        np.asarray(L0_ffn_b1, f32).reshape(NE, 2, NFS, P).transpose(2, 0, 1, 3))
    b2f = np.ascontiguousarray(np.asarray(L0_ffn_b2, f32).reshape(1, NE * D))
    ident = np.eye(P, dtype=f32)
    onesr = np.ones((1, P), f32)
    epsc = np.tile(np.array([[EPS, -12.0]], f32), (P, 1))
    in_maps = []
    for c in range(8):
        seq = c // 4
        in_maps.append(dict(
            x=np.ascontiguousarray(
                x2[c * TCORE:(c + 1) * TCORE].reshape(NT, P, D).transpose(1, 0, 2)),
            xg=np.ascontiguousarray(x[seq, 0:GT, :]),
            wqt=wqt, wkt=wkt, wvt=wvt, wot=wot, gw=gw, gb=gb,
            w1r=w1r, w2=w2, b1r=b1r, b2f=b2f,
            ident=ident, onesr=onesr, epsc=epsc))
    return in_maps


def kernel(**inputs):
    in_maps = prep_inputs(**inputs)
    nc = build_nc()
    res = run_bass_kernel_spmd(nc, in_maps, core_ids=list(range(8)))
    out = np.concatenate(
        [r["out"].reshape(P, NT, D).transpose(1, 0, 2).reshape(TCORE, D)
         for r in res.results], axis=0)
    return out.reshape(2, 8192, D)


# version 22
# speedup vs baseline: 1.0333x; 1.0014x over previous
import contextlib
import numpy as np
import ml_dtypes
import concourse.bass as bass
import concourse.mybir as mybir
from concourse.bass_utils import run_bass_kernel_spmd

F32 = mybir.dt.float32
BF16 = mybir.dt.bfloat16
AF = mybir.ActivationFunctionType
ALU = mybir.AluOpType
AX = mybir.AxisListType

P = 128
D = 1024
DC = 8
TCORE = 2048
NT = 16
H = 16
HD = 64
NWIN = 16
GT = 64
NE = 8
FH = 4096
NFS = 32
NJ = 4
FS_JF = NFS // NJ
NTH = 4
EPS = 1e-5
W1SLOTS = 4
W2SLOTS = 10


class Ctr:
    __slots__ = ("sem", "v", "step", "key")
    _next = 0

    def __init__(self, sem, step=1):
        self.sem = sem
        self.v = 0
        self.step = step
        self.key = Ctr._next
        Ctr._next += 1


def build_nc(taps=()):
    nc = bass.Bass("TRN2", target_bir_lowering=False)
    es = contextlib.ExitStack()

    d_x = nc.dram_tensor("x", [P, NT, D], F32, kind="ExternalInput").ap()
    d_xg = nc.dram_tensor("xg", [GT, D], F32, kind="ExternalInput").ap()
    d_wqt = nc.dram_tensor("wqt", [P, DC, D], F32, kind="ExternalInput").ap()
    d_wkt = nc.dram_tensor("wkt", [P, DC, D], F32, kind="ExternalInput").ap()
    d_wvt = nc.dram_tensor("wvt", [P, DC, D], F32, kind="ExternalInput").ap()
    d_wot = nc.dram_tensor("wot", [P, DC, D], F32, kind="ExternalInput").ap()
    d_gw = nc.dram_tensor("gw", [P, DC, NE], F32, kind="ExternalInput").ap()
    d_gb = nc.dram_tensor("gb", [1, NE], F32, kind="ExternalInput").ap()
    d_w1r = nc.dram_tensor("w1r", [NE, NFS, P, DC, 2, P], F32, kind="ExternalInput").ap()
    d_w2 = nc.dram_tensor("w2", [NE, FH, D], F32, kind="ExternalInput").ap()
    d_b1r = nc.dram_tensor("b1r", [NFS, NE, 2, P], F32, kind="ExternalInput").ap()
    d_b2f = nc.dram_tensor("b2f", [1, NE * D], F32, kind="ExternalInput").ap()
    d_id = nc.dram_tensor("ident", [P, P], F32, kind="ExternalInput").ap()
    d_ones = nc.dram_tensor("onesr", [1, P], F32, kind="ExternalInput").ap()
    d_eps = nc.dram_tensor("epsc", [P, 2], F32, kind="ExternalInput").ap()
    d_out = nc.dram_tensor("out", [P, NT, D], F32, kind="ExternalOutput").ap()

    x_sb = nc.alloc_sbuf_tensor("x_sb", [P, NT, D], F32)
    s_id = nc.alloc_sbuf_tensor("s_id", [P, P], BF16)
    s_ones = nc.alloc_sbuf_tensor("s_ones", [1, P], BF16)
    s_eps = nc.alloc_sbuf_tensor("s_eps", [P, 2], F32)
    s_gwf = nc.alloc_sbuf_tensor("s_gwf", [P, DC, NE], F32)
    s_gh = nc.alloc_sbuf_tensor("s_gh", [P, DC, NE], BF16)
    s_gl = nc.alloc_sbuf_tensor("s_gl", [P, DC, NE], BF16)
    s_gbf = nc.alloc_sbuf_tensor("s_gbf", [1, NE], F32)
    s_gbh = nc.alloc_sbuf_tensor("s_gbh", [1, NE], BF16)
    s_gbl = nc.alloc_sbuf_tensor("s_gbl", [1, NE], BF16)
    b1c = nc.alloc_sbuf_tensor("b1c", [P, NE, 2 * NFS], F32)
    gateW = nc.alloc_sbuf_tensor("gateW", [P, NT, NE], F32)
    g_s1 = nc.alloc_sbuf_tensor("g_s1", [P, NT], F32)
    g_s2 = nc.alloc_sbuf_tensor("g_s2", [P, NT], F32)
    g_mean = nc.alloc_sbuf_tensor("g_mean", [P, NT], F32)
    g_var = nc.alloc_sbuf_tensor("g_var", [P, NT], F32)
    g_rstd = nc.alloc_sbuf_tensor("g_rstd", [P, NT], F32)
    g_nmr = nc.alloc_sbuf_tensor("g_nmr", [P, NT], F32)
    lnsq = nc.alloc_sbuf_tensor("lnsq", [P, 2 * D], F32)
    xn_stage = nc.alloc_sbuf_tensor("xn_stage", [P, 4, D], BF16)
    gt_t = nc.alloc_sbuf_tensor("gt_t", [P, 4, NE], F32)
    gc_t = nc.alloc_sbuf_tensor("gc_t", [P, 8], F32)
    es_b1 = contextlib.ExitStack()
    b1tm = es_b1.enter_context(nc.sbuf_tensor("b1tm", [NFS, NE * 2, P], BF16))

    pA = nc.alloc_psum_tensor("pA", [P, 512], F32)
    pB = nc.alloc_psum_tensor("pB", [P, 512], F32)
    pC = nc.alloc_psum_tensor("pC", [P, 512], F32)
    pD = nc.alloc_psum_tensor("pD", [P, 512], F32)
    pE = nc.alloc_psum_tensor("pE", [P, 512], F32)
    pF = nc.alloc_psum_tensor("pF", [P, 512], F32)
    pT = nc.alloc_psum_tensor("pT", [P, D], BF16)
    pU = nc.alloc_psum_tensor("pU", [P, D], BF16)

    sem_pe = es.enter_context(nc.semaphore("sem_pe"))
    sem_act = es.enter_context(nc.semaphore("sem_act"))
    sem_dve = es.enter_context(nc.semaphore("sem_dve"))
    sem_setup = es.enter_context(nc.semaphore("sem_setup"))
    sem_sy = es.enter_context(nc.semaphore("sem_sy"))
    c_pe = Ctr(sem_pe)
    c_act = Ctr(sem_act)
    c_dve = Ctr(sem_dve)
    c_setup = Ctr(sem_setup, 16)
    c_sy = Ctr(sem_sy, 16)
    c_wsl = [Ctr(es.enter_context(nc.semaphore(f"sem_wsl{i}")), 16) for i in range(2)]
    c_w1 = [Ctr(es.enter_context(nc.semaphore(f"sem_w1_{i}")), 16) for i in range(W1SLOTS)]
    c_w2 = [Ctr(es.enter_context(nc.semaphore(f"sem_w2_{i}")), 16) for i in range(W2SLOTS)]
    c_x = [Ctr(es.enter_context(nc.semaphore(f"sem_x{i}")), 16) for i in range(4)]

    streams = {k: [] for k in ("tensor", "scalar", "vector", "gpsimd", "sync")}

    def op(eng, fn, waits=(), inc=None):
        streams[eng].append((fn, tuple((c, v) for (c, v) in waits), inc))
        if inc is not None:
            inc.v += inc.step
        return inc.v if inc is not None else None

    v_sync_all = [0]
    xg_sb_holder = [None]

    op("sync", lambda e: e.dma_start(out=s_eps[:], in_=d_eps), inc=c_sy)
    op("sync", lambda e: e.dma_start(out=s_gwf[:], in_=d_gw), inc=c_sy)
    op("sync", lambda e: e.dma_start(out=s_gbf[:], in_=d_gb), inc=c_sy)
    op("sync", lambda e: e.dma_start(out=xg_sb_holder[0][:], in_=d_xg), inc=c_sy)
    v_sync_all[0] = c_sy.v
    xr = d_x
    for ch in range(4):
        op("sync" if ch % 2 == 0 else "scalar", lambda e, ch=ch: e.dma_start(
            out=x_sb[:, ch * 4:(ch + 1) * 4, :], in_=xr[:, ch * 4:(ch + 1) * 4, :]),
           inc=c_x[ch])

    op("gpsimd", lambda e: e.dma_start(out=s_id[:], in_=d_id), inc=c_setup)
    op("gpsimd", lambda e: e.dma_start(out=s_ones[:], in_=d_ones), inc=c_setup)
    op("gpsimd", lambda e: e.dma_start(out=s_gh[:], in_=d_gw), inc=c_setup)
    op("gpsimd", lambda e: e.dma_start(out=s_gbh[:], in_=d_gb), inc=c_setup)
    op("gpsimd", lambda e: e.dma_start(out=b1tm[:], in_=d_b1r.rearrange(
        "c e h f -> c (e h) f")), inc=c_setup)
    v_gpsetup = c_setup.v
    v_id = v_gpsetup

    v_gate_consts_holder = [0]

    b1_ev = []
    for grp in range(2):
        pdst = pT if grp == 0 else pU
        for k in range(8):
            idx = grp * 8 + k
            e_, h_ = idx // 2, idx % 2

            def f(eng, pdst=pdst, k=k, idx=idx):
                return eng.transpose(pdst[:, k * NFS:(k + 1) * NFS],
                                     b1tm[:, idx, :], s_id[0:NFS, 0:NFS])
            w = [(c_setup, v_gpsetup)] if (grp == 0 and k == 0) else []
            op("tensor", f, waits=w, inc=c_pe)
        v_tr = c_pe.v

        def fev(eng, pdst=pdst, grp=grp):
            return eng.copy(
                b1c[:].rearrange("p e (h c) -> p (e h) c", h=2)[:, grp * 8:(grp + 1) * 8, :],
                pdst[:, 0:8 * NFS].rearrange("p (k c) -> p k c", c=NFS))
        op("scalar", fev, waits=[(c_pe, v_tr)], inc=c_act)
    v_b1_done = c_act.v
    v_b1_tr_pe = c_pe.v
    es_b1.close()

    def ln_stats(src_tiles, n_tiles, parts=P, chunked=False):
        v_r = {}
        v_n = {}
        step = 4 if chunked else n_tiles
        for c0 in range(0, n_tiles, step):
            for tt in range(c0, min(c0 + step, n_tiles)):
                src = src_tiles(tt)
                op("scalar", lambda e, src=src, tt=tt: e.activation(
                    lnsq[0:parts, 0:D], src, AF.Square,
                    accum_out=g_s2[0:parts, tt:tt + 1]),
                   waits=[(c_x[tt // 4], 16)], inc=c_act)
                op("vector", lambda e, src=src, tt=tt: e.reduce_sum(
                    g_s1[0:parts, tt:tt + 1], src, axis=AX.X),
                   waits=[(c_x[tt // 4], 16)], inc=c_dve)
            v_stats = (c_act.v, c_dve.v)
            c1 = min(c0 + step, n_tiles)
            op("vector", lambda e, c0=c0, c1=c1: e.tensor_scalar_mul(
                g_mean[0:parts, c0:c1], g_s1[0:parts, c0:c1], 1.0 / D),
               waits=[(c_act, v_stats[0])], inc=c_dve)
            op("vector", lambda e, c0=c0, c1=c1: e.tensor_mul(
                g_var[0:parts, c0:c1], g_mean[0:parts, c0:c1],
                g_mean[0:parts, c0:c1]), inc=c_dve)
            op("vector", lambda e, c0=c0, c1=c1: e.scalar_tensor_tensor(
                g_var[0:parts, c0:c1], g_s2[0:parts, c0:c1], 1.0 / D,
                g_var[0:parts, c0:c1], ALU.mult, ALU.subtract), inc=c_dve)
            v_var = c_dve.v
            op("scalar", lambda e, c0=c0, c1=c1: e.activation(
                g_rstd[0:parts, c0:c1], g_var[0:parts, c0:c1], AF.Ln,
                bias=s_eps[0:parts, 0:1]), waits=[(c_dve, v_var), (c_sy, 64)],
               inc=c_act)
            op("scalar", lambda e, c0=c0, c1=c1: e.activation(
                g_rstd[0:parts, c0:c1], g_rstd[0:parts, c0:c1], AF.Exp,
                scale=-0.5), inc=c_act)
            v_rstd = c_act.v
            op("vector", lambda e, c0=c0, c1=c1: e.scalar_tensor_tensor(
                g_nmr[0:parts, c0:c1], g_mean[0:parts, c0:c1], -1.0,
                g_rstd[0:parts, c0:c1], ALU.mult, ALU.mult),
               waits=[(c_act, v_rstd)], inc=c_dve)
            for tt in range(c0, c1):
                v_r[tt] = v_rstd
                v_n[tt] = c_dve.v
        return v_r, v_n

    tr_parity = [0]
    stage_last_tr = {}

    def norm_transpose(tt, dst_view, v_extra_waits=(), src=None, parts=P,
                       n_dc=DC, stage_slot=None, rstd_col=None, nmr_col=None,
                       out_stage=None):
        par = tr_parity[0] % 2 if stage_slot is None else stage_slot
        tr_parity[0] += 1
        stg = xn_stage[0:parts, par, :] if out_stage is None else out_stage
        src = x_sb[:, tt, :] if src is None else src
        rc = g_rstd[0:parts, tt:tt + 1] if rstd_col is None else rstd_col
        nc_ = g_nmr[0:parts, tt:tt + 1] if nmr_col is None else nmr_col
        sg = stage_last_tr.get(par, 0)
        pg = stage_last_tr.get(("ev", par), 0)
        op("vector", lambda e, src=src, stg=stg, rc=rc, nc_=nc_: e.tensor_scalar(
            stg, src, rc, nc_, ALU.mult, ALU.add),
           waits=list(v_extra_waits) + [(c_pe, sg)], inc=c_dve)
        v_norm = c_dve.v
        pdst = pT if par == 0 else pU
        for dc in range(n_dc):
            op("tensor", lambda e, dc=dc, stg=stg, pdst=pdst, parts=parts: e.transpose(
                pdst[:, dc * parts:(dc + 1) * parts],
                stg.rearrange("p (c f) -> p c f", f=P)[:, dc, :],
                s_id[0:parts, 0:parts]),
               waits=[(c_dve, v_norm), (c_setup, v_id), (c_act, pg)], inc=c_pe)
        v_tr = c_pe.v
        stage_last_tr[par] = v_tr
        op("scalar", lambda e, pdst=pdst, dst_view=dst_view, parts=parts, n_dc=n_dc:
           e.copy(dst_view,
                         pdst[:, 0:n_dc * parts].rearrange("p (c f) -> p c f", c=n_dc)),
           waits=[(c_pe, v_tr)], inc=c_act)
        stage_last_tr[("ev", par)] = c_act.v
        return c_act.v

    es_attn = contextlib.ExitStack()
    xn1h = es_attn.enter_context(nc.sbuf_tensor("xn1h", [P, DC, 1024], BF16))
    qf = es_attn.enter_context(nc.sbuf_tensor("qf", [P, DC, 1024], BF16))
    kf = es_attn.enter_context(nc.sbuf_tensor("kf", [P, DC, 1024], BF16))
    vt = es_attn.enter_context(nc.sbuf_tensor("vt", [P, DC, 1024], BF16))
    aof = es_attn.enter_context(nc.sbuf_tensor("aof", [P, DC, 1024], BF16))
    wsl = es_attn.enter_context(nc.sbuf_tensor("wsl", [P, 2, DC, 1024], BF16))
    xg_sb = es_attn.enter_context(nc.sbuf_tensor("xg_sb", [GT, D], F32))
    xg_sb_holder[0] = xg_sb
    xgn_fm = es_attn.enter_context(nc.sbuf_tensor("xgn_fm", [P, DC, GT], BF16))
    kgf = es_attn.enter_context(nc.sbuf_tensor("kgf", [P, DC, GT], BF16))
    vgt = es_attn.enter_context(nc.sbuf_tensor("vgt", [GT, D], BF16))
    gx = es_attn.enter_context(nc.sbuf_tensor("gx", [GT, 8], F32))
    P_sb = es_attn.enter_context(nc.sbuf_tensor("P_sb", [P, 4, 192], BF16))
    sPT = es_attn.enter_context(nc.sbuf_tensor("sPT", [P, 2, 512], BF16))
    rr = es_attn.enter_context(nc.sbuf_tensor("rr", [P, 4, 4], F32))



    v_ln1_rstd, v_ln1_nmr = ln_stats(lambda tt: x_sb[:, tt, :], NT, chunked=True)

    op("scalar", lambda e: e.activation(
        lnsq[0:GT, 0:D], xg_sb[:], AF.Square, accum_out=gx[:, 1:2]),
       waits=[(c_sy, v_sync_all[0])], inc=c_act)
    op("vector", lambda e: e.reduce_sum(gx[:, 0:1], xg_sb[:], axis=AX.X),
       waits=[(c_sy, v_sync_all[0])], inc=c_dve)
    v_gxs = (c_act.v, c_dve.v)
    op("vector", lambda e: e.tensor_scalar_mul(gx[:, 2:3], gx[:, 0:1], 1.0 / D),
       waits=[(c_act, v_gxs[0])], inc=c_dve)
    op("vector", lambda e: e.tensor_mul(gx[:, 3:4], gx[:, 2:3], gx[:, 2:3]), inc=c_dve)
    op("vector", lambda e: e.scalar_tensor_tensor(
        gx[:, 3:4], gx[:, 1:2], 1.0 / D, gx[:, 3:4], ALU.mult, ALU.subtract), inc=c_dve)
    v_gv = c_dve.v
    op("scalar", lambda e: e.activation(gx[:, 4:5], gx[:, 3:4], AF.Ln,
                                        bias=s_eps[0:GT, 0:1]),
       waits=[(c_dve, v_gv), (c_sy, 64)], inc=c_act)
    op("scalar", lambda e: e.activation(gx[:, 4:5], gx[:, 4:5], AF.Exp, scale=-0.5),
       inc=c_act)
    v_grstd = c_act.v
    op("vector", lambda e: e.scalar_tensor_tensor(
        gx[:, 5:6], gx[:, 2:3], -1.0, gx[:, 4:5], ALU.mult, ALU.mult),
       waits=[(c_act, v_grstd)], inc=c_dve)
    v_gnmr = c_dve.v
    v_xgn = norm_transpose(
        0, xgn_fm[:], v_extra_waits=[(c_act, v_grstd), (c_dve, v_gnmr)],
        src=xg_sb[:], parts=GT, rstd_col=gx[0:GT, 4:5], nmr_col=gx[0:GT, 5:6])

    proj_ps = [pE, pF]
    S_banks = [pA, pB, pE]
    PV_banks = [pC, pD]
    v_evac_prev = {}
    v_aof_done = [0]
    v_attn_tail_pe = [0]

    def wproj_load(slot, dram, vfree_pe):
        op("gpsimd", lambda e: e.dma_start(out=wsl[:, slot], in_=dram),
           waits=([(c_pe, vfree_pe)] if vfree_pe else []), inc=c_wsl[slot])
        return (c_wsl[slot], c_wsl[slot].v)

    def projection(dst, wslot, vw, half, scale, v_xn, kind, extra=()):
        par = [0]
        for ic in range(DC):
            for tc in range(2):
                ps = proj_ps[par[0] % 2]
                waits = [vw, (c_act, v_xn)] + list(extra)
                pv = v_evac_prev.get(("proj", par[0] % 2))
                if pv:
                    waits.append((c_act, pv))
                for dc in range(DC):
                    def f(e, ps=ps, dc=dc, ic=ic, tc=tc, wslot=wslot, half=half):
                        if kind == "fm":
                            return e.matmul(
                                ps[:], wsl[:, wslot, dc, ic * P:(ic + 1) * P],
                                xn1h[:, dc, tc * 512:(tc + 1) * 512],
                                start=(dc == 0), stop=(dc == DC - 1))
                        else:
                            return e.matmul(
                                ps[:], xn1h[:, dc, ic * P:(ic + 1) * P],
                                wsl[:, wslot, dc, tc * 512:(tc + 1) * 512],
                                start=(dc == 0), stop=(dc == DC - 1))
                    op("tensor", f, waits=waits if dc == 0 else [], inc=(c_pe if dc == DC - 1 else None))
                v_mm = c_pe.v

                def fe(e, ps=ps, ic=ic, tc=tc, scale=scale):
                    return e.activation(dst[:, ic, tc * 512:(tc + 1) * 512], ps[:],
                                        AF.Copy, scale=scale)
                op("scalar", fe, waits=[(c_pe, v_mm)], inc=c_act)
                v_evac_prev[("proj", par[0] % 2)] = c_act.v
                par[0] += 1
        return c_act.v, c_pe.v

    for half in range(2):
        hbase = half * 8
        v_xn1 = 0
        for t8 in range(8):
            tt = hbase + t8
            waits = [(c_act, v_ln1_rstd[tt]), (c_dve, v_ln1_nmr[tt])]
            if half == 1:
                waits.append((c_pe, v_attn_tail_pe[0]))
            v_xn1 = norm_transpose(tt, xn1h[:, :, t8 * P:(t8 + 1) * P],
                                   v_extra_waits=waits)
        vfree0 = v_attn_tail_pe[0] if half == 1 else v_b1_tr_pe
        vwq = wproj_load(0, d_wqt, vfree0)
        extra = []
        vv = v_evac_prev.get("oproj1")
        if vv:
            extra.append((c_dve, vv))
        v_q_act, v_q_pe = projection(qf, 0, vwq, half, 0.125, v_xn1, "fm",
                                     extra=extra)
        vwk = wproj_load(1, d_wkt, vfree0)
        v_k_act, v_k_pe = projection(kf, 1, vwk, half, 1.0, v_xn1, "fm")
        if half == 0:
            for ic in range(DC):
                ps = proj_ps[ic % 2]
                waits = [vwk, (c_act, v_xgn)]
                pv = v_evac_prev.get(("proj", ic % 2))
                if pv:
                    waits.append((c_act, pv))
                for dc in range(DC):
                    def f(e, ps=ps, dc=dc, ic=ic):
                        return e.matmul(ps[:, 0:GT],
                                        wsl[:, 1, dc, ic * P:(ic + 1) * P],
                                        xgn_fm[:, dc, :],
                                        start=(dc == 0), stop=(dc == DC - 1))
                    op("tensor", f, waits=waits if dc == 0 else [],
                       inc=(c_pe if dc == DC - 1 else None))
                v_mm = c_pe.v
                op("scalar", lambda e, ps=ps, ic=ic: e.copy(
                    kgf[:, ic, :], ps[:, 0:GT]), waits=[(c_pe, v_mm)], inc=c_act)
                v_evac_prev[("proj", ic % 2)] = c_act.v
            v_k_pe = c_pe.v
        vwv = wproj_load(0, d_wvt, v_q_pe)
        v_v_act, v_v_pe = projection(vt, 0, vwv, half, 1.0, v_xn1, "tm")
        if half == 0:
            for tc in range(2):
                ps = proj_ps[tc % 2]
                waits = [vwv, (c_act, v_xgn)]
                pv = v_evac_prev.get(("proj", tc % 2))
                if pv:
                    waits.append((c_act, pv))
                for dc in range(DC):
                    def f(e, ps=ps, dc=dc, tc=tc):
                        return e.matmul(ps[0:GT, :], xgn_fm[:, dc, :],
                                        wsl[:, 0, dc, tc * 512:(tc + 1) * 512],
                                        start=(dc == 0), stop=(dc == DC - 1))
                    op("tensor", f, waits=waits if dc == 0 else [],
                       inc=(c_pe if dc == DC - 1 else None))
                v_mm = c_pe.v
                op("scalar", lambda e, ps=ps, tc=tc: e.copy(
                    vgt[:, tc * 512:(tc + 1) * 512], ps[0:GT, :]),
                   waits=[(c_pe, v_mm)], inc=c_act)
                v_evac_prev[("proj", tc % 2)] = c_act.v
        vwo = wproj_load(1, d_wot, v_k_pe)

        LAG = 2
        NPAIR = 8 * H
        v_seed_act = c_act.v
        vS = {}
        vEXP = {}
        vPN = {}
        vPT = {}
        vPTEV = {}
        vPV = {}
        vAO = {}

        def pemit_S(pi):
            w8, head = pi // H, pi % H
            sb = S_banks[pi % 3]
            pb = (head % 2) * HD
            ic = head // 2
            waits = [(c_act, v_q_act), (c_act, v_k_act),
                     (c_act, vEXP.get(pi - 3, v_seed_act))]

            def fs1(e, sb=sb, pb=pb, ic=ic, w8=w8):
                return e.matmul(sb[:, 0:GT], qf[pb:pb + HD, ic, w8 * P:(w8 + 1) * P],
                                kgf[pb:pb + HD, ic, :], start=True, stop=True)
            op("tensor", fs1, waits=waits, inc=None)

            def fs2(e, sb=sb, pb=pb, ic=ic, w8=w8):
                return e.matmul(sb[:, GT:192], qf[pb:pb + HD, ic, w8 * P:(w8 + 1) * P],
                                kf[pb:pb + HD, ic, w8 * P:(w8 + 1) * P],
                                start=True, stop=True)
            op("tensor", fs2, inc=c_pe)
            vS[pi] = c_pe.v

        def pemit_front(pi):
            sb = S_banks[pi % 3]
            op("scalar", lambda e, sb=sb, pi=pi: e.activation(
                P_sb[:, pi % 4, :], sb[:, 0:192], AF.Exp,
                bias=s_eps[:, 1:2], accum_out=rr[:, pi % 4, 1:2]),
               waits=[(c_pe, vS[pi]), (c_pe, vPT.get(pi - 4, 0)), (c_sy, 64)],
               inc=c_act)
            vEXP[pi] = c_act.v
            op("vector", lambda e, pi=pi: e.reciprocal(
                rr[:, pi % 4, 2:3], rr[:, pi % 4, 1:2]),
               waits=[(c_act, vEXP[pi])], inc=c_dve)
            op("vector", lambda e, pi=pi: e.tensor_scalar_mul(
                P_sb[:, pi % 4, :], P_sb[:, pi % 4, :], rr[:, pi % 4, 2:3]),
               inc=c_dve)
            vPN[pi] = c_dve.v

        def pemit_tr_duo(d):
            ptb = pT if d % 2 == 0 else pU
            for sp in range(2):
                pi = 2 * d + sp

                def ft1(e, ptb=ptb, pi=pi, sp=sp):
                    return e.transpose(ptb[0:GT, sp * 256:sp * 256 + P],
                                       P_sb[:, pi % 4, 0:GT], s_id[:])
                op("tensor", ft1,
                   waits=[(c_dve, vPN[2 * d + 1]), (c_dve, vPTEV.get(d - 2, 0))]
                   if sp == 0 else [], inc=None)

                def ft2(e, ptb=ptb, pi=pi, sp=sp):
                    return e.transpose(ptb[:, sp * 256 + P:sp * 256 + 256],
                                       P_sb[:, pi % 4, GT:192], s_id[:])
                op("tensor", ft2, inc=(c_pe if sp == 1 else None))
            vPT[2 * d] = vPT[2 * d + 1] = c_pe.v
            op("vector", lambda e, ptb=ptb, d=d: e.tensor_copy(
                sPT[:, d % 2, :], ptb[:, 0:512]),
               waits=[(c_pe, vPT[2 * d]), (c_pe, vPV.get(2 * (d - 2) + 1, 0))],
               inc=c_dve)
            vPTEV[d] = c_dve.v

        def pemit_pv(pi):
            w8, head = pi // H, pi % H
            pvb = PV_banks[pi % 2]
            pb = (head % 2) * HD
            ic = head // 2

            sp = pi % 2
            d2 = pi // 2

            def fpv1(e, pvb=pvb, head=head, d2=d2, sp=sp):
                return e.matmul(pvb[0:HD, 0:P], vgt[0:GT, head * HD:(head + 1) * HD],
                                sPT[0:GT, d2 % 2, sp * 256:sp * 256 + P],
                                start=True, stop=False)
            op("tensor", fpv1,
               waits=[(c_dve, vPTEV[d2]), (c_act, v_v_act),
                      (c_act, vAO.get(pi - 2, 0))], inc=None)

            def fpv2(e, pvb=pvb, head=head, w8=w8, d2=d2, sp=sp):
                return e.matmul(pvb[0:HD, 0:P], vt[:, w8, head * HD:(head + 1) * HD],
                                sPT[:, d2 % 2, sp * 256 + P:sp * 256 + 256],
                                start=False, stop=True)
            op("tensor", fpv2, inc=c_pe)
            vPV[pi] = c_pe.v
            op("scalar", lambda e, pvb=pvb, pb=pb, ic=ic, w8=w8: e.copy(
                aof[pb:pb + HD, ic, w8 * P:(w8 + 1) * P], pvb[0:HD, 0:P]),
               waits=[(c_pe, vPV[pi])], inc=c_act)
            vAO[pi] = c_act.v

        def pemit_oproj(t8):
            v_ao_w = vAO[t8 * H + H - 1]
            for dh in range(2):
                gidx = t8 * 2 + dh
                ps = pF
                waits = [vwo, (c_act, v_ao_w)]
                pv = v_evac_prev.get("oproj1")
                if pv:
                    waits.append((c_dve, pv))
                pv2 = v_evac_prev.get(("proj", 1))
                if pv2:
                    waits.append((c_act, pv2))
                for ic in range(DC):
                    def f(e, ps=ps, ic=ic, t8=t8, dh=dh):
                        return e.matmul(ps[:], aof[:, ic, t8 * P:(t8 + 1) * P],
                                        wsl[:, 1, ic, dh * 512:(dh + 1) * 512],
                                        start=(ic == 0), stop=(ic == DC - 1))
                    op("tensor", f, waits=waits if ic == 0 else [],
                       inc=(c_pe if ic == DC - 1 else None))
                v_mm = c_pe.v

                def fr(e, ps=ps, t8=t8, dh=dh, hbase=hbase):
                    xs = x_sb[:, hbase + t8, dh * 512:(dh + 1) * 512]
                    return e.tensor_tensor(xs, ps[:], xs, ALU.add)
                op("vector", fr, waits=[(c_pe, v_mm)], inc=c_dve)
                v_evac_prev["oproj1"] = c_dve.v
                if dh == 1:
                    tt = hbase + t8
                    v_add = c_dve.v
                    op("scalar", lambda e, tt=tt: e.activation(
                        lnsq[:, 0:D], x_sb[:, tt, :], AF.Square,
                        accum_out=g_s2[:, tt:tt + 1]),
                       waits=[(c_dve, v_add)], inc=c_act)
                    op("vector", lambda e, tt=tt: e.reduce_sum(
                        g_s1[:, tt:tt + 1], x_sb[:, tt, :], axis=AX.X), inc=c_dve)

        for j in range(NPAIR + 24):
            t8o = (j - 22) // 16
            if j >= 22 and (j - 22) % 16 == 0 and 0 <= t8o < 8:
                pemit_oproj(t8o)
            if 0 <= j - 6 < NPAIR:
                pemit_pv(j - 6)
            if j % 2 == 1 and 0 <= (j - 5) // 2 < NPAIR // 2:
                pemit_tr_duo((j - 5) // 2)
            if 0 <= j - 3 < NPAIR:
                pemit_front(j - 3)
            if j < NPAIR:
                pemit_S(j)
        v_aof_done[0] = c_act.v

        v_attn_tail_pe[0] = c_pe.v
    v_attn_done_dve = c_dve.v
    v_attn_done_pe = c_pe.v
    v_attn_done_act = c_act.v

    es_attn.close()
    xn2f = nc.alloc_sbuf_tensor("xn2f", [P, DC, TCORE], BF16)
    act_sb = nc.alloc_sbuf_tensor("act_sb", [P, FS_JF, TCORE], BF16)
    lo_fm = act_sb
    w1rg = nc.alloc_sbuf_tensor("w1rg", [P, W1SLOTS, DC, 2, P], BF16)
    w2rg = nc.alloc_sbuf_tensor("w2rg", [P, W2SLOTS, D], BF16)
    a1st = nc.alloc_sbuf_tensor("a1st", [P, 2, 512], BF16)
    s_b2 = nc.alloc_sbuf_tensor("s_b2", [1, NE * D], BF16)

    op("vector", lambda e: e.tensor_tensor(s_gl[:], s_gwf[:], s_gh[:], ALU.subtract),
       waits=[(c_sy, v_sync_all[0]), (c_setup, v_gpsetup)], inc=c_dve)
    op("vector", lambda e: e.tensor_tensor(s_gbl[:], s_gbf[:], s_gbh[:], ALU.subtract),
       inc=c_dve)
    v_gate_consts_holder[0] = c_dve.v

    v_st = (v_attn_done_act, v_attn_done_dve)
    op("vector", lambda e: e.tensor_scalar_mul(g_mean[:], g_s1[:], 1.0 / D),
       waits=[(c_act, v_st[0])], inc=c_dve)
    op("vector", lambda e: e.tensor_mul(g_var[:], g_mean[:], g_mean[:]), inc=c_dve)
    op("vector", lambda e: e.scalar_tensor_tensor(
        g_var[:], g_s2[:], 1.0 / D, g_var[:], ALU.mult, ALU.subtract), inc=c_dve)
    v_var = c_dve.v
    op("scalar", lambda e: e.activation(g_rstd[:], g_var[:], AF.Ln,
                                        bias=s_eps[:, 0:1]),
       waits=[(c_dve, v_var), (c_sy, 64)], inc=c_act)
    op("scalar", lambda e: e.activation(g_rstd[:], g_rstd[:], AF.Exp, scale=-0.5),
       inc=c_act)
    v_rstd2 = c_act.v
    op("vector", lambda e: e.scalar_tensor_tensor(
        g_nmr[:], g_mean[:], -1.0, g_rstd[:], ALU.mult, ALU.mult),
       waits=[(c_act, v_rstd2)], inc=c_dve)
    v_nmr2 = c_dve.v

    v_gate_done_dve = 0
    v_gate_done_pe = 0
    gate_ps = [pA, pB]
    vU = {}
    vHI = {}
    vLO = {}
    vTRH = {}
    vTRL = {}
    vHIEV = {}
    vLOEV = {}
    vGATE = {}

    def ln2_stageA(tt):
        q = tt % 2
        op("vector", lambda e, tt=tt, q=q: e.tensor_scalar(
            lnsq[:, q * D:(q + 1) * D], x_sb[:, tt, :], g_rstd[:, tt:tt + 1],
            g_nmr[:, tt:tt + 1], ALU.mult, ALU.add),
           waits=[(c_act, v_rstd2), (c_dve, v_nmr2),
                  (c_act, vHI.get(tt - 2, 0))], inc=c_dve)
        vU[tt] = c_dve.v
        op("scalar", lambda e, q=q: e.copy(xn_stage[:, q, :], lnsq[:, q * D:(q + 1) * D]),
           waits=[(c_dve, vU[tt]), (c_pe, vTRH.get(tt - 2, 0))], inc=c_act)
        vHI[tt] = c_act.v
        op("vector", lambda e, q=q: e.tensor_tensor(
            xn_stage[:, 2 + q, :], lnsq[:, q * D:(q + 1) * D], xn_stage[:, q, :],
            ALU.subtract),
           waits=[(c_act, vHI[tt]), (c_pe, vTRL.get(tt - 2, 0))], inc=c_dve)
        vLO[tt] = c_dve.v

    def ln2_stageB(tt):
        q = tt % 2
        for dc in range(DC):
            op("tensor", lambda e, dc=dc, q=q: e.transpose(
                pT[:, dc * P:(dc + 1) * P], xn_stage[:, q, dc * P:(dc + 1) * P],
                s_id[:]),
               waits=[(c_act, vHI[tt]), (c_act, vHIEV.get(tt - 1, 0))]
               if dc == 0 else [], inc=(c_pe if dc == DC - 1 else None))
        vTRH[tt] = c_pe.v
        for dc in range(DC):
            op("tensor", lambda e, dc=dc, q=q: e.transpose(
                pU[:, dc * P:(dc + 1) * P], xn_stage[:, 2 + q, dc * P:(dc + 1) * P],
                s_id[:]),
               waits=[(c_dve, vLO[tt]), (c_act, vLOEV.get(tt - 1, 0))]
               if dc == 0 else [], inc=(c_pe if dc == DC - 1 else None))
        vTRL[tt] = c_pe.v
        op("scalar", lambda e, tt=tt: e.copy(
            xn2f[:, :, tt * P:(tt + 1) * P],
            pT[:].rearrange("p (c f) -> p c f", c=DC)),
           waits=[(c_pe, vTRH[tt])], inc=c_act)
        vHIEV[tt] = c_act.v
        op("scalar", lambda e, tt=tt: e.copy(
            lo_fm[:, :, tt * P:(tt + 1) * P],
            pU[:].rearrange("p (c f) -> p c f", c=DC)),
           waits=[(c_pe, vTRL[tt])], inc=c_act)
        vLOEV[tt] = c_act.v

    def ln2_stageB2(tt):
        gp = gate_ps[tt % 2]
        wg = vGATE.get(tt - 2)
        first = True
        for (gsrc, gmat) in ((xn2f, s_gh), (xn2f, s_gl), (lo_fm, s_gh)):
            for dc in range(DC):
                def f(e, gsrc=gsrc, gmat=gmat, dc=dc, tt=tt, gp=gp, first=first):
                    return e.matmul(gp[:, 0:NE], gsrc[:, dc, tt * P:(tt + 1) * P],
                                    gmat[:, dc, :], start=first, stop=False)
                w = []
                if first:
                    w = [(c_act, vHIEV[tt]), (c_act, vLOEV[tt]),
                         (c_dve, v_gate_consts_holder[0]), (c_setup, v_gpsetup)]
                    if wg:
                        w.append((c_dve, wg))
                op("tensor", f, waits=w, inc=None)
                first = False
        for (brow, last) in ((s_gbh, False), (s_gbl, True)):
            def f(e, brow=brow, gp=gp, tt=tt, last=last):
                return e.matmul(gp[:, 0:NE], s_ones[:], brow[:],
                                start=False, stop=last)
            op("tensor", f, inc=(c_pe if last else None))
        v_gmm = c_pe.v
        op("vector", lambda e, gp=gp: e.reduce_max(gc_t[:, 0:1], gp[:, 0:NE],
                                                   axis=AX.X),
           waits=[(c_pe, v_gmm)], inc=c_dve)
        op("vector", lambda e, gp=gp: e.tensor_scalar(
            gt_t[:, 0, :], gp[:, 0:NE], gc_t[:, 0:1], None, ALU.is_equal), inc=c_dve)
        op("vector", lambda e, gp=gp: e.scalar_tensor_tensor(
            gt_t[:, 1, :], gt_t[:, 0, :], -1e30, gp[:, 0:NE], ALU.mult, ALU.add),
           inc=c_dve)
        op("vector", lambda e: e.reduce_max(gc_t[:, 1:2], gt_t[:, 1, :], axis=AX.X),
           inc=c_dve)
        op("vector", lambda e: e.tensor_scalar(
            gt_t[:, 2, :], gt_t[:, 1, :], gc_t[:, 1:2], None, ALU.is_equal), inc=c_dve)
        op("vector", lambda e: e.tensor_tensor(
            gc_t[:, 2:3], gc_t[:, 1:2], gc_t[:, 0:1], ALU.subtract), inc=c_dve)
        v_d = c_dve.v
        op("scalar", lambda e: e.activation(gc_t[:, 3:4], gc_t[:, 2:3], AF.Sigmoid),
           waits=[(c_dve, v_d)], inc=c_act)
        v_sg = c_act.v
        op("vector", lambda e: e.tensor_scalar(
            gc_t[:, 5:6], gc_t[:, 3:4], -1.0, 1.0, ALU.mult, ALU.add),
           waits=[(c_act, v_sg)], inc=c_dve)
        op("vector", lambda e: e.tensor_scalar(
            gt_t[:, 3, :], gt_t[:, 0, :], gc_t[:, 5:6], None, ALU.mult), inc=c_dve)
        op("vector", lambda e, tt=tt: e.scalar_tensor_tensor(
            gateW[:, tt, :], gt_t[:, 2, :], gc_t[:, 3:4], gt_t[:, 3, :],
            ALU.mult, ALU.add), inc=c_dve)
        vGATE[tt] = c_dve.v

    for j in range(NT + 3):
        if 3 <= j < NT + 3:
            ln2_stageB2(j - 3)
        if 2 <= j < NT + 2:
            ln2_stageB(j - 2)
        if j < NT:
            ln2_stageA(j)
    v_gate_done_dve = c_dve.v
    v_gate_done_pe = c_pe.v
    v_ln2_done_act = c_act.v

    op("gpsimd", lambda e: e.dma_start(out=s_b2[:], in_=d_b2f),
       waits=[(c_pe, v_attn_done_pe)], inc=c_setup)
    v_b2ld = c_setup.v

    w1_fill_v = {}
    w2_fill_v = {}
    w1_free_v = {}
    w2_free_v = {}
    mm1_ep = {}
    mm2_ev = {}
    xh_b = [pA, pC]
    gh_b = [pB, pD]
    m2_b = [pE, pF]
    g1 = [0]
    store_after = {}

    def w1_fill(ei, fs):
        slot = (ei * NFS + fs) % W1SLOTS
        waits = []
        fv = w1_free_v.get(slot)
        if fv:
            waits.append((c_pe, fv))
        else:
            waits.append((c_pe, v_attn_done_pe))
        op("gpsimd", lambda e, ei=ei, fs=fs, slot=slot: e.dma_start(
            out=w1rg[:, slot], in_=d_w1r[ei, fs]), waits=waits, inc=c_w1[slot])
        w1_fill_v[(ei, fs)] = (c_w1[slot], c_w1[slot].v)

    def w2_fill(ei, fs):
        gfs = ei * NFS + fs
        slot = gfs % W2SLOTS
        waits = []
        fv = w2_free_v.get(slot)
        if fv:
            waits.append((c_pe, fv))
        else:
            waits.append((c_pe, v_attn_done_pe))
        op("gpsimd", lambda e, ei=ei, fs=fs, slot=slot: e.dma_start(
            out=w2rg[:, slot], in_=d_w2[ei, fs * P:(fs + 1) * P, :]),
           waits=waits, inc=c_w2[slot])
        w2_fill_v[gfs] = (c_w2[slot], c_w2[slot].v)

    fill_q1 = [(ei, fs) for ei in range(NE) for fs in range(NFS)]
    fill_q2 = list(fill_q1)
    fq1 = [0]
    fq2 = [0]

    def pump_fills(n1, n2):
        for _ in range(n1):
            if fq1[0] < len(fill_q1):
                w1_fill(*fill_q1[fq1[0]])
                fq1[0] += 1
        for _ in range(n2):
            if fq2[0] < len(fill_q2):
                w2_fill(*fill_q2[fq2[0]])
                fq2[0] += 1

    pump_fills(W1SLOTS, W2SLOTS)

    for ei in range(NE):
        for jfi in range(NJ):
            for fsl in range(FS_JF):
                fs = jfi * FS_JF + fsl
                slot = (ei * NFS + fs) % W1SLOTS
                for th in range(NTH):
                    par = g1[0] % 2
                    xh, gh = xh_b[par], gh_b[par]
                    waits = [w1_fill_v[(ei, fs)],
                             (c_act, v_ln2_done_act),
                             (c_dve, v_gate_done_dve)]
                    ep = mm1_ep.get(par)
                    if ep:
                        waits.append((c_act, ep[0]))
                        waits.append((c_dve, ep[1]))
                    mv = mm2_ev.get("last_jf")
                    if mv and fsl == 0 and th == 0:
                        waits.append((c_dve, mv))
                    for hx in range(2):
                        ps = xh if hx == 0 else gh
                        for dc in range(DC):
                            def f(e, ps=ps, slot=slot, dc=dc, hx=hx, th=th):
                                return e.matmul(
                                    ps[:], w1rg[:, slot, dc, hx, :],
                                    xn2f[:, dc, th * 512:(th + 1) * 512],
                                    start=(dc == 0), stop=(dc == DC - 1))
                            op("tensor", f,
                               waits=waits if (hx == 0 and dc == 0) else [],
                               inc=(c_pe if (hx == 1 and dc == DC - 1) else None))
                    v_mm = c_pe.v
                    if th == NTH - 1:
                        w1_free_v[slot] = v_mm
                        pump_fills(1, 0)
                    op("scalar", lambda e, gh=gh, par=par, ei=ei, fs=fs: e.activation(
                        a1st[:, par, :], gh[:], AF.Silu,
                        bias=b1c[:, ei, NFS + fs:NFS + fs + 1]),
                       waits=[(c_pe, v_mm), (c_act, v_b1_done)], inc=c_act)
                    v_a1 = c_act.v

                    def fa2(e, xh=xh, par=par, ei=ei, fs=fs, fsl=fsl, th=th):
                        return e.scalar_tensor_tensor(
                            act_sb[:, fsl, th * 512:(th + 1) * 512], xh[:],
                            b1c[:, ei, fs:fs + 1], a1st[:, par, :],
                            ALU.add, ALU.mult)
                    op("vector", fa2, waits=[(c_act, v_a1), (c_pe, v_mm)],
                       inc=c_dve)
                    mm1_ep[par] = (v_a1, c_dve.v)
                    g1[0] += 1
            v_act_ready = c_dve.v
            for tt in range(NT):
                for dh in range(2):
                    gq = (tt * 2 + dh) % 2
                    ps = m2_b[gq]
                    waits = [(c_dve, v_act_ready),
                             (c_dve, v_gate_done_dve),
                             (c_setup, v_b2ld)]
                    for _fsl in range(FS_JF):
                        waits.append(w2_fill_v[ei * NFS + jfi * FS_JF + _fsl])
                    mv = mm2_ev.get(gq)
                    if mv:
                        waits.append((c_dve, mv))
                    first = True
                    if jfi == 0:
                        def fb(e, ps=ps, ei=ei, dh=dh):
                            return e.matmul(
                                ps[:], s_ones[:],
                                s_b2[:, ei * D + dh * 512: ei * D + (dh + 1) * 512],
                                start=True, stop=False)
                        op("tensor", fb, waits=waits, inc=None)
                        first = False
                        waits = []
                    for fsl in range(FS_JF):
                        fs = jfi * FS_JF + fsl
                        slot = (ei * NFS + fs) % W2SLOTS

                        def f(e, ps=ps, fsl=fsl, slot=slot, tt=tt, dh=dh,
                              first=first, fin=(fsl == FS_JF - 1)):
                            return e.matmul(
                                ps[:], act_sb[:, fsl, tt * P:(tt + 1) * P],
                                w2rg[:, slot, dh * 512:(dh + 1) * 512],
                                start=first, stop=fin)
                        op("tensor", f, waits=waits if first else [],
                           inc=(c_pe if fsl == FS_JF - 1 else None))
                        first = False
                        waits = []
                    v_mm = c_pe.v

                    def fc(e, ps=ps, tt=tt, dh=dh, ei=ei):
                        xs = x_sb[:, tt, dh * 512:(dh + 1) * 512]
                        return e.scalar_tensor_tensor(
                            xs, ps[:], gateW[:, tt, ei:ei + 1], xs,
                            ALU.mult, ALU.add)
                    op("vector", fc, waits=[(c_pe, v_mm)], inc=c_dve)
                    mm2_ev[gq] = c_dve.v
                    if ei == NE - 1 and jfi == NJ - 1:
                        store_after[(tt, dh)] = c_dve.v
            v_jf_pe = c_pe.v
            for fsl in range(FS_JF):
                w2_free_v[(ei * NFS + jfi * FS_JF + fsl) % W2SLOTS] = v_jf_pe
            pump_fills(0, FS_JF)
            mm2_ev["last_jf"] = c_dve.v

    out_r = d_out
    for tt in range(NT):
        op("sync", lambda e, tt=tt: e.dma_start(
            out=out_r[:, tt, :], in_=x_sb[:, tt, :]),
           waits=[(c_dve, store_after[(tt, 1)])], inc=c_sy)
    v_all_stored = c_sy.v

    tapmap = {
        "xn2f": (xn2f, BF16, [P, DC, TCORE]),
        "act_sb": (act_sb, BF16, [P, FS_JF, TCORE]),
        "gateW": (gateW, F32, [P, NT, NE]),
        "g_rstd": (g_rstd, F32, [P, NT]),
        "x_sb": (x_sb, F32, [P, NT, D]),
    }
    for name in taps:
        t, dt_, shp = tapmap[name]
        dtap = nc.dram_tensor("tap_" + name, shp, F32, kind="ExternalOutput").ap()
        if dt_ == F32:
            op("sync", lambda e, t=t, dtap=dtap: e.dma_start(out=dtap, in_=t[:]),
               waits=[(c_dve, c_dve.v), (c_act, c_act.v), (c_pe, c_pe.v)], inc=c_sy)
        else:
            op("gpsimd", lambda e, t=t, dtap=dtap: e.dma_start(out=dtap, in_=t[:]),
               waits=[(c_dve, c_dve.v), (c_act, c_act.v), (c_pe, c_pe.v)], inc=c_setup)
    v_tap_wq = c_setup.v

    op("sync", lambda e: None, waits=[(c_sy, c_sy.v), (c_setup, v_tap_wq)])

    with nc.Block() as block:
        own_ctr = {"scalar": c_act, "vector": c_dve}

        def runner(name):
            def body(eng):
                marks = {}
                own = own_ctr.get(name)
                n_own = 0
                for fn, waits, inc in streams[name]:
                    if own is not None and n_own > 0:
                        eng.wait_ge(own.sem, n_own)
                    for (ctr, v) in waits:
                        if v > marks.get(ctr.key, 0):
                            eng.wait_ge(ctr.sem, v)
                            marks[ctr.key] = v
                    ins = fn(eng)
                    if inc is not None:
                        assert ins is not None
                        ins.then_inc(inc.sem, inc.step)
                        if inc is own:
                            n_own += 1
            return body
        block.tensor(runner("tensor"))
        block.scalar(runner("scalar"))
        block.vector(runner("vector"))
        block.gpsimd(runner("gpsimd"))
        block.sync(runner("sync"))
    es.close()
    return nc


def prep_inputs(L0_attn_x, L0_attn_Wq, L0_attn_Wk, L0_attn_Wv, L0_attn_Wo,
                L0_ffn_gate_w, L0_ffn_gate_b, L0_ffn_w1, L0_ffn_b1,
                L0_ffn_w2, L0_ffn_b2):
    f32 = np.float32
    x = np.asarray(L0_attn_x, f32)
    x2 = np.ascontiguousarray(x.reshape(2 * 8192, D))

    def wt(w):
        return np.ascontiguousarray(
            np.asarray(w, f32).T.reshape(DC, P, D).transpose(1, 0, 2))
    wqt, wkt, wvt, wot = wt(L0_attn_Wq), wt(L0_attn_Wk), wt(L0_attn_Wv), wt(L0_attn_Wo)
    gw = np.ascontiguousarray(
        np.asarray(L0_ffn_gate_w, f32).reshape(DC, P, NE).transpose(1, 0, 2))
    gb = np.asarray(L0_ffn_gate_b, f32).reshape(1, NE)
    w1r = np.ascontiguousarray(
        np.asarray(L0_ffn_w1, f32).reshape(NE, DC, P, 2, NFS, P)
        .transpose(0, 4, 2, 1, 3, 5))
    w2 = np.ascontiguousarray(np.asarray(L0_ffn_w2, f32))
    b1r = np.ascontiguousarray(
        np.asarray(L0_ffn_b1, f32).reshape(NE, 2, NFS, P).transpose(2, 0, 1, 3))
    b2f = np.ascontiguousarray(np.asarray(L0_ffn_b2, f32).reshape(1, NE * D))
    ident = np.eye(P, dtype=f32)
    onesr = np.ones((1, P), f32)
    epsc = np.tile(np.array([[EPS, -12.0]], f32), (P, 1))
    in_maps = []
    for c in range(8):
        seq = c // 4
        in_maps.append(dict(
            x=np.ascontiguousarray(
                x2[c * TCORE:(c + 1) * TCORE].reshape(NT, P, D).transpose(1, 0, 2)),
            xg=np.ascontiguousarray(x[seq, 0:GT, :]),
            wqt=wqt, wkt=wkt, wvt=wvt, wot=wot, gw=gw, gb=gb,
            w1r=w1r, w2=w2, b1r=b1r, b2f=b2f,
            ident=ident, onesr=onesr, epsc=epsc))
    return in_maps


def kernel(**inputs):
    in_maps = prep_inputs(**inputs)
    nc = build_nc()
    res = run_bass_kernel_spmd(nc, in_maps, core_ids=list(range(8)))
    out = np.concatenate(
        [r["out"].reshape(P, NT, D).transpose(1, 0, 2).reshape(TCORE, D)
         for r in res.results], axis=0)
    return out.reshape(2, 8192, D)
